# version 01
# baseline (speedup 1.0000x reference)
import contextlib
import numpy as np
import ml_dtypes
import concourse.bass as bass
import concourse.mybir as mybir
from concourse.bass_utils import run_bass_kernel_spmd

F32 = mybir.dt.float32
BF16 = mybir.dt.bfloat16
AF = mybir.ActivationFunctionType
ALU = mybir.AluOpType
AXL = mybir.AxisListType

P = 128
D = 2048
T = 2048
NT = 16
NDC = 16
NH = 16
DH = 128
FF = 4096
NFC = 32
NE = 9
S = 8192
NKT = 64
EPS = 1e-5
ISQ_DH = float(1.0 / np.sqrt(128.0))

BF = ml_dtypes.bfloat16
ENGS = ("sync", "gpsimd", "tensor", "vector", "scalar")



class Step:
    __slots__ = ("eng", "cum")

    def __init__(self, eng, cum):
        self.eng = eng
        self.cum = cum


class Prog:

    def __init__(self, nc):
        self.nc = nc
        self.q = {e: [] for e in ENGS}
        self.cum = {e: 0 for e in ENGS}
        self.sem_guards = []
        self.sems = {}

    def _flat(self, deps):
        for d in deps:
            if d is None:
                continue
            if isinstance(d, (list, tuple)):
                yield from self._flat(d)
            else:
                yield d

    def _mk(self, eng, fn, deps, kind, weight):
        waits = {}
        for d in self._flat(deps):
            waits[d.eng] = max(waits.get(d.eng, 0), d.cum)
        self.q[eng].append((fn, waits, kind))
        self.cum[eng] += weight
        return Step(eng, self.cum[eng])

    def step(self, eng, fn, deps=()):
        return self._mk(eng, fn, deps, "c", 1)

    def dma(self, eng, fn, deps=(), n=1):
        return self._mk(eng, fn, deps, "d", 16 * n)

    def emit(self, block, final_deps):
        for e in ENGS:
            g = self.nc.semaphore("sem_" + e)
            self.sem_guards.append(g)
            self.sems[e] = g.__enter__()
        semh = self.sems

        def section(eng_name):
            def body(eng):
                mysem = semh[eng_name]
                for fn, waits, kind in self.q[eng_name]:
                    for we, wv in waits.items():
                        eng.wait_ge(semh[we], wv)
                    r = fn(eng)
                    if kind == "c":
                        r.then_inc(mysem, 1)
                    else:
                        rr = r if isinstance(r, (list, tuple)) else [r]
                        for instr in rr:
                            instr.then_inc(mysem, 16)
            return body

        for eng_name in ENGS:
            if self.q[eng_name]:
                getattr(block, eng_name)(section(eng_name))

        def tail(eng):
            waits = {}
            for d in self._flat(final_deps):
                waits[d.eng] = max(waits.get(d.eng, 0), d.cum)
            for we, wv in waits.items():
                eng.wait_ge(semh[we], wv)
        block.sync(tail)

    def close(self):
        for g in reversed(self.sem_guards):
            g.__exit__(None, None, None)



def rec_ln(pr, st, xsq, xt, dep_x, out_ap, out_war=(), war_sq=None, base=0):
    c = [st[:, base + i:base + i + 1] for i in range(9)]
    s_sq = pr.step("scalar", lambda e: e.activation(xsq[:, :], xt, AF.Square),
                   deps=[dep_x, war_sq])
    s_s1 = pr.step("vector", lambda e: e.reduce_sum(c[0], xt, AXL.X), deps=[dep_x])
    s_s2 = pr.step("vector", lambda e: e.reduce_sum(c[1], xsq[:, :], AXL.X), deps=[s_sq])
    s_me = pr.step("vector", lambda e: [e.tensor_scalar(c[2], c[0], 1.0 / D, None, ALU.mult),
                                        e.tensor_scalar(c[3], c[1], 1.0 / D, None, ALU.mult)][-1],
                   deps=[s_s1, s_s2])
    s_m2 = pr.step("vector", lambda e: e.tensor_scalar(c[4], c[2], c[2], None, ALU.mult),
                   deps=[s_me])
    s_va = pr.step("vector", lambda e: e.tensor_tensor(c[5], c[3], c[4], ALU.subtract),
                   deps=[s_m2])
    s_sd = pr.step("scalar", lambda e: e.activation(c[6], c[5], AF.Sqrt,
                                                    bias=globals()["_EPS_AP"]),
                   deps=[s_va, globals().get("_EPS_DEP")])
    s_rs = pr.step("vector", lambda e: e.reciprocal(c[7], c[6]), deps=[s_sd])
    s_nm = pr.step("vector", lambda e: e.scalar_tensor_tensor(c[8], c[2], -1.0, c[7],
                                                              ALU.mult, ALU.mult),
                   deps=[s_rs])
    s_ap = pr.step("scalar", lambda e: e.activation(out_ap, xt, AF.Identity,
                                                    bias=c[8], scale=c[7]),
                   deps=[s_nm, dep_x] + list(out_war))
    return s_ap, s_s2


def rec_top2(pr, st, p8, msk, ps_ap, dep_ps, wout_ap, wscale=0.5):
    lm, nlm, sm, m1, m2, rs = (st[:, 24 + i:25 + i] for i in range(6))
    s_lm = pr.step("vector", lambda e: e.reduce_max(lm, ps_ap, AXL.X), deps=[dep_ps])
    s_nl = pr.step("vector", lambda e: e.tensor_scalar(nlm, lm, -1.0, None, ALU.mult),
                   deps=[s_lm])
    s_ex = pr.step("scalar", lambda e: e.activation(p8[:, :], ps_ap, AF.Exp,
                                                    bias=nlm, accum_out=sm),
                   deps=[s_nl, dep_ps])
    s_rc = pr.step("vector", lambda e: e.reciprocal(sm, sm), deps=[s_ex])
    s_pn = pr.step("vector", lambda e: e.tensor_scalar(p8[:, :], p8[:, :], sm, None, ALU.mult),
                   deps=[s_rc])
    s_m1 = pr.step("vector", lambda e: e.reduce_max(m1, p8[:, :], AXL.X), deps=[s_pn])
    s_mk = pr.step("vector", lambda e: e.tensor_scalar(msk[:, :], p8[:, :], m1, None, ALU.is_lt),
                   deps=[s_m1])
    s_mk2 = pr.step("vector", lambda e: e.tensor_tensor(msk[:, :], msk[:, :], p8[:, :], ALU.mult),
                    deps=[s_mk])
    s_m2 = pr.step("vector", lambda e: e.reduce_max(m2, msk[:, :], AXL.X), deps=[s_mk2])
    s_su = pr.step("vector", lambda e: e.tensor_tensor(rs, m1, m2, ALU.add), deps=[s_m2])
    s_rr = pr.step("vector", lambda e: e.reciprocal(rs, rs), deps=[s_su])
    s_rh = pr.step("vector", lambda e: e.tensor_scalar(rs, rs, float(wscale), None, ALU.mult),
                   deps=[s_rr])
    s_ge = pr.step("vector", lambda e: e.tensor_scalar(wout_ap, p8[:, :], m2, None, ALU.is_ge),
                   deps=[s_m2])
    s_w1 = pr.step("vector", lambda e: e.tensor_tensor(wout_ap, wout_ap, p8[:, :], ALU.mult),
                   deps=[s_ge])
    s_w2 = pr.step("vector", lambda e: e.tensor_scalar(wout_ap, wout_ap, rs, None, ALU.mult),
                   deps=[s_w1, s_rh])
    return s_w2, s_ex


def rec_ffn0(pr, bufs, xnf_d, w_getcol, x_src, x_dst, W1r, W2b, dep_ready, xres_war):
    xnfh, hfs, w1sb, w2b, xres, bank = (bufs[k] for k in
                                        ("xnfh", "hfs", "w1sb", "w2b", "xres", "bank"))
    last_h_mm = [None] * 4
    last_silu = [None, None]
    last_w1_mm = [None, None]
    last_w2_mm = [None, None]
    last_o_dve = [None] * 4
    last_hfs_reader = None
    store_of = {}
    load_war = [list(xres_war), list(xres_war)]
    s_xnfh = None

    for th in range(2):
        xv = xnf_d.rearrange("(dc p) t -> p dc t", p=P)

        def ld_xnfh(e, th=th, xv=xv):
            outv = xnfh.rearrange("p (dc t) -> p dc t", t=1024)
            return [e.dma_start(out=outv[:, 4 * q:4 * q + 4, :],
                                in_=xv[:, 4 * q:4 * q + 4, th * 1024:(th + 1) * 1024])
                    for q in range(4)]
        s_xnfh = pr.dma("gpsimd", ld_xnfh, deps=[dep_ready] + last_h_mm, n=4)

        for e_i in range(NE):
            for fc in range(NFC):
                sl = fc % 2
                w1dst = w1sb[:, sl * D:(sl + 1) * D]

                def ld_w1(e, e_i=e_i, fc=fc, w1dst=w1dst):
                    return [e.dma_start(out=w1dst.rearrange("p (dc f) -> p dc f", f=P),
                                        in_=W1r[e_i, fc].rearrange("dc p f -> p dc f"))]
                s_w1l = pr.dma("gpsimd", ld_w1, deps=[last_w1_mm[sl]], n=1)

                b0, b1 = 2 * sl, 2 * sl + 1

                def mm_w1(e, w1dst=w1dst, b0=b0, b1=b1, xnfh=xnfh):
                    for dc in range(NDC):
                        e.matmul(bank[b0], w1dst[:, dc * P:(dc + 1) * P],
                                 xnfh[:, dc * 1024:dc * 1024 + 512],
                                 start=(dc == 0), stop=(dc == NDC - 1))
                        mi = e.matmul(bank[b1], w1dst[:, dc * P:(dc + 1) * P],
                                      xnfh[:, dc * 1024 + 512:(dc + 1) * 1024],
                                      start=(dc == 0), stop=(dc == NDC - 1))
                    return mi
                s_mm = pr.step("tensor", mm_w1, deps=[s_w1l, s_xnfh, last_silu[sl]])
                last_w1_mm[sl] = s_mm
                last_h_mm[b0] = s_mm
                last_h_mm[b1] = s_mm

                def silu(e, fc=fc, b0=b0, b1=b1, hfs=hfs):
                    e.activation(hfs[:, fc * 1024:fc * 1024 + 512], bank[b0], AF.Silu)
                    return e.activation(hfs[:, fc * 1024 + 512:(fc + 1) * 1024],
                                        bank[b1], AF.Silu)
                s_si = pr.step("scalar", silu, deps=[s_mm, last_hfs_reader])
                last_silu[sl] = s_si
            silu_last = last_silu[(NFC - 1) % 2]

            for Dc in range(4):
                sl = Dc % 2
                w2dst = w2b[:, sl * NFC * 512:(sl + 1) * NFC * 512]

                def ld_w2(e, e_i=e_i, Dc=Dc, w2dst=w2dst):
                    outv = w2dst.rearrange("p (fc j) -> p fc j", j=512)
                    return [e.dma_start(out=outv[:, 16 * q:16 * q + 16, :],
                                        in_=W2b[e_i, Dc, 16 * q:16 * q + 16].rearrange(
                                            "fc p j -> p fc j"))
                            for q in range(2)]
                s_w2l = pr.dma("gpsimd", ld_w2, deps=[last_w2_mm[sl]], n=2)

                for tp in range(4):
                    ob0 = 4 + 2 * (tp % 2)
                    ob1 = ob0 + 1

                    def mm_w2(e, tp=tp, w2dst=w2dst, ob0=ob0, ob1=ob1, hfs=hfs):
                        for fc in range(NFC):
                            e.matmul(bank[ob0],
                                     hfs[:, fc * 1024 + 2 * tp * P:fc * 1024 + (2 * tp + 1) * P],
                                     w2dst[:, fc * 512:(fc + 1) * 512],
                                     start=(fc == 0), stop=(fc == NFC - 1))
                            mi = e.matmul(bank[ob1],
                                          hfs[:, fc * 1024 + (2 * tp + 1) * P:
                                              fc * 1024 + (2 * tp + 2) * P],
                                          w2dst[:, fc * 512:(fc + 1) * 512],
                                          start=(fc == 0), stop=(fc == NFC - 1))
                        return mi
                    s_w2mm = pr.step("tensor", mm_w2,
                                     deps=[s_w2l, silu_last,
                                           last_o_dve[ob0 - 4], last_o_dve[ob1 - 4]])
                    last_hfs_reader = s_w2mm
                    last_w2_mm[sl] = s_w2mm

                    for tt in range(2):
                        tloc = th * 8 + tp * 2 + tt
                        obank = ob0 if tt == 0 else ob1
                        src = x_src if e_i == 0 else x_dst
                        xsl = tt
                        xr = xres[:, xsl * 512:(xsl + 1) * 512]

                        def ld_x(e, src=src, tloc=tloc, Dc=Dc, xr=xr):
                            return [e.dma_start(out=xr, in_=src[tloc * P:(tloc + 1) * P,
                                                                Dc * 512:(Dc + 1) * 512])]
                        s_lx = pr.dma("sync", ld_x,
                                      deps=load_war[xsl] + [store_of.get((tloc, Dc))], n=1)

                        wcol = w_getcol(tloc, e_i)

                        def dve_add(e, obank=obank, wcol=wcol, xr=xr):
                            return e.scalar_tensor_tensor(xr, bank[obank], wcol, xr,
                                                          ALU.mult, ALU.add)
                        s_dv = pr.step("vector", dve_add, deps=[s_w2mm, s_lx])
                        last_o_dve[obank - 4] = s_dv
                        load_war[xsl] = [s_dv]

                        def st_x(e, tloc=tloc, Dc=Dc, xr=xr):
                            return [e.dma_start(out=x_dst[tloc * P:(tloc + 1) * P,
                                                          Dc * 512:(Dc + 1) * 512], in_=xr)]
                        s_sx = pr.dma("sync", st_x, deps=[s_dv], n=1)
                        store_of[(tloc, Dc)] = s_sx
    return list(store_of.values())




def _rec_launchA_phases(pr, C, C2):
    (x_in, W1r, W2b, x_out, qf_out, kf_out, v_out, xnf_d, wqkv,
     big1, big2, mid1, w1sb, xt_sb, xsq_sb, xn_sb, gb_bc, stat_sb, w_sb,
     p8_sb, p8b_sb, wg_sb, ones_sb, gbrow_sb, bvrow_sb, bqk_sb, bank,
     s_ones, s_wg, s_bqk, s_gbrow, s_bvrow) = C
    xt_war = {0: [], 1: []}
    xn_war = {0: [], 1: []}
    stg_war = {0: [], 1: []}
    rt_free = [None, None]
    p1_stores = []
    war_sq = None
    s_w = None
    for t in range(NT):
        sl = t % 2
        xt = xt_sb[:, sl * D:(sl + 1) * D]
        s_lx = pr.dma("sync", (lambda e, t=t, xt=xt: [e.dma_start(
            out=xt, in_=x_in[t * P:(t + 1) * P, :])]), deps=xt_war[sl], n=1)
        xn = xn_sb[:, sl * D:(sl + 1) * D]
        s_ap, s_sq2 = rec_ln(pr, stat_sb, xsq_sb, xt, s_lx, xn,
                             out_war=xn_war[sl], war_sq=war_sq)
        war_sq = s_sq2
        xt_war[sl] = [s_ap]
        stg = mid1[:, sl * D:(sl + 1) * D]

        def tp_fn(e, xn=xn, stg=stg):
            return [e.dma_start(out=stg[:, dc * P:(dc + 1) * P],
                                in_=xn[:, dc * P:(dc + 1) * P], transpose=True)
                    for dc in range(NDC)]
        s_tp = pr.dma("sync", tp_fn, deps=[s_ap] + stg_war[sl], n=NDC)
        xn_war[sl] = [s_tp]

        def st_fn(e, t=t, stg=stg):
            return [e.dma_start(
                out=xnf_d.rearrange("(dc p) tt -> p dc tt", p=P)[:, :, t * P:(t + 1) * P],
                in_=stg.rearrange("p (dc j) -> p dc j", j=P))]
        s_st = pr.dma("sync", st_fn, deps=[s_tp], n=1)
        p1_stores.append(s_st)

        rb = t % 2
        ps_r = bank[rb][:, 0:8]

        def rt_fn(e, stg=stg, ps_r=ps_r):
            for dc in range(NDC):
                mi = e.matmul(ps_r, stg[:, dc * P:(dc + 1) * P],
                              wg_sb[:, dc * 8:(dc + 1) * 8],
                              start=(dc == 0), stop=(dc == NDC - 1))
            return mi
        s_rt = pr.step("tensor", rt_fn, deps=[s_tp, s_wg, rt_free[rb]])
        s_w, s_psfree = rec_top2(pr, stat_sb, p8_sb, p8b_sb, ps_r, s_rt,
                                 w_sb[:, t * 8:(t + 1) * 8])
        rt_free[rb] = s_psfree
        stg_war[sl] = [s_st, s_rt]

    bufs = dict(xnfh=mid1, hfs=big1, w1sb=w1sb, w2b=big2,
                xres=xt_sb[:, 0:1024], bank=bank)

    def w_getcol(tloc, e_i):
        if e_i == 8:
            return 0.5
        return w_sb[:, tloc * 8 + e_i:tloc * 8 + e_i + 1]

    dep_ready = p1_stores + [s_w] + stg_war[0] + stg_war[1] + \
        [rt_free[0], rt_free[1]]
    stores = rec_ffn0(pr, bufs, xnf_d, w_getcol, x_in, x_out, W1r, W2b,
                      dep_ready, xt_war[0] + xt_war[1] + [war_sq])

    ln_gb = C2["ln_gb"]
    bv_row = C2["bv_row"]
    s_gbrow = pr.dma("gpsimd", lambda e: [
        e.dma_start(out=gbrow_sb[:, 0:D], in_=ln_gb[0:1, :]),
        e.dma_start(out=gbrow_sb[:, D:2 * D], in_=ln_gb[1:2, :])], deps=[stores], n=2)
    s_bvrow = pr.dma("gpsimd", lambda e: [e.dma_start(out=bvrow_sb[:, :], in_=bv_row)],
                     deps=[stores])
    gb_done = []
    prev_bc = {0: stores, 1: stores}
    for r in range(2):
        for cq in range(4):
            bk = cq % 2

            def bc_mm(e, r=r, cq=cq, bk=bk):
                return e.matmul(bank[bk], ones_sb[0:1, 0:P],
                                gbrow_sb[0:1, r * D + cq * 512:r * D + (cq + 1) * 512],
                                start=True, stop=True)
            s_bm = pr.step("tensor", bc_mm, deps=[s_gbrow, s_ones, prev_bc[bk]])
            s_bc = pr.step("vector", (lambda e, r=r, cq=cq, bk=bk: e.tensor_copy(
                gb_bc[:, r * D + cq * 512:r * D + (cq + 1) * 512], bank[bk])),
                deps=[s_bm])
            prev_bc[bk] = [s_bc]
            gb_done.append(s_bc)

    xt3_war = {0: stores, 1: stores}
    h1_war = {0: [], 1: []}
    tmp_war = {0: stores, 1: stores}
    tp2_steps = []
    war_sq3 = war_sq
    for t in range(NT):
        sl = t % 2
        xt = xt_sb[:, sl * D:(sl + 1) * D]
        s_lx = pr.dma("sync", (lambda e, t=t, xt=xt: [e.dma_start(
            out=xt, in_=x_out[t * P:(t + 1) * P, :])]),
            deps=[stores] + xt3_war[sl], n=1)
        h1 = xn_sb[:, sl * D:(sl + 1) * D]
        s_ap1, s_sq1 = rec_ln(pr, stat_sb, xsq_sb, xt, s_lx, h1,
                              out_war=h1_war[sl], war_sq=war_sq3, base=0)
        xt3_war[sl] = [s_ap1]
        c = [stat_sb[:, 9 + i:10 + i] for i in range(9)]
        s_sqB = pr.step("scalar", (lambda e, h1=h1: e.activation(
            xsq_sb[:, :], h1, AF.Square)), deps=[s_ap1, s_sq1])
        s_s1B = pr.step("vector", (lambda e, h1=h1: e.reduce_sum(c[0], h1, AXL.X)),
                        deps=[s_ap1])
        s_s2B = pr.step("vector", lambda e: e.reduce_sum(c[1], xsq_sb[:, :], AXL.X),
                        deps=[s_sqB])
        war_sq3 = s_s2B
        s_meB = pr.step("vector", lambda e: [
            e.tensor_scalar(c[2], c[0], 1.0 / D, None, ALU.mult),
            e.tensor_scalar(c[3], c[1], 1.0 / D, None, ALU.mult)][-1],
            deps=[s_s1B, s_s2B])
        s_m2B = pr.step("vector", lambda e: e.tensor_scalar(c[4], c[2], c[2], None,
                                                            ALU.mult), deps=[s_meB])
        s_vaB = pr.step("vector", lambda e: e.tensor_tensor(c[5], c[3], c[4],
                                                            ALU.subtract), deps=[s_m2B])
        s_sdB = pr.step("scalar", lambda e: e.activation(c[6], c[5], AF.Sqrt,
                                                         bias=globals()["_EPS_AP"]),
                        deps=[s_vaB])
        s_rsB = pr.step("vector", lambda e: e.reciprocal(c[7], c[6]), deps=[s_sdB])
        s_nmB = pr.step("vector", lambda e: e.scalar_tensor_tensor(
            c[8], c[2], -1.0, c[7], ALU.mult, ALU.mult), deps=[s_rsB])
        tmp = mid1[:, sl * D:(sl + 1) * D]
        s_apB = pr.step("scalar", (lambda e, h1=h1, tmp=tmp: e.activation(
            tmp, h1, AF.Identity, bias=c[8], scale=c[7])),
            deps=[s_nmB] + tmp_war[sl])
        h1_war[sl] = [s_apB]
        s_mg = pr.step("vector", (lambda e, tmp=tmp: e.tensor_tensor(
            tmp, tmp, gb_bc[:, 0:D], ALU.mult)), deps=[s_apB] + gb_done)
        s_ab = pr.step("vector", (lambda e, tmp=tmp: e.tensor_tensor(
            tmp, tmp, gb_bc[:, D:2 * D], ALU.add)), deps=[s_mg])

        def tp2(e, tmp=tmp, t=t):
            return [e.dma_start(out=big1[:, dc * D + t * P:dc * D + (t + 1) * P],
                                in_=tmp[:, dc * P:(dc + 1) * P], transpose=True)
                    for dc in range(NDC)]
        s_t2 = pr.dma("sync", tp2, deps=[s_ab] + ([stores] if t == 0 else []), n=NDC)
        tmp_war[sl] = [s_t2]
        tp2_steps.append(s_t2)

    out_dmas = []
    wload_prev = stores
    evac_war = [None, None]
    bank_war = {k: prev_bc.get(k % 2, [None]) if k < 2 else [None] for k in range(8)}
    bank_war = {0: prev_bc[0], 1: prev_bc[1]}
    for k in range(2, 8):
        bank_war[k] = [None]
    for proj in range(3):
        def ld_w(e, proj=proj):
            src = wqkv[proj].rearrange("(dc p) cc -> p dc cc", p=P)
            outv = big2[:, 0:NDC * D].rearrange("p (dc cc) -> p dc cc", cc=D)
            return [e.dma_start(out=outv[:, 4 * q:4 * q + 4, :],
                                in_=src[:, 4 * q:4 * q + 4, :])
                    for q in range(4)]
        s_wl = pr.dma("gpsimd", ld_w, deps=[wload_prev], n=4)
        mm_all = []
        if proj < 2:
            dst = qf_out if proj == 0 else kf_out
            for hc in range(NH):
                for tg in range(4):
                    b = (hc * 4 + tg) % 2

                    def mm_qk(e, hc=hc, tg=tg, b=b):
                        for dc in range(NDC):
                            mi = e.matmul(bank[b],
                                          big2[:, dc * D + hc * P:dc * D + (hc + 1) * P],
                                          big1[:, dc * D + tg * 512:dc * D + (tg + 1) * 512],
                                          start=(dc == 0), stop=(dc == NDC - 1))
                        return mi
                    s_mm = pr.step("tensor", mm_qk,
                                   deps=[s_wl, bank_war[b]] + tp2_steps)
                    esl = (hc * 4 + tg) % 2
                    est = mid1[:, (4 + esl) * 512:(5 + esl) * 512]

                    def ev(e, hc=hc, b=b, est=est, proj=proj):
                        return e.activation(est, bank[b], AF.Identity,
                                            bias=bqk_sb[:, proj * NDC + hc:
                                                        proj * NDC + hc + 1])
                    s_ev = pr.step("scalar", ev, deps=[s_mm, s_bqk, evac_war[esl]])
                    bank_war[b] = [s_ev]

                    def so(e, hc=hc, tg=tg, est=est, dst=dst):
                        return [e.dma_start(out=dst[hc * P:(hc + 1) * P,
                                                    tg * 512:(tg + 1) * 512], in_=est)]
                    s_so = pr.dma("sync", so, deps=[s_ev], n=1)
                    evac_war[esl] = s_so
                    out_dmas.append(s_so)
                    mm_all.append(s_mm)
        else:
            for t in range(NT):
                bs = 4 * (t % 2)

                def mm_v(e, t=t, bs=bs):
                    for vc in range(4):
                        e.matmul(bank[bs + vc], ones_sb[0:1, 0:P],
                                 bvrow_sb[0:1, vc * 512:(vc + 1) * 512],
                                 start=True, stop=False)
                    for dc in range(NDC):
                        for vc in range(4):
                            mi = e.matmul(bank[bs + vc],
                                          big1[:, dc * D + t * P:dc * D + (t + 1) * P],
                                          big2[:, dc * D + vc * 512:dc * D + (vc + 1) * 512],
                                          start=False, stop=(dc == NDC - 1))
                    return mi
                s_mm = pr.step("tensor", mm_v,
                               deps=[s_wl, s_bvrow, s_ones] + tp2_steps +
                                    [bank_war[bs + i] for i in range(4)])
                for vc in range(4):
                    esl = (t * 4 + vc) % 2
                    est = mid1[:, (6 + esl) * 512:(7 + esl) * 512]

                    def ev(e, bs=bs, vc=vc, est=est):
                        return e.activation(est, bank[bs + vc], AF.Copy)
                    s_ev = pr.step("scalar", ev, deps=[s_mm, evac_war[esl]])
                    bank_war[bs + vc] = [s_ev]

                    def so(e, t=t, vc=vc, est=est):
                        return [e.dma_start(out=v_out[t * P:(t + 1) * P,
                                                      vc * 512:(vc + 1) * 512], in_=est)]
                    s_so = pr.dma("sync", so, deps=[s_ev], n=1)
                    evac_war[esl] = s_so
                    out_dmas.append(s_so)
                mm_all.append(s_mm)
        wload_prev = mm_all


    return out_dmas, stores



def build_launch_A():
    nc = bass.Bass("TRN2", target_bir_lowering=False)

    x_in = nc.dram_tensor("x_in", [T, D], F32, kind="ExternalInput").ap()
    W1r = nc.dram_tensor("W1r", [NE, NFC, NDC, P, P], F32, kind="ExternalInput").ap()
    W2b = nc.dram_tensor("W2b", [NE, 4, NFC, P, 512], F32, kind="ExternalInput").ap()
    Wg = nc.dram_tensor("Wg", [D, 8], F32, kind="ExternalInput").ap()
    ln_gb = nc.dram_tensor("ln_gb", [2, D], F32, kind="ExternalInput").ap()
    wqkv = nc.dram_tensor("wqkv", [3, D, D], F32, kind="ExternalInput").ap()
    bqk = nc.dram_tensor("bqk", [2, P, NDC], F32, kind="ExternalInput").ap()
    bv_row = nc.dram_tensor("bv_row", [1, D], F32, kind="ExternalInput").ap()
    ones_in = nc.dram_tensor("ones_in", [P, P], F32, kind="ExternalInput").ap()
    consts_in = nc.dram_tensor("consts_in", [P, 8], F32, kind="ExternalInput").ap()

    x_out = nc.dram_tensor("x_out", [T, D], F32, kind="ExternalOutput").ap()
    qf_out = nc.dram_tensor("qf_out", [D, T], BF16, kind="ExternalOutput").ap()
    kf_out = nc.dram_tensor("kf_out", [D, T], BF16, kind="ExternalOutput").ap()
    v_out = nc.dram_tensor("v_out", [T, D], BF16, kind="ExternalOutput").ap()

    xnf_d = nc.dram_tensor("xnf_d", [D, T], BF16)

    with (
        nc.sbuf_tensor("big1", [P, 32 * 1024], BF16) as big1,
        nc.sbuf_tensor("big2", [P, 2 * NFC * 512], BF16) as big2,
        nc.sbuf_tensor("mid1", [P, NDC * 1024], BF16) as mid1,
        nc.sbuf_tensor("w1sb", [P, 2 * D], BF16) as w1sb,
        nc.sbuf_tensor("xt_sb", [P, 2 * D], F32) as xt_sb,
        nc.sbuf_tensor("xn_sb", [P, 2 * D], BF16) as xn_sb,
        nc.sbuf_tensor("gb_bc", [P, 2 * D], BF16) as gb_bc,
        nc.sbuf_tensor("stat_sb", [P, 32], F32) as stat_sb,
        nc.sbuf_tensor("w_sb", [P, NT * 8], F32) as w_sb,
        nc.sbuf_tensor("p8_sb", [P, 8], F32) as p8_sb,
        nc.sbuf_tensor("p8b_sb", [P, 8], F32) as p8b_sb,
        nc.sbuf_tensor("wg_sb", [P, NDC * 8], BF16) as wg_sb,
        nc.sbuf_tensor("ones_sb", [P, P], BF16) as ones_sb,
        nc.sbuf_tensor("bqk_sb", [P, 2 * NDC], F32) as bqk_sb,
        nc.sbuf_tensor("consts_sb", [P, 8], F32) as consts_sb,
        nc.psum_tensor("ps", [P, 8 * 512], F32) as ps,
        nc.Block() as block,
    ):
        pr = Prog(nc)
        bank = [ps[:, k * 512:(k + 1) * 512] for k in range(8)]
        xsq_sb = mid1[:, 2 * D:3 * D]
        gbrow_sb = mid1[0:1, 3 * D:5 * D]
        bvrow_sb = mid1[0:1, 5 * D:6 * D]

        s_ones = pr.dma("gpsimd", lambda e: [e.dma_start(out=ones_sb[:, :], in_=ones_in)])
        s_csts = pr.dma("sync", lambda e: [e.dma_start(out=consts_sb[:, :], in_=consts_in)])
        globals()["_EPS_AP"] = consts_sb[:, 0:1]
        globals()["_EPS_DEP"] = s_csts
        s_wg = pr.dma("gpsimd", lambda e: [e.dma_start(
            out=wg_sb.rearrange("p (dc e8) -> p dc e8", e8=8),
            in_=Wg.rearrange("(dc p) e8 -> p dc e8", p=P))])
        s_bqk = pr.dma("sync", lambda e: [e.dma_start(
            out=bqk_sb.rearrange("p (r h) -> p r h", h=NDC),
            in_=bqk.rearrange("r p h -> p r h"))])
        s_gbrow = None
        s_bvrow = None

        out_dmas, stores = _rec_launchA_phases(pr, (
            x_in, W1r, W2b, x_out, qf_out, kf_out, v_out, xnf_d, wqkv,
            big1, big2, mid1, w1sb, xt_sb, xsq_sb, xn_sb, gb_bc, stat_sb, w_sb,
            p8_sb, p8b_sb, wg_sb, ones_sb, gbrow_sb, bvrow_sb, bqk_sb, bank,
            s_ones, s_wg, s_bqk, s_gbrow, s_bvrow), dict(ln_gb=ln_gb, bv_row=bv_row))

        pr.emit(block, final_deps=out_dmas + stores)
        pr.close()
    return nc




def rec_ln_router(pr, X, x_src, entry, xt_war_init, want_dram, want_sbuf,
                  wg_ap, wg_dep, wscale, psum_war):
    xt_sb, xn_sb, mid1, big1 = X["xt_sb"], X["xn_sb"], X["mid1"], X["big1"]
    stat_sb, xsq_sb, w_sb = X["stat_sb"], X["xsq_sb"], X["w_sb"]
    p8_sb, p8b_sb, bank = X["p8_sb"], X["p8b_sb"], X["bank"]
    xnf_d = X.get("xnf_d")
    xt_war = {0: list(xt_war_init), 1: list(xt_war_init)}
    xn_war = {0: [], 1: []}
    stg_war = {0: list(entry), 1: list(entry)}
    rt_free = [psum_war[0], psum_war[1]]
    stores = []
    tps = []
    war_sq = X.get("war_sq")
    s_w = None
    for t in range(NT):
        sl = t % 2
        xt = xt_sb[:, sl * D:(sl + 1) * D]
        s_lx = pr.dma("sync", (lambda e, t=t, xt=xt: [e.dma_start(
            out=xt, in_=x_src[t * P:(t + 1) * P, :])]), deps=xt_war[sl] + list(entry), n=1)
        xn = xn_sb[:, sl * D:(sl + 1) * D]
        s_ap, s_sq2 = rec_ln(pr, stat_sb, xsq_sb, xt, s_lx, xn,
                             out_war=xn_war[sl], war_sq=war_sq)
        war_sq = s_sq2
        xt_war[sl] = [s_ap]
        stg = mid1[:, sl * D:(sl + 1) * D]

        def tp_fn(e, xn=xn, stg=stg):
            return [e.dma_start(out=stg[:, dc * P:(dc + 1) * P],
                                in_=xn[:, dc * P:(dc + 1) * P], transpose=True)
                    for dc in range(NDC)]
        s_tp = pr.dma("sync", tp_fn, deps=[s_ap] + stg_war[sl], n=NDC)
        xn_war[sl] = [s_tp]
        consumers = []
        if want_dram:
            def st_fn(e, t=t, stg=stg):
                return [e.dma_start(
                    out=xnf_d.rearrange("(dc p) tt -> p dc tt", p=P)[:, :, t * P:(t + 1) * P],
                    in_=stg.rearrange("p (dc j) -> p dc j", j=P))]
            s_st = pr.dma("sync", st_fn, deps=[s_tp], n=1)
            stores.append(s_st)
            consumers.append(s_st)
        if want_sbuf:
            def sb_fn(e, t=t, stg=stg):
                return [e.dma_start(out=big1[:, dc * D + t * P:dc * D + (t + 1) * P],
                                    in_=stg[:, dc * P:(dc + 1) * P])
                        for dc in range(NDC)]
            s_sb = pr.dma("sync", sb_fn, deps=[s_tp] + list(entry), n=NDC)
            tps.append(s_sb)
            consumers.append(s_sb)

        rb = t % 2
        ps_r = bank[rb][:, 0:8]

        def rt_fn(e, stg=stg, ps_r=ps_r, wg_ap=wg_ap):
            for dc in range(NDC):
                mi = e.matmul(ps_r, stg[:, dc * P:(dc + 1) * P],
                              wg_ap[:, dc * 8:(dc + 1) * 8],
                              start=(dc == 0), stop=(dc == NDC - 1))
            return mi
        s_rt = pr.step("tensor", rt_fn, deps=[s_tp, wg_dep, rt_free[rb]])
        s_w, s_psfree = rec_top2(pr, stat_sb, p8_sb, p8b_sb, ps_r, s_rt,
                                 w_sb[:, t * 8:(t + 1) * 8], wscale=wscale)
        rt_free[rb] = s_psfree
        stg_war[sl] = consumers + [s_rt]
    return dict(stores=stores, tps=tps, s_w=s_w, xt_war=xt_war, war_sq=war_sq,
                stg_war=stg_war, rt_free=rt_free)


def rec_l0_attn(pr, X, C, x_src, x_dst, entry):
    (qf_in, gk, gv, msk_in, wo_in, bo_row) = C
    big1, big2, mid1 = X["big1"], X["big2"], X["mid1"]
    xt_sb, ones_sb, stat_sb, bank = X["xt_sb"], X["ones_sb"], X["stat_sb"], X["bank"]
    eps_dep = X["s_csts"]

    kh = mid1[:, 0:8192]
    qh = mid1[:, 8192:10240]
    pt = [mid1[:, 10240 + i * 512:10240 + (i + 1) * 512] for i in range(2)]
    nsc = mid1[:, 11264 + 0:11264 + 2048]
    rbx = mid1[:, 13312:13312 + 512]
    vh = big2[:, 0:8192]
    mskb = big2[:, 8192:16384]

    s_msk = pr.dma("gpsimd", lambda e: [e.dma_start(
        out=mskb.rearrange("p (s j) -> p s j", j=512),
        in_=msk_in.rearrange("s p j -> p s j"))], deps=list(entry), n=1)

    accs_war = [None]
    kh_war = list(entry)
    vh_war = [s_msk]
    qh_war = list(entry)
    ncol = stat_sb[:, 30:31]
    acc_done = []
    last_exp = None
    last_ptuser = [None, None]
    sGn_free = {}
    for h in range(NH):
        def ldk(e, h=h):
            return [e.dma_start(out=kh[:, r * 2048:(r + 1) * 2048],
                                in_=gk[r, h * P:(h + 1) * P, :]) for r in range(4)]
        s_kh = pr.dma("gpsimd", ldk, deps=kh_war, n=4)

        def ldv(e, h=h):
            return [e.dma_start(
                out=vh.rearrange("p (rm j) -> p rm j", j=P)[:, r * 16:(r + 1) * 16, :],
                in_=gv[r].rearrange("(m p) d -> p m d", p=P)[:, :, h * P:(h + 1) * P])
                for r in range(4)]
        s_vh = pr.dma("gpsimd", ldv, deps=vh_war, n=4)

        def ldq(e, h=h):
            return [e.dma_start(out=qh[:, :], in_=qf_in[h * P:(h + 1) * P, :])]
        s_qh = pr.dma("gpsimd", ldq, deps=qh_war, n=1)

        nb = bank[6][0:1, 0:512]
        chunks = [(qh, cq, 26) for cq in range(4)] + \
                 [(kh, ck, 29) for ck in range(16)]
        prev = sGn_free.get(6)
        s_mq = None
        s_mk = None
        first = {26: True, 29: True}
        for src_t, cq, col in chunks:
            s_sqc = pr.step("scalar", (lambda e, src_t=src_t, cq=cq: e.activation(
                nsc[:, 0:512], src_t[:, cq * 512:(cq + 1) * 512], AF.Square)),
                deps=[s_qh, s_kh, prev] + ([last_exp] if last_exp else []))
            s_nmc = pr.step("tensor", (lambda e: e.matmul(
                nb, ones_sb[:, 0:1], nsc[:, 0:512], start=True, stop=True)),
                deps=[s_sqc, prev])
            if first[col]:
                s_red = pr.step("vector", (lambda e, col=col: e.reduce_max(
                    stat_sb[0:1, col:col + 1], nb, AXL.X)), deps=[s_nmc, prev])
                first[col] = False
            else:
                s_tmp = pr.step("vector", (lambda e: e.reduce_max(
                    stat_sb[0:1, 31:32], nb, AXL.X)), deps=[s_nmc, prev])
                s_red = pr.step("vector", (lambda e, col=col: e.tensor_tensor(
                    stat_sb[0:1, col:col + 1], stat_sb[0:1, col:col + 1],
                    stat_sb[0:1, 31:32], ALU.max)), deps=[s_tmp])
            prev = s_red
            if col == 26:
                s_mq = s_red
            else:
                s_mk = s_red
        s_c1 = pr.step("scalar", lambda e: e.activation(
            stat_sb[0:1, 26:27], stat_sb[0:1, 26:27], AF.Sqrt), deps=[s_mq, s_mk])
        s_c2 = pr.step("scalar", lambda e: e.activation(
            stat_sb[0:1, 29:30], stat_sb[0:1, 29:30], AF.Sqrt), deps=[s_mk, s_c1])
        s_c3 = pr.step("vector", lambda e: e.tensor_tensor(
            stat_sb[0:1, 26:27], stat_sb[0:1, 26:27], stat_sb[0:1, 29:30], ALU.mult),
            deps=[s_c1, s_c2])
        s_c4 = pr.step("vector", lambda e: e.tensor_scalar(
            stat_sb[0:1, 26:27], stat_sb[0:1, 26:27], -ISQ_DH, None, ALU.mult),
            deps=[s_c3])
        s_c5 = pr.step("tensor", lambda e: e.matmul(
            bank[6][:, 0:1], X["onesf"].ap()[0:1, 0:P], stat_sb[0:1, 26:27],
            start=True, stop=True), deps=[s_c4, s_mq, X["s_onesf"]])
        s_c6 = pr.step("vector", lambda e: e.tensor_copy(ncol, bank[6][:, 0:1]),
                       deps=[s_c5] + ([last_exp] if last_exp else []))
        sGn_free[6] = s_c6

        for n in range(4):
            bl = bank[2][0:1, 0:512]
            ba = bank[3]
            nkt = 16 * (n + 1)
            s_l = None
            s_a = None
            for kt in range(nkt):
                koff = (kt % 4) * 2048 + (kt // 4) * P
                sb = bank[kt % 2]
                ptb = pt[kt % 2]

                def smm(e, koff=koff, sb=sb, n=n):
                    return e.matmul(sb, kh[:, koff:koff + P],
                                    qh[:, n * 512:(n + 1) * 512], start=True, stop=True)
                s_s = pr.step("tensor", smm,
                              deps=[s_kh, s_qh, last_ptuser[kt % 2], s_c6,
                                    sGn_free.get(("s", kt % 2))])

                def expf(e, sb=sb, ptb=ptb):
                    return e.activation(ptb, sb, AF.Exp, bias=ncol, scale=ISQ_DH)
                s_e = pr.step("scalar", expf, deps=[s_s, s_c6, eps_dep])
                last_exp = s_e
                sGn_free[("s", kt % 2)] = s_e
                dep_pt = s_e
                if kt >= nkt - 16:
                    slot = kt - (nkt - 16)

                    def mskf(e, ptb=ptb, slot=slot):
                        return e.tensor_tensor(ptb, ptb,
                                               mskb[:, slot * 512:(slot + 1) * 512],
                                               ALU.mult)
                    dep_pt = pr.step("vector", mskf, deps=[s_e, s_msk])

                def lmm(e, ptb=ptb, bl=bl, kt=kt, nkt=nkt):
                    return e.matmul(bl, ones_sb[:, 0:1], ptb,
                                    start=(kt == 0), stop=(kt == nkt - 1))
                s_l = pr.step("tensor", lmm,
                              deps=[dep_pt] +
                                   ([sGn_free.get("l")] if kt == 0 else []))

                def amm(e, ptb=ptb, ba=ba, koff=koff, kt=kt, nkt=nkt):
                    return e.matmul(ba, vh[:, koff:koff + P], ptb,
                                    start=(kt == 0), stop=(kt == nkt - 1))
                s_a = pr.step("tensor", amm,
                              deps=[dep_pt, s_vh] +
                                   ([sGn_free.get("a")] if kt == 0 else []))
                last_ptuser[kt % 2] = s_a
            s_r2 = pr.step("vector", (lambda e, bl=bl: e.reciprocal(
                X["lrow"][0:1, :], bl)), deps=[s_l, sGn_free.get("lrow")])
            sGn_free["l"] = s_r2
            s_b = pr.step("tensor", (lambda e: e.matmul(
                bank[4][:, 0:512], X["onesf"].ap()[0:1, 0:P],
                X["lrow"][0:1, :], start=True, stop=True)),
                deps=[s_r2, sGn_free.get("b"), X["s_onesf"]])
            sGn_free["lrow"] = s_b
            s_bc = pr.step("vector", (lambda e: e.tensor_copy(rbx, bank[4][:, 0:512])),
                           deps=[s_b, sGn_free.get("rbx")])
            sGn_free["b"] = s_bc
            s_ev = pr.step("vector", (lambda e, h=h, n=n, ba=ba: e.tensor_tensor(
                big1[:, h * 2048 + n * 512:h * 2048 + (n + 1) * 512], ba, rbx, ALU.mult)),
                deps=[s_a, s_bc])
            sGn_free["a"] = s_ev
            sGn_free["rbx"] = s_ev
            acc_done.append(s_ev)
        kh_war = [s_s]
        vh_war = [s_a]
        qh_war = [s_s]

    def ldwo(e):
        src = wo_in.rearrange("(hh p) c -> p hh c", p=P)
        outv = big2.rearrange("p (hh c) -> p hh c", c=D)
        return [e.dma_start(out=outv[:, 4 * q:4 * q + 4, :],
                            in_=src[:, 4 * q:4 * q + 4, :]) for q in range(4)]
    s_wo = pr.dma("gpsimd", ldwo, deps=[kh_war, vh_war, s_msk] + acc_done, n=4)
    s_bo = pr.dma("gpsimd", lambda e: [e.dma_start(out=X["borow"][0:1, :], in_=bo_row)],
                  deps=acc_done, n=1)

    xres = xt_sb[:, 0:1024]
    load_war = [list(entry), list(entry)]
    obank_war = {k: [acc_done[-1], last_exp] for k in range(8)}
    store_of = {}
    for tl in range(NT):
        for Dc in range(4):
            ob = (tl * 4 + Dc) % 8

            def omm(e, tl=tl, Dc=Dc, ob=ob):
                e.matmul(bank[ob], ones_sb[0:1, 0:P],
                         X["borow"][0:1, Dc * 512:(Dc + 1) * 512], start=True, stop=False)
                for hh in range(NH):
                    mi = e.matmul(bank[ob], big1[:, hh * 2048 + tl * P:hh * 2048 + (tl + 1) * P],
                                  big2[:, hh * D + Dc * 512:hh * D + (Dc + 1) * 512],
                                  start=False, stop=(hh == NH - 1))
                return mi
            s_omm = pr.step("tensor", omm, deps=[s_wo, s_bo, obank_war[ob]] + acc_done)
            xsl = Dc % 2
            xr = xres[:, xsl * 512:(xsl + 1) * 512]

            def lx(e, tl=tl, Dc=Dc, xr=xr):
                return [e.dma_start(out=xr, in_=x_src[tl * P:(tl + 1) * P,
                                                      Dc * 512:(Dc + 1) * 512])]
            s_lx = pr.dma("sync", lx, deps=load_war[xsl], n=1)
            s_ad = pr.step("vector", (lambda e, ob=ob, xr=xr: e.tensor_tensor(
                xr, bank[ob], xr, ALU.add)), deps=[s_omm, s_lx])
            obank_war[ob] = [s_ad]
            load_war[xsl] = [s_ad]

            def sx(e, tl=tl, Dc=Dc, xr=xr):
                return [e.dma_start(out=x_dst[tl * P:(tl + 1) * P,
                                              Dc * 512:(Dc + 1) * 512], in_=xr)]
            s_sx = pr.dma("sync", sx, deps=[s_ad], n=1)
            store_of[(tl, Dc)] = s_sx
    return list(store_of.values())



def rec_ln_feature(pr, X, x_src, entry, xt_war_init, war_sq):
    xt_sb, xn_sb, mid1, big1 = X["xt_sb"], X["xn_sb"], X["mid1"], X["big1"]
    stat_sb, xsq_sb = X["stat_sb"], X["xsq_sb"]
    xt_war = {0: list(xt_war_init), 1: list(xt_war_init)}
    xn_war = {0: [], 1: []}
    stg_war = {0: list(entry), 1: list(entry)}
    tps = []
    for t in range(NT):
        sl = t % 2
        xt = xt_sb[:, sl * D:(sl + 1) * D]
        s_lx = pr.dma("sync", (lambda e, t=t, xt=xt: [e.dma_start(
            out=xt, in_=x_src[t * P:(t + 1) * P, :])]), deps=xt_war[sl] + list(entry), n=1)
        xn = xn_sb[:, sl * D:(sl + 1) * D]
        s_ap, s_sq2 = rec_ln(pr, stat_sb, xsq_sb, xt, s_lx, xn,
                             out_war=xn_war[sl], war_sq=war_sq)
        war_sq = s_sq2
        xt_war[sl] = [s_ap]
        stg = mid1[:, sl * D:(sl + 1) * D]

        def tp_fn(e, xn=xn, stg=stg):
            return [e.dma_start(out=stg[:, dc * P:(dc + 1) * P],
                                in_=xn[:, dc * P:(dc + 1) * P], transpose=True)
                    for dc in range(NDC)]
        s_tp = pr.dma("sync", tp_fn, deps=[s_ap] + stg_war[sl], n=NDC)
        xn_war[sl] = [s_tp]

        def sb_fn(e, t=t, stg=stg):
            return [e.dma_start(out=big1[:, dc * D + t * P:dc * D + (t + 1) * P],
                                in_=stg[:, dc * P:(dc + 1) * P])
                    for dc in range(NDC)]
        s_sb = pr.dma("sync", sb_fn, deps=[s_tp] + list(entry), n=NDC)
        tps.append(s_sb)
        stg_war[sl] = [s_sb]
    return tps, xt_war, war_sq


def rec_ffn1(pr, X, C1, x_io, entry, xt_war_init, psum_war):
    (weightT, AsT, Bst, rwT_sb, rw_dep) = C1
    big1, big2, mid1 = X["big1"], X["big2"], X["mid1"]
    xt_sb, ones_sb, stat_sb, bank, w_sb = (X["xt_sb"], X["ones_sb"], X["stat_sb"],
                                           X["bank"], X["w_sb"])
    r = rec_ln_router(pr, X, x_io, entry, xt_war_init, False, True,
                      rwT_sb, rw_dep, 1.0, psum_war)
    tps, s_w = r["tps"], r["s_w"]
    ent2 = tps + [s_w] + r["stg_war"][0] + r["stg_war"][1] + r["rt_free"]

    def ldw(e):
        srcv = weightT.rearrange("(dc p) c -> p dc c", p=P)
        outv = big2.rearrange("p (dc c) -> p dc c", c=D)
        return [e.dma_start(out=outv[:, 4 * q:4 * q + 4, :],
                            in_=srcv[:, 4 * q:4 * q + 4, :]) for q in range(4)]
    s_ldw = pr.dma("gpsimd", ldw, deps=list(entry), n=4)
    ast_sb = mid1[:, 8192:10240]
    bst_sb = mid1[:, 10240:12288]
    axs = mid1[:, 12288:12800]
    t2f = mid1[:, 12800:13312]
    wts = X["lrow2"]
    s_lda = pr.dma("gpsimd", (lambda e: [e.dma_start(
        out=ast_sb.rearrange("p (dc er) -> p dc er", er=P),
        in_=AsT.rearrange("(dc p) er -> p dc er", p=P))]), deps=ent2, n=1)
    s_ldb = pr.dma("gpsimd", (lambda e: [e.dma_start(out=bst_sb, in_=Bst)]),
                   deps=ent2, n=1)

    load_war = [r["xt_war"][0], r["xt_war"][1]]
    store_of = {}
    ob_war = {k: list(psum_war) + [s_w] for k in range(8)}
    prev_ax = None
    prev_t2f = None
    prev_wt = None
    prev_wrep = None
    for tg in range(4):
        def axmm(e, tg=tg):
            for dc in range(NDC):
                mi = e.matmul(bank[4], ast_sb[:, dc * P:(dc + 1) * P],
                              big1[:, dc * D + tg * 512:dc * D + (tg + 1) * 512],
                              start=(dc == 0), stop=(dc == NDC - 1))
            return mi
        s_ax = pr.step("tensor", axmm, deps=[s_lda, prev_ax] + tps)
        s_axe = pr.step("scalar", (lambda e: e.activation(axs, bank[4], AF.Copy)),
                        deps=[s_ax, prev_t2f])
        prev_ax = s_axe
        def wtp(e, tg=tg):
            for i in range(4):
                tl = tg * 4 + i
                mi = e.transpose(bank[5][0:8, i * P:(i + 1) * P],
                                 w_sb[:, tl * 8:(tl + 1) * 8], X["identf"][:, :])
            return mi
        s_wt = pr.step("tensor", wtp, deps=[s_w, X["s_identf"], prev_wt])
        s_wte = pr.step("vector", (lambda e: e.tensor_copy(wts[0:8, :], bank[5][0:8, 0:512])),
                        deps=[s_wt, prev_wrep])
        prev_wt = s_wte

        def wrep(e):
            return e.matmul(bank[6], X["repl8"][0:8, :], wts[0:8, :],
                            start=True, stop=True)
        s_wr = pr.step("tensor", wrep, deps=[s_wte, X["s_repl8"], prev_wrep])

        s_t2 = pr.step("vector", (lambda e: e.tensor_tensor(t2f, axs, bank[6], ALU.mult)),
                       deps=[s_wr, s_axe, prev_t2f])
        prev_wrep = s_t2
        prev_t2f = s_t2

        for i in range(4):
            tl = tg * 4 + i
            for Dc in range(4):
                ob = Dc

                def omm(e, tl=tl, Dc=Dc, ob=ob, i=i):
                    for dc in range(NDC):
                        e.matmul(bank[ob], big1[:, dc * D + tl * P:dc * D + (tl + 1) * P],
                                 big2[:, dc * D + Dc * 512:dc * D + (Dc + 1) * 512],
                                 start=(dc == 0), stop=False)
                    return e.matmul(bank[ob], t2f[:, i * P:(i + 1) * P],
                                    bst_sb[:, Dc * 512:(Dc + 1) * 512],
                                    start=False, stop=True)
                s_omm = pr.step("tensor", omm,
                                deps=[s_ldw, s_ldb, s_t2, ob_war[ob]] + tps)
                xsl = Dc % 2
                xr = xt_sb[:, 0:1024][:, xsl * 512:(xsl + 1) * 512]

                def lx(e, tl=tl, Dc=Dc, xr=xr):
                    return [e.dma_start(out=xr, in_=x_io[tl * P:(tl + 1) * P,
                                                         Dc * 512:(Dc + 1) * 512])]
                s_lx = pr.dma("sync", lx, deps=load_war[xsl] +
                              [store_of.get((tl, Dc))], n=1)
                s_ad = pr.step("vector", (lambda e, ob=ob, xr=xr: e.scalar_tensor_tensor(
                    xr, bank[ob], 0.5, xr, ALU.mult, ALU.add)), deps=[s_omm, s_lx])
                ob_war[ob] = [s_ad]
                load_war[xsl] = [s_ad]

                def sx(e, tl=tl, Dc=Dc, xr=xr):
                    return [e.dma_start(out=x_io[tl * P:(tl + 1) * P,
                                                 Dc * 512:(Dc + 1) * 512], in_=xr)]
                s_sx = pr.dma("sync", sx, deps=[s_ad], n=1)
                store_of[(tl, Dc)] = s_sx
    return list(store_of.values()), r["war_sq"]


def rec_l1_qkv(pr, X, C, x_src, outs, entry, xt_war_init, war_sq):
    (w1qkvT, cos_f, sin_f, qraw_d, kraw_d) = C
    (qf_out, kf_out, v_out, mem_out) = outs
    big1, big2, mid1 = X["big1"], X["big2"], X["mid1"]
    xt_sb, ones_sb, bank = X["xt_sb"], X["ones_sb"], X["bank"]

    tps, xt_war, war_sq = rec_ln_feature(pr, X, x_src, entry, xt_war_init, war_sq)

    out_dmas = []
    wload_prev = list(entry)
    evac_war = [None, None]
    bank_war = {k: [None] for k in range(8)}
    raw_done = {}
    v_dmas = []
    for proj in range(3):
        def ld_w(e, proj=proj):
            srcv = w1qkvT[proj].rearrange("(dc p) c -> p dc c", p=P)
            outv = big2.rearrange("p (dc c) -> p dc c", c=D)
            return [e.dma_start(out=outv[:, 4 * q:4 * q + 4, :],
                                in_=srcv[:, 4 * q:4 * q + 4, :]) for q in range(4)]
        s_wl = pr.dma("gpsimd", ld_w, deps=[wload_prev], n=4)
        mm_all = []
        if proj == 2:
            for t in range(NT):
                bs = 4 * (t % 2)

                def mm_v(e, t=t, bs=bs):
                    for dc in range(NDC):
                        for vc in range(4):
                            mi = e.matmul(bank[bs + vc],
                                          big1[:, dc * D + t * P:dc * D + (t + 1) * P],
                                          big2[:, dc * D + vc * 512:dc * D + (vc + 1) * 512],
                                          start=(dc == 0), stop=(dc == NDC - 1))
                    return mi
                s_mm = pr.step("tensor", mm_v, deps=[s_wl] + tps +
                               [bank_war[bs + i] for i in range(4)])
                for vc in range(4):
                    esl = (t * 4 + vc) % 2
                    est = mid1[:, 13312 + esl * 512:13312 + (esl + 1) * 512]

                    def ev(e, bs=bs, vc=vc, est=est):
                        return e.activation(est, bank[bs + vc], AF.Copy)
                    s_ev = pr.step("scalar", ev, deps=[s_mm, evac_war[esl]])
                    bank_war[bs + vc] = [s_ev]

                    def so(e, t=t, vc=vc, est=est):
                        return [e.dma_start(out=v_out[t * P:(t + 1) * P,
                                                      vc * 512:(vc + 1) * 512], in_=est)]
                    s_so = pr.dma("sync", so, deps=[s_ev], n=1)
                    evac_war[esl] = s_so
                    out_dmas.append(s_so)
                    v_dmas.append(s_so)
                mm_all.append(s_mm)
        else:
            dst = qraw_d if proj == 0 else kraw_d
            for hc in range(NH):
                for tg in range(4):
                    b = (hc * 4 + tg) % 2

                    def mm_qk(e, hc=hc, tg=tg, b=b):
                        for dc in range(NDC):
                            mi = e.matmul(bank[b],
                                          big2[:, dc * D + hc * P:dc * D + (hc + 1) * P],
                                          big1[:, dc * D + tg * 512:dc * D + (tg + 1) * 512],
                                          start=(dc == 0), stop=(dc == NDC - 1))
                        return mi
                    s_mm = pr.step("tensor", mm_qk, deps=[s_wl, bank_war[b]] + tps)
                    esl = (hc * 4 + tg) % 2
                    est = mid1[:, 13312 + esl * 512:13312 + (esl + 1) * 512]

                    def ev(e, b=b, est=est):
                        return e.activation(est, bank[b], AF.Copy)
                    s_ev = pr.step("scalar", ev, deps=[s_mm, evac_war[esl]])
                    bank_war[b] = [s_ev]

                    def so(e, hc=hc, tg=tg, est=est, dst=dst):
                        return [e.dma_start(out=dst[hc * P:(hc + 1) * P,
                                                    tg * 512:(tg + 1) * 512], in_=est)]
                    s_so = pr.dma("sync", so, deps=[s_ev], n=1)
                    evac_war[esl] = s_so
                    out_dmas.append(s_so)
                    mm_all.append(s_mm)
            raw_done[proj] = list(out_dmas)
        wload_prev = mm_all

    rope_done = {}
    a_sl = mid1[:, 0:2048]
    b_sl = mid1[:, 2048:4096]
    t1_sl = mid1[:, 4096:6144]
    t2_sl = mid1[:, 6144:8192]
    cs = xt_sb[:, 0:D]
    sn = xt_sb[:, D:2 * D]
    prev = [None]
    for proj in range(2):
        raw = qraw_d if proj == 0 else kraw_d
        dst = qf_out if proj == 0 else kf_out
        dones = []
        for hc in range(NH):
            dep0 = raw_done[proj] + prev + (xt_war[0] + xt_war[1] if proj == 0 and hc == 0 else [])

            def lda(e, raw=raw, hc=hc):
                return [e.dma_start(out=a_sl, in_=raw[hc * P:(hc + 1) * P, :]),
                        e.dma_start(out=b_sl, in_=raw[(hc ^ 8) * P:((hc ^ 8) + 1) * P, :])]
            s_l1 = pr.dma("sync", lda, deps=dep0, n=2)

            def ldc(e, hc=hc):
                return [e.dma_start(out=cs, in_=cos_f[hc * P:(hc + 1) * P, :]),
                        e.dma_start(out=sn, in_=sin_f[hc * P:(hc + 1) * P, :])]
            s_l2 = pr.dma("sync", ldc, deps=dep0, n=2)
            s_m1 = pr.step("vector", lambda e: e.tensor_tensor(t1_sl, a_sl, cs, ALU.mult),
                           deps=[s_l1, s_l2, prev])
            s_m2 = pr.step("vector", lambda e: e.tensor_tensor(t2_sl, b_sl, sn, ALU.mult),
                           deps=[s_l1, s_l2, prev])
            op = ALU.subtract if hc < 8 else ALU.add
            s_m3 = pr.step("vector", (lambda e, op=op: e.tensor_tensor(
                t1_sl, t1_sl, t2_sl, op)), deps=[s_m1, s_m2])

            def so(e, hc=hc, dst=dst):
                return [e.dma_start(out=dst[hc * P:(hc + 1) * P, :], in_=t1_sl)]
            s_so = pr.dma("sync", so, deps=[s_m3], n=1)
            prev = [s_so, s_m3]
            dones.append(s_so)
            out_dmas.append(s_so)
        rope_done[proj] = dones

    kf_done = []
    prevk = rope_done[1] + tps + [wload_prev]
    for t in range(NT):
        def tpk(e, t=t):
            return [e.dma_start(out=a_sl[:, hc * P:(hc + 1) * P],
                                in_=kf_out[hc * P:(hc + 1) * P, t * P:(t + 1) * P],
                                transpose=True) for hc in range(NH)]
        s_tk = pr.dma("sync", tpk, deps=prevk, n=NH)
        s_e1 = pr.step("scalar", lambda e: e.activation(b_sl, a_sl, AF.Relu, scale=-1.0),
                       deps=[s_tk])
        s_e2 = pr.step("scalar", lambda e: e.activation(t2_sl, b_sl, AF.Exp, scale=-1.0),
                       deps=[s_e1])
        s_e3 = pr.step("vector", lambda e: e.tensor_relu(t1_sl, a_sl), deps=[s_tk])
        s_e4 = pr.step("vector", (lambda e, t=t: e.tensor_tensor(
            big1[:, t * D:(t + 1) * D], t2_sl, t1_sl, ALU.add)),
            deps=[s_e2, s_e3] + (tps if t == 0 else []))
        prevk = [s_e4]
        kf_done.append(s_e4)

    vt_done = []
    prevv = [wload_prev]
    for t in range(NT):
        def ldvt(e, t=t):
            return [e.dma_start(out=big2[:, t * D:(t + 1) * D],
                                in_=v_out[t * P:(t + 1) * P, :])]
        s_lv = pr.dma("gpsimd", ldvt, deps=prevv + v_dmas, n=1)
        prevv = [s_lv]
        vt_done.append(s_lv)

    mem_dmas = []
    mb_war = {0: [None], 1: [None]}
    est_war = {0: [None], 1: [None]}
    for h in range(NH):
        mb = bank[h % 2]
        nbk = bank[2 + h % 2]

        def memm(e, h=h, mb=mb, nbk=nbk):
            for t in range(NT):
                e.matmul(mb[:, 0:128], big1[:, t * D + h * P:t * D + (h + 1) * P],
                         big2[:, t * D + h * P:t * D + (h + 1) * P],
                         start=(t == 0), stop=(t == NT - 1))
                mi = e.matmul(nbk[:, 0:1], big1[:, t * D + h * P:t * D + (h + 1) * P],
                              ones_sb[:, 0:1], start=(t == 0), stop=(t == NT - 1))
            return mi
        s_me = pr.step("tensor", memm, deps=kf_done + vt_done + mb_war[h % 2])
        est = mid1[:, 8192 + (h % 2) * 256:8192 + (h % 2) * 256 + 129]
        s_ev = pr.step("vector", (lambda e, mb=mb, est=est: e.tensor_copy(
            est[:, 0:128], mb[:, 0:128])), deps=[s_me, est_war[h % 2]])
        s_ev2 = pr.step("vector", (lambda e, nbk=nbk, est=est: e.tensor_copy(
            est[:, 128:129], nbk[:, 0:1])), deps=[s_me, s_ev])
        mb_war[h % 2] = [s_ev2]

        def so(e, h=h, est=est):
            return [e.dma_start(out=mem_out[h * P:(h + 1) * P, :], in_=est)]
        s_so = pr.dma("sync", so, deps=[s_ev2], n=1)
        est_war[h % 2] = s_so
        mem_dmas.append(s_so)
    return out_dmas + mem_dmas



def build_launch_B():
    nc = bass.Bass("TRN2", target_bir_lowering=False)

    x_in = nc.dram_tensor("x_in", [T, D], F32, kind="ExternalInput").ap()
    qf_in = nc.dram_tensor("qf_in", [D, T], BF16, kind="ExternalInput").ap()
    gk = nc.dram_tensor("gk", [4, D, T], BF16, kind="ExternalInput").ap()
    gv = nc.dram_tensor("gv", [4, T, D], BF16, kind="ExternalInput").ap()
    msk_in = nc.dram_tensor("msk_in", [16, P, 512], BF16, kind="ExternalInput").ap()
    wo_in = nc.dram_tensor("wo_in", [D, D], F32, kind="ExternalInput").ap()
    bo_row = nc.dram_tensor("bo_row", [1, D], F32, kind="ExternalInput").ap()
    W1r = nc.dram_tensor("W1r", [NE, NFC, NDC, P, P], F32, kind="ExternalInput").ap()
    W2b = nc.dram_tensor("W2b", [NE, 4, NFC, P, 512], F32, kind="ExternalInput").ap()
    Wg = nc.dram_tensor("Wg", [D, 8], F32, kind="ExternalInput").ap()
    weightT = nc.dram_tensor("weightT", [D, D], F32, kind="ExternalInput").ap()
    AsT = nc.dram_tensor("AsT", [D, P], F32, kind="ExternalInput").ap()
    Bst = nc.dram_tensor("Bst", [P, D], F32, kind="ExternalInput").ap()
    rwT = nc.dram_tensor("rwT", [D, 8], F32, kind="ExternalInput").ap()
    w1qkvT = nc.dram_tensor("w1qkvT", [3, D, D], F32, kind="ExternalInput").ap()
    cos_f = nc.dram_tensor("cos_f", [D, T], F32, kind="ExternalInput").ap()
    sin_f = nc.dram_tensor("sin_f", [D, T], F32, kind="ExternalInput").ap()
    repl8_in = nc.dram_tensor("repl8_in", [8, P], F32, kind="ExternalInput").ap()
    ident_in = nc.dram_tensor("ident_in", [P, P], F32, kind="ExternalInput").ap()
    ones_in = nc.dram_tensor("ones_in", [P, P], F32, kind="ExternalInput").ap()
    consts_in = nc.dram_tensor("consts_in", [P, 8], F32, kind="ExternalInput").ap()

    x_out = nc.dram_tensor("x_out", [T, D], F32, kind="ExternalOutput").ap()
    qf_out = nc.dram_tensor("qf_out", [D, T], BF16, kind="ExternalOutput").ap()
    kf_out = nc.dram_tensor("kf_out", [D, T], BF16, kind="ExternalOutput").ap()
    v_out = nc.dram_tensor("v_out", [T, D], BF16, kind="ExternalOutput").ap()
    mem_out = nc.dram_tensor("mem_out", [NH * P, 129], BF16, kind="ExternalOutput").ap()

    xnf_d = nc.dram_tensor("xnf_d", [D, T], BF16)
    qraw_d = nc.dram_tensor("qraw_d", [D, T], BF16)
    kraw_d = nc.dram_tensor("kraw_d", [D, T], BF16)

    with contextlib.ExitStack() as _st:
        def _sb(name, shape, dt):
            return _st.enter_context(nc.sbuf_tensor(name, shape, dt))
        big1 = _sb("big1", [P, 32 * 1024], BF16)
        big2 = _sb("big2", [P, 2 * NFC * 512], BF16)
        mid1 = _sb("mid1", [P, NDC * 1024], BF16)
        w1sb = _sb("w1sb", [P, 2 * D], BF16)
        xt_sb = _sb("xt_sb", [P, 2 * D], F32)
        xn_sb = _sb("xn_sb", [P, 2 * D], BF16)
        stat_sb = _sb("stat_sb", [P, 32], F32)
        w_sb = _sb("w_sb", [P, NT * 8], F32)
        p8_sb = _sb("p8_sb", [P, 8], F32)
        p8b_sb = _sb("p8b_sb", [P, 8], F32)
        wg_sb = _sb("wg_sb", [P, NDC * 8], BF16)
        wg2_sb = _sb("wg2_sb", [P, NDC * 8], BF16)
        ones_sb = _sb("ones_sb", [P, P], BF16)
        consts_sb = _sb("consts_sb", [P, 8], F32)
        lrow_sb = _sb("lrow_sb", [1, 512], F32)
        lrow2_sb = _sb("lrow2_sb", [8, 512], BF16)
        borow_sb = _sb("borow_sb", [1, D], BF16)
        repl8_sb = _sb("repl8_sb", [8, P], BF16)
        identf_sb = _sb("identf_sb", [P, P], F32)
        onesf_sb = _sb("onesf_sb", [P, P], F32)
        ps = _st.enter_context(nc.psum_tensor("ps", [P, 8 * 512], F32))
        block = _st.enter_context(nc.Block())
        pr = Prog(nc)
        bank = [ps[:, k * 512:(k + 1) * 512] for k in range(8)]
        xsq_sb = mid1[:, 2 * D:3 * D]

        s_ones = pr.dma("gpsimd", lambda e: [e.dma_start(out=ones_sb[:, :], in_=ones_in)])
        s_csts = pr.dma("sync", lambda e: [e.dma_start(out=consts_sb[:, :], in_=consts_in)])
        globals()["_EPS_AP"] = consts_sb[:, 0:1]
        globals()["_EPS_DEP"] = s_csts
        s_wg = pr.dma("gpsimd", lambda e: [e.dma_start(
            out=wg_sb.rearrange("p (dc e8) -> p dc e8", e8=8),
            in_=Wg.rearrange("(dc p) e8 -> p dc e8", p=P))])
        s_wg2 = pr.dma("gpsimd", lambda e: [e.dma_start(
            out=wg2_sb.rearrange("p (dc e8) -> p dc e8", e8=8),
            in_=rwT.rearrange("(dc p) e8 -> p dc e8", p=P))])
        s_repl = pr.dma("gpsimd", lambda e: [e.dma_start(out=repl8_sb[:, :], in_=repl8_in)])
        s_idf = pr.dma("sync", lambda e: [e.dma_start(out=identf_sb[:, :], in_=ident_in)])
        s_onf = pr.dma("sync", lambda e: [e.dma_start(out=onesf_sb[:, :], in_=ones_in)])

        X = dict(big1=big1, big2=big2, mid1=mid1, w1sb=w1sb, xt_sb=xt_sb,
                 xn_sb=xn_sb, stat_sb=stat_sb, w_sb=w_sb, p8_sb=p8_sb,
                 p8b_sb=p8b_sb, ones_sb=ones_sb, bank=bank, xsq_sb=xsq_sb,
                 s_csts=s_csts, lrow=lrow_sb, lrow2=lrow2_sb, borow=borow_sb,
                 repl8=repl8_sb, identf=identf_sb, s_repl8=s_repl, s_identf=s_idf,
                 onesf=onesf_sb, s_onesf=s_onf, xnf_d=xnf_d)

        attn_stores = rec_l0_attn(pr, X, (qf_in, gk, gv, msk_in, wo_in, bo_row),
                                  x_in, x_out, [s_csts, s_ones])

        r = rec_ln_router(pr, X, x_out, attn_stores, attn_stores, True, False,
                          wg_sb, s_wg, 0.5, [attn_stores, attn_stores])
        bufs = dict(xnfh=mid1, hfs=big1, w1sb=w1sb, w2b=big2,
                    xres=xt_sb[:, 0:1024], bank=bank)

        def w_getcol(tloc, e_i):
            if e_i == 8:
                return 0.5
            return w_sb[:, tloc * 8 + e_i:tloc * 8 + e_i + 1]

        dep_ready = r["stores"] + [r["s_w"]] + r["stg_war"][0] + r["stg_war"][1] + \
            r["rt_free"] + attn_stores
        ffn0_stores = rec_ffn0(pr, bufs, xnf_d, w_getcol, x_out, x_out, W1r, W2b,
                               dep_ready, r["xt_war"][0] + r["xt_war"][1] + [r["war_sq"]])
        X["war_sq"] = None

        ffn1_stores, wsq = rec_ffn1(pr, X, (weightT, AsT, Bst, wg2_sb, s_wg2),
                                    x_out, ffn0_stores, ffn0_stores,
                                    [ffn0_stores, ffn0_stores])

        qkv_dmas = rec_l1_qkv(pr, X, (w1qkvT, cos_f, sin_f, qraw_d, kraw_d),
                              x_out, (qf_out, kf_out, v_out, mem_out),
                              ffn1_stores, [ffn1_stores], wsq)

        pr.emit(block, final_deps=qkv_dmas + ffn1_stores + attn_stores + ffn0_stores)
        pr.close()
    return nc


def make_masks(g):
    m = np.zeros((16, P, 512), np.float32)
    for s_ in range(16):
        for jj in range(4):
            for kr in range(P):
                lo = s_ * P + kr - (4 * jj + g) * P
                lo = max(lo, 0)
                if lo < P:
                    m[s_, kr, jj * P + lo:(jj + 1) * P] = 1.0
    return m


def rope_tables(core):
    b, g = core // 4, core % 4
    pos = (np.arange(NT)[:, None] * 4 + g) * P + np.arange(P)[None, :]
    pos = pos.reshape(-1).astype(np.float64)
    inv = 1.0 / (10000.0 ** (np.arange(0, D, 2, dtype=np.float64) / D))
    ang = inv[:, None] * pos[None, :]
    cos = np.cos(ang)
    sin = np.sin(ang)
    cos_f = np.concatenate([cos, cos], 0).astype(np.float32)
    sin_f = np.concatenate([sin, sin], 0).astype(np.float32)
    return np.ascontiguousarray(cos_f), np.ascontiguousarray(sin_f)


def prep_launch_B_inputs(inputs, x1_shards, qf, kf, v):
    W1 = np.concatenate([inputs["L0_ffn_W1"], inputs["L0_ffn_Ws1"][None]], axis=0)
    W1r = np.ascontiguousarray(W1.reshape(NE, NDC, P, NFC, P).transpose(0, 3, 1, 2, 4))
    W2 = np.concatenate([inputs["L0_ffn_W2"], inputs["L0_ffn_Ws2"][None]], axis=0)
    W2bh = np.ascontiguousarray(W2.reshape(NE, NFC, P, 4, 512).transpose(0, 3, 1, 2, 4))
    A = inputs["L1_ffn_lora_A"]
    AsT = np.ascontiguousarray(A.transpose(2, 0, 1).reshape(D, P))
    Bm = inputs["L1_ffn_lora_B"]
    Bst = np.ascontiguousarray(Bm.transpose(0, 2, 1).reshape(P, D))
    common = dict(
        W1r=W1r, W2b=W2bh, Wg=np.ascontiguousarray(inputs["L0_ffn_Wg"]),
        wo_in=np.ascontiguousarray(inputs["L0_attn_Wo"]),
        bo_row=np.ascontiguousarray(inputs["L0_attn_bo"][None]),
        weightT=np.ascontiguousarray(inputs["L1_ffn_weight"].T),
        AsT=AsT, Bst=Bst,
        rwT=np.ascontiguousarray(inputs["L1_ffn_router_w"].T),
        w1qkvT=np.ascontiguousarray(np.stack([inputs["L1_attn_Wq"].T,
                                              inputs["L1_attn_Wk"].T,
                                              inputs["L1_attn_Wv"].T])),
        repl8_in=np.ascontiguousarray(
            np.kron(np.eye(8, dtype=np.float32), np.ones((1, 16), np.float32)) * 2.0),
        ident_in=np.eye(P, dtype=np.float32),
        ones_in=np.ones((P, P), np.float32),
        consts_in=np.tile(np.array([[EPS, 1.0, 0, 0, 0, 0, 0, 0]], np.float32), (P, 1)),
    )
    maps = []
    for c in range(8):
        grp = (c // 4) * 4
        cosf, sinf = rope_tables(c)
        maps.append(dict(
            common, x_in=x1_shards[c], qf_in=qf[c],
            gk=np.ascontiguousarray(np.stack([kf[grp + r] for r in range(4)])),
            gv=np.ascontiguousarray(np.stack([v[grp + r] for r in range(4)])),
            msk_in=make_masks(c % 4).astype(BF),
            cos_f=cosf, sin_f=sinf,
        ))
    return maps



def rec_l1_attn(pr, X, C, x_src, x_dst, entry):
    (qf_in, gk, gv, mem_parts, gate_in, woT_in) = C
    big1, big2, mid1 = X["big1"], X["big2"], X["mid1"]
    xt_sb, ones_sb, stat_sb, bank = X["xt_sb"], X["ones_sb"], X["stat_sb"], X["bank"]
    eps_dep = X["s_csts"]
    memf, memb, gsb = X["memf"], X["memb"], X["gsb"]

    kh = mid1[:, 0:8192]
    qh = mid1[:, 8192:10240]
    pt = [mid1[:, 10240 + i * 512:10240 + (i + 1) * 512] for i in range(2)]
    nsc = mid1[:, 11264 + 0:11264 + 2048]
    rbx = mid1[:, 13312:13312 + 512]
    rbx2 = mid1[:, 13824:13824 + 512]
    qfe = mid1[:, 14336:14336 + 2048]
    vh = big2[:, 0:8192]
    cmbt = big2[:, 8192:8704]

    s_g0 = pr.dma("sync", lambda e: [e.dma_start(out=gsb[0:1, 0:16], in_=gate_in)],
                  deps=list(entry), n=1)
    s_g1 = pr.step("scalar", lambda e: e.activation(gsb[0:1, 16:32], gsb[0:1, 0:16],
                                                    AF.Sigmoid), deps=[s_g0])
    s_g2 = pr.step("vector", lambda e: e.tensor_scalar(gsb[0:1, 32:48], gsb[0:1, 16:32],
                                                       -1.0, 1.0, ALU.mult, ALU.add),
                   deps=[s_g1])

    prevm = list(entry)
    s_sum = None
    for r in range(4):
        tmpm = mid1[:, 0:2064]

        def ldm(e, r=r, tmpm=tmpm):
            return [e.dma_start(out=tmpm.rearrange("p (h c) -> p h c", c=129),
                                in_=mem_parts[r].rearrange("(h p) c -> p h c", p=P))]
        s_lm = pr.dma("sync", ldm, deps=prevm, n=1)
        if r == 0:
            s_sum = pr.step("vector", (lambda e, tmpm=tmpm: e.tensor_copy(
                memf[:, 0:2064], tmpm)), deps=[s_lm, s_sum])
        else:
            s_sum = pr.step("vector", (lambda e, tmpm=tmpm: e.tensor_tensor(
                memf[:, 0:2064], memf[:, 0:2064], tmpm, ALU.add)), deps=[s_lm, s_sum])
        prevm = [s_sum]
    s_memb = pr.step("vector", lambda e: e.tensor_copy(memb[:, 0:2064], memf[:, 0:2064]),
                     deps=[s_sum])

    accs_done = []
    kh_war = [s_sum] + list(entry)
    vh_war = list(entry)
    qh_war = list(entry)
    ncol = stat_sb[:, 30:31]
    last_exp = None
    last_ptuser = [None, None]
    sGn_free = {}
    s_s = None
    s_a = None
    for h in range(NH):
        def ldk(e, h=h):
            return [e.dma_start(out=kh[:, r * 2048:(r + 1) * 2048],
                                in_=gk[r, h * P:(h + 1) * P, :]) for r in range(4)]
        s_kh = pr.dma("gpsimd", ldk, deps=kh_war, n=4)

        def ldv(e, h=h):
            return [e.dma_start(
                out=vh.rearrange("p (rm j) -> p rm j", j=P)[:, r * 16:(r + 1) * 16, :],
                in_=gv[r].rearrange("(m p) d -> p m d", p=P)[:, :, h * P:(h + 1) * P])
                for r in range(4)]
        s_vh = pr.dma("gpsimd", ldv, deps=vh_war, n=4)

        def ldq(e, h=h):
            return [e.dma_start(out=qh[:, :], in_=qf_in[h * P:(h + 1) * P, :])]
        s_qh = pr.dma("gpsimd", ldq, deps=qh_war, n=1)

        nb = bank[6][0:1, 0:512]
        chunks = [(qh, cq, 26) for cq in range(4)] + [(kh, ck, 29) for ck in range(16)]
        prev = sGn_free.get(6)
        s_mq = None
        s_mk = None
        first = {26: True, 29: True}
        for src_t, cq, col in chunks:
            s_sqc = pr.step("scalar", (lambda e, src_t=src_t, cq=cq: e.activation(
                nsc[:, 0:512], src_t[:, cq * 512:(cq + 1) * 512], AF.Square)),
                deps=[s_qh, s_kh, prev] + ([last_exp] if last_exp else []))
            s_nmc = pr.step("tensor", (lambda e: e.matmul(
                nb, ones_sb[:, 0:1], nsc[:, 0:512], start=True, stop=True)),
                deps=[s_sqc, prev])
            if first[col]:
                s_red = pr.step("vector", (lambda e, col=col: e.reduce_max(
                    stat_sb[0:1, col:col + 1], nb, AXL.X)), deps=[s_nmc, prev])
                first[col] = False
            else:
                s_tmp = pr.step("vector", (lambda e: e.reduce_max(
                    stat_sb[0:1, 31:32], nb, AXL.X)), deps=[s_nmc, prev])
                s_red = pr.step("vector", (lambda e, col=col: e.tensor_tensor(
                    stat_sb[0:1, col:col + 1], stat_sb[0:1, col:col + 1],
                    stat_sb[0:1, 31:32], ALU.max)), deps=[s_tmp])
            prev = s_red
            if col == 26:
                s_mq = s_red
            else:
                s_mk = s_red
        s_c1 = pr.step("scalar", lambda e: e.activation(
            stat_sb[0:1, 26:27], stat_sb[0:1, 26:27], AF.Sqrt), deps=[s_mq, s_mk])
        s_c2 = pr.step("scalar", lambda e: e.activation(
            stat_sb[0:1, 29:30], stat_sb[0:1, 29:30], AF.Sqrt), deps=[s_mk, s_c1])
        s_c3 = pr.step("vector", lambda e: e.tensor_tensor(
            stat_sb[0:1, 26:27], stat_sb[0:1, 26:27], stat_sb[0:1, 29:30], ALU.mult),
            deps=[s_c1, s_c2])
        s_c4 = pr.step("vector", lambda e: e.tensor_scalar(
            stat_sb[0:1, 26:27], stat_sb[0:1, 26:27], -ISQ_DH, None, ALU.mult),
            deps=[s_c3])
        s_c5 = pr.step("tensor", lambda e: e.matmul(
            bank[6][:, 0:1], X["onesf"].ap()[0:1, 0:P], stat_sb[0:1, 26:27],
            start=True, stop=True), deps=[s_c4, s_mq, X["s_onesf"]])
        s_c6 = pr.step("vector", lambda e: e.tensor_copy(ncol, bank[6][:, 0:1]),
                       deps=[s_c5] + ([last_exp] if last_exp else []))
        sGn_free[6] = s_c6

        s_f1 = pr.step("scalar", lambda e: e.activation(nsc[:, :], qh[:, :],
                                                        AF.Relu, scale=-1.0),
                       deps=[s_qh, s_c6] + ([last_exp] if last_exp else []))
        s_f2 = pr.step("scalar", lambda e: e.activation(qfe[:, :], nsc[:, :],
                                                        AF.Exp, scale=-1.0),
                       deps=[s_f1, sGn_free.get("qfe")])
        s_f3 = pr.step("vector", lambda e: e.tensor_relu(nsc[:, :], qh[:, :]),
                       deps=[s_f2, s_qh])
        s_f4 = pr.step("vector", lambda e: e.tensor_tensor(qfe[:, :], qfe[:, :],
                                                           nsc[:, :], ALU.add),
                       deps=[s_f3])

        for n in range(4):
            bl = bank[2][0:1, 0:512]
            bd = bank[7][0:1, 0:512]
            ba = bank[3]
            bm = bank[5]
            s_mo = pr.step("tensor", (lambda e, h=h, n=n, bm=bm: e.matmul(
                bm, memb[:, h * 129:h * 129 + 128], qfe[:, n * 512:(n + 1) * 512],
                start=True, stop=True)),
                deps=[s_memb, s_f4, sGn_free.get("mo")])
            s_de = pr.step("tensor", (lambda e, h=h, n=n, bd=bd: e.matmul(
                bd, memb[:, h * 129 + 128:h * 129 + 129], qfe[:, n * 512:(n + 1) * 512],
                start=True, stop=True)),
                deps=[s_memb, s_f4, sGn_free.get("l")])
            nkt = NKT
            s_l = None
            s_a = None
            for kt in range(nkt):
                koff = (kt % 4) * 2048 + (kt // 4) * P
                sb = bank[kt % 2]
                ptb = pt[kt % 2]

                def smm(e, koff=koff, sb=sb, n=n):
                    return e.matmul(sb, kh[:, koff:koff + P],
                                    qh[:, n * 512:(n + 1) * 512], start=True, stop=True)
                s_s = pr.step("tensor", smm,
                              deps=[s_kh, s_qh, last_ptuser[kt % 2], s_c6,
                                    sGn_free.get(("s", kt % 2))])

                def expf(e, sb=sb, ptb=ptb):
                    return e.activation(ptb, sb, AF.Exp, bias=ncol, scale=ISQ_DH)
                s_e = pr.step("scalar", expf, deps=[s_s, s_c6, eps_dep, s_f2])
                last_exp = s_e
                sGn_free[("s", kt % 2)] = s_e

                def lmm(e, ptb=ptb, bl=bl, kt=kt, nkt=nkt):
                    return e.matmul(bl, ones_sb[:, 0:1], ptb,
                                    start=(kt == 0), stop=(kt == nkt - 1))
                s_l = pr.step("tensor", lmm,
                              deps=[s_e] + ([sGn_free.get("l2"), s_de] if kt == 0 else []))

                def amm(e, ptb=ptb, ba=ba, koff=koff, kt=kt, nkt=nkt):
                    return e.matmul(ba, vh[:, koff:koff + P], ptb,
                                    start=(kt == 0), stop=(kt == nkt - 1))
                s_a = pr.step("tensor", amm,
                              deps=[s_e, s_vh] + ([sGn_free.get("a")] if kt == 0 else []))
                last_ptuser[kt % 2] = s_a
            s_r1 = pr.step("vector", (lambda e, bl=bl: e.reciprocal(
                X["lrow"][0:1, 0:512], bl)), deps=[s_l, sGn_free.get("lrow")])
            s_r2 = pr.step("vector", (lambda e, bd=bd: e.reciprocal(
                X["lrow"][0:1, 512:1024], bd)), deps=[s_de, s_r1])
            sGn_free["l"] = s_r2
            sGn_free["l2"] = s_r2
            s_r3 = pr.step("vector", (lambda e, h=h: e.tensor_scalar(
                X["lrow"][0:1, 0:512], X["lrow"][0:1, 0:512],
                gsb[0:1, 32 + h:33 + h], None, ALU.mult)), deps=[s_r1, s_g2])
            s_r4 = pr.step("vector", (lambda e, h=h: e.tensor_scalar(
                X["lrow"][0:1, 512:1024], X["lrow"][0:1, 512:1024],
                gsb[0:1, 16 + h:17 + h], None, ALU.mult)), deps=[s_r2, s_g1])
            s_b1 = pr.step("tensor", (lambda e: e.matmul(
                bank[4][:, 0:512], X["onesf"].ap()[0:1, 0:P],
                X["lrow"][0:1, 0:512], start=True, stop=True)),
                deps=[s_r3, sGn_free.get("b"), X["s_onesf"]])
            s_b2 = pr.step("tensor", (lambda e: e.matmul(
                bank[6][:, 0:512], X["onesf"].ap()[0:1, 0:P],
                X["lrow"][0:1, 512:1024], start=True, stop=True)),
                deps=[s_r4, s_b1, s_c6, sGn_free.get("b2")])
            sGn_free["lrow"] = s_b2
            s_bc1 = pr.step("vector", (lambda e: e.tensor_copy(rbx, bank[4][:, 0:512])),
                            deps=[s_b1, sGn_free.get("rbx")])
            s_bc2 = pr.step("vector", (lambda e: e.tensor_copy(rbx2, bank[6][:, 0:512])),
                            deps=[s_b2, sGn_free.get("rbx")])
            sGn_free["b"] = s_bc2
            s_cv1 = pr.step("vector", (lambda e, h=h, n=n, ba=ba: e.tensor_tensor(
                big1[:, h * 2048 + n * 512:h * 2048 + (n + 1) * 512], ba, rbx, ALU.mult)),
                deps=[s_a, s_bc1])
            sGn_free["a"] = s_cv1
            s_cv2 = pr.step("vector", (lambda e, bm=bm: e.tensor_tensor(
                cmbt, bm, rbx2, ALU.mult)), deps=[s_mo, s_bc2, sGn_free.get("cmbt")])
            sGn_free["mo"] = s_cv2
            s_cv3 = pr.step("vector", (lambda e, h=h, n=n: e.tensor_tensor(
                big1[:, h * 2048 + n * 512:h * 2048 + (n + 1) * 512],
                big1[:, h * 2048 + n * 512:h * 2048 + (n + 1) * 512], cmbt, ALU.add)),
                deps=[s_cv1, s_cv2])
            sGn_free["rbx"] = s_cv3
            sGn_free["cmbt"] = s_cv3
            sGn_free["qfe"] = s_mo
            sGn_free["b2"] = s_bc2
            sGn_free[6] = s_bc2
            accs_done.append(s_cv3)
        kh_war = [s_s]
        vh_war = [s_a]
        qh_war = [s_s, s_f3]

    def ldwo(e):
        srcv = woT_in.rearrange("(hh p) c -> p hh c", p=P)
        outv = big2.rearrange("p (hh c) -> p hh c", c=D)
        return [e.dma_start(out=outv[:, 4 * q:4 * q + 4, :],
                            in_=srcv[:, 4 * q:4 * q + 4, :]) for q in range(4)]
    s_wo = pr.dma("gpsimd", ldwo, deps=[kh_war, vh_war] + accs_done, n=4)

    xres = xt_sb[:, 0:1024]
    load_war = [list(entry), list(entry)]
    obank_war = {k: [accs_done[-1], last_exp] for k in range(8)}
    store_of = {}
    for tl in range(NT):
        for Dc in range(4):
            ob = (tl * 4 + Dc) % 8

            def omm(e, tl=tl, Dc=Dc, ob=ob):
                for hh in range(NH):
                    mi = e.matmul(bank[ob],
                                  big1[:, hh * 2048 + tl * P:hh * 2048 + (tl + 1) * P],
                                  big2[:, hh * D + Dc * 512:hh * D + (Dc + 1) * 512],
                                  start=(hh == 0), stop=(hh == NH - 1))
                return mi
            s_omm = pr.step("tensor", omm, deps=[s_wo, obank_war[ob]] + accs_done)
            xsl = Dc % 2
            xr = xres[:, xsl * 512:(xsl + 1) * 512]

            def lx(e, tl=tl, Dc=Dc, xr=xr):
                return [e.dma_start(out=xr, in_=x_src[tl * P:(tl + 1) * P,
                                                      Dc * 512:(Dc + 1) * 512])]
            s_lx = pr.dma("sync", lx, deps=load_war[xsl], n=1)
            s_ad = pr.step("vector", (lambda e, ob=ob, xr=xr: e.tensor_tensor(
                xr, bank[ob], xr, ALU.add)), deps=[s_omm, s_lx])
            obank_war[ob] = [s_ad]
            load_war[xsl] = [s_ad]

            def sx(e, tl=tl, Dc=Dc, xr=xr):
                return [e.dma_start(out=x_dst[tl * P:(tl + 1) * P,
                                              Dc * 512:(Dc + 1) * 512], in_=xr)]
            s_sx = pr.dma("sync", sx, deps=[s_ad], n=1)
            store_of[(tl, Dc)] = s_sx
    return list(store_of.values())


def build_launch_C():
    nc = bass.Bass("TRN2", target_bir_lowering=False)

    x_in = nc.dram_tensor("x_in", [T, D], F32, kind="ExternalInput").ap()
    qf_in = nc.dram_tensor("qf_in", [D, T], BF16, kind="ExternalInput").ap()
    gk = nc.dram_tensor("gk", [4, D, T], BF16, kind="ExternalInput").ap()
    gv = nc.dram_tensor("gv", [4, T, D], BF16, kind="ExternalInput").ap()
    mem_parts = nc.dram_tensor("mem_parts", [4, NH * P, 129], BF16,
                               kind="ExternalInput").ap()
    gate_in = nc.dram_tensor("gate_in", [1, 16], F32, kind="ExternalInput").ap()
    woT_in = nc.dram_tensor("woT_in", [D, D], F32, kind="ExternalInput").ap()
    weightT = nc.dram_tensor("weightT", [D, D], F32, kind="ExternalInput").ap()
    AsT = nc.dram_tensor("AsT", [D, P], F32, kind="ExternalInput").ap()
    Bst = nc.dram_tensor("Bst", [P, D], F32, kind="ExternalInput").ap()
    rwT = nc.dram_tensor("rwT", [D, 8], F32, kind="ExternalInput").ap()
    repl8_in = nc.dram_tensor("repl8_in", [8, P], F32, kind="ExternalInput").ap()
    ident_in = nc.dram_tensor("ident_in", [P, P], F32, kind="ExternalInput").ap()
    ones_in = nc.dram_tensor("ones_in", [P, P], F32, kind="ExternalInput").ap()
    consts_in = nc.dram_tensor("consts_in", [P, 8], F32, kind="ExternalInput").ap()

    y_out = nc.dram_tensor("y_out", [T, D], F32, kind="ExternalOutput").ap()

    with contextlib.ExitStack() as _st:
        def _sb(name, shape, dt):
            return _st.enter_context(nc.sbuf_tensor(name, shape, dt))
        big1 = _sb("big1", [P, 32 * 1024], BF16)
        big2 = _sb("big2", [P, 2 * NFC * 512], BF16)
        mid1 = _sb("mid1", [P, NDC * 1024], BF16)
        xt_sb = _sb("xt_sb", [P, 2 * D], F32)
        xn_sb = _sb("xn_sb", [P, 2 * D], BF16)
        stat_sb = _sb("stat_sb", [P, 32], F32)
        w_sb = _sb("w_sb", [P, NT * 8], F32)
        p8_sb = _sb("p8_sb", [P, 8], F32)
        p8b_sb = _sb("p8b_sb", [P, 8], F32)
        wg2_sb = _sb("wg2_sb", [P, NDC * 8], BF16)
        ones_sb = _sb("ones_sb", [P, P], BF16)
        consts_sb = _sb("consts_sb", [P, 8], F32)
        lrow_sb = _sb("lrow_sb", [1, 1024], F32)
        lrow2_sb = _sb("lrow2_sb", [8, 512], BF16)
        repl8_sb = _sb("repl8_sb", [8, P], BF16)
        identf_sb = _sb("identf_sb", [P, P], F32)
        onesf_sb = _sb("onesf_sb", [P, P], F32)
        memf_sb = _sb("memf_sb", [P, 2064], F32)
        memb_sb = _sb("memb_sb", [P, 2064], BF16)
        gsb = _sb("gsb", [1, 64], F32)
        ps = _st.enter_context(nc.psum_tensor("ps", [P, 8 * 512], F32))
        block = _st.enter_context(nc.Block())

        pr = Prog(nc)
        bank = [ps[:, k * 512:(k + 1) * 512] for k in range(8)]
        xsq_sb = mid1[:, 2 * D:3 * D]

        s_ones = pr.dma("gpsimd", lambda e: [e.dma_start(out=ones_sb[:, :], in_=ones_in)])
        s_csts = pr.dma("sync", lambda e: [e.dma_start(out=consts_sb[:, :], in_=consts_in)])
        globals()["_EPS_AP"] = consts_sb[:, 0:1]
        globals()["_EPS_DEP"] = s_csts
        s_wg2 = pr.dma("gpsimd", lambda e: [e.dma_start(
            out=wg2_sb.rearrange("p (dc e8) -> p dc e8", e8=8),
            in_=rwT.rearrange("(dc p) e8 -> p dc e8", p=P))])
        s_repl = pr.dma("gpsimd", lambda e: [e.dma_start(out=repl8_sb[:, :], in_=repl8_in)])
        s_idf = pr.dma("sync", lambda e: [e.dma_start(out=identf_sb[:, :], in_=ident_in)])
        s_onf = pr.dma("sync", lambda e: [e.dma_start(out=onesf_sb[:, :], in_=ones_in)])

        X = dict(big1=big1, big2=big2, mid1=mid1, xt_sb=xt_sb,
                 xn_sb=xn_sb, stat_sb=stat_sb, w_sb=w_sb, p8_sb=p8_sb,
                 p8b_sb=p8b_sb, ones_sb=ones_sb, bank=bank, xsq_sb=xsq_sb,
                 s_csts=s_csts, lrow=lrow_sb, lrow2=lrow2_sb,
                 repl8=repl8_sb, identf=identf_sb, s_repl8=s_repl, s_identf=s_idf,
                 onesf=onesf_sb, s_onesf=s_onf, memf=memf_sb, memb=memb_sb, gsb=gsb)

        attn_stores = rec_l1_attn(pr, X, (qf_in, gk, gv, mem_parts, gate_in, woT_in),
                                  x_in, y_out, [s_csts, s_ones])

        ffn1_stores, _ = rec_ffn1(pr, X, (weightT, AsT, Bst, wg2_sb, s_wg2),
                                  y_out, attn_stores, attn_stores,
                                  [attn_stores, attn_stores])

        pr.emit(block, final_deps=ffn1_stores + attn_stores)
        pr.close()
    return nc


def prep_launch_C_inputs(inputs, x4_shards, qf, kf, v, mems):
    A = inputs["L1_ffn_lora_A"]
    AsT = np.ascontiguousarray(A.transpose(2, 0, 1).reshape(D, P))
    Bm = inputs["L1_ffn_lora_B"]
    Bst = np.ascontiguousarray(Bm.transpose(0, 2, 1).reshape(P, D))
    common = dict(
        gate_in=np.ascontiguousarray(inputs["L1_attn_gate"].reshape(1, 16)),
        woT_in=np.ascontiguousarray(inputs["L1_attn_Wo"].T),
        weightT=np.ascontiguousarray(inputs["L1_ffn_weight"].T),
        AsT=AsT, Bst=Bst,
        rwT=np.ascontiguousarray(inputs["L1_ffn_router_w"].T),
        repl8_in=np.ascontiguousarray(
            np.kron(np.eye(8, dtype=np.float32), np.ones((1, 16), np.float32)) * 2.0),
        ident_in=np.eye(P, dtype=np.float32),
        ones_in=np.ones((P, P), np.float32),
        consts_in=np.tile(np.array([[EPS, 1.0, 0, 0, 0, 0, 0, 0]], np.float32), (P, 1)),
    )
    maps = []
    for c in range(8):
        grp = (c // 4) * 4
        maps.append(dict(
            common, x_in=x4_shards[c], qf_in=qf[c],
            gk=np.ascontiguousarray(np.stack([kf[grp + r] for r in range(4)])),
            gv=np.ascontiguousarray(np.stack([v[grp + r] for r in range(4)])),
            mem_parts=np.ascontiguousarray(np.stack([mems[grp + r] for r in range(4)])),
        ))
    return maps



def shard_tokens(x_full):
    shards = []
    xt = np.asarray(x_full).reshape(2, 64, P, -1)
    for core in range(8):
        b, g = core // 4, core % 4
        shards.append(np.ascontiguousarray(xt[b, g::4].reshape(T, -1)))
    return shards


def unshard_tokens(shards):
    d = shards[0].shape[-1]
    out = np.empty((2, 64, P, d), dtype=np.asarray(shards[0]).dtype)
    for core in range(8):
        b, g = core // 4, core % 4
        out[b, g::4] = np.asarray(shards[core]).reshape(NT, P, d)
    return out.reshape(2, S, d)


def prep_launch_A_inputs(inputs, x_shards):
    W1 = np.concatenate([inputs["L0_ffn_W1"], inputs["L0_ffn_Ws1"][None]], axis=0)
    W1r = np.ascontiguousarray(W1.reshape(NE, NDC, P, NFC, P).transpose(0, 3, 1, 2, 4))
    W2 = np.concatenate([inputs["L0_ffn_W2"], inputs["L0_ffn_Ws2"][None]], axis=0)
    W2bh = np.ascontiguousarray(W2.reshape(NE, NFC, P, 4, 512).transpose(0, 3, 1, 2, 4))
    wqkv = np.ascontiguousarray(np.stack([inputs["L0_attn_Wq"],
                                          inputs["L0_attn_Wk"],
                                          inputs["L0_attn_Wv"]]))
    bqk = np.ascontiguousarray(np.stack([
        inputs["L0_attn_bq"].reshape(NDC, P).T,
        inputs["L0_attn_bk"].reshape(NDC, P).T]))
    common = dict(
        W1r=W1r, W2b=W2bh, Wg=np.ascontiguousarray(inputs["L0_ffn_Wg"]),
        ln_gb=np.ascontiguousarray(np.stack([inputs["L0_attn_ln_g"],
                                             inputs["L0_attn_ln_b"]])),
        wqkv=wqkv, bqk=bqk,
        bv_row=np.ascontiguousarray(inputs["L0_attn_bv"][None]),
        ones_in=np.ones((P, P), np.float32),
        consts_in=np.tile(np.array([[EPS, 1.0, 0.0, 0.0, 0.0, 0.0, 0.0, 0.0]],
                                   np.float32), (P, 1)),
    )
    return [dict(common, x_in=x_shards[c]) for c in range(8)]


def run_launch_A(inputs):
    x_shards = shard_tokens(inputs["L0_attn_X"])
    in_maps = prep_launch_A_inputs(inputs, x_shards)
    nc = build_launch_A()
    res = run_bass_kernel_spmd(nc, in_maps, core_ids=list(range(8)))
    return res.results


_NC_CACHE = {}


def _get_nc(which):
    if which not in _NC_CACHE:
        _NC_CACHE[which] = {"A": build_launch_A, "B": build_launch_B,
                            "C": build_launch_C}[which]()
    return _NC_CACHE[which]


def kernel(**inputs):
    cores = list(range(8))
    x_shards = shard_tokens(np.ascontiguousarray(inputs["L0_attn_X"]))

    resA = run_bass_kernel_spmd(_get_nc("A"), prep_launch_A_inputs(inputs, x_shards),
                                core_ids=cores).results
    x1 = [np.asarray(resA[c]["x_out"]) for c in range(8)]
    qf = [np.asarray(resA[c]["qf_out"]) for c in range(8)]
    kf = [np.asarray(resA[c]["kf_out"]) for c in range(8)]
    vv = [np.asarray(resA[c]["v_out"]) for c in range(8)]

    resB = run_bass_kernel_spmd(_get_nc("B"),
                                prep_launch_B_inputs(inputs, x1, qf, kf, vv),
                                core_ids=cores).results
    x4 = [np.asarray(resB[c]["x_out"]) for c in range(8)]
    qf1 = [np.asarray(resB[c]["qf_out"]) for c in range(8)]
    kf1 = [np.asarray(resB[c]["kf_out"]) for c in range(8)]
    vv1 = [np.asarray(resB[c]["v_out"]) for c in range(8)]
    mems = [np.asarray(resB[c]["mem_out"]) for c in range(8)]

    resC = run_bass_kernel_spmd(_get_nc("C"),
                                prep_launch_C_inputs(inputs, x4, qf1, kf1, vv1, mems),
                                core_ids=cores).results
    y = unshard_tokens([np.asarray(resC[c]["y_out"]) for c in range(8)])
    return np.ascontiguousarray(y.astype(np.float32))


# version 02 (final)
# speedup vs baseline: 1.4885x; 1.4885x over previous
import contextlib
import numpy as np
import ml_dtypes
import concourse.bass as bass
import concourse.mybir as mybir
from concourse.bass_utils import run_bass_kernel_spmd

F32 = mybir.dt.float32
BF16 = mybir.dt.bfloat16
AF = mybir.ActivationFunctionType
ALU = mybir.AluOpType
AXL = mybir.AxisListType

P = 128
D = 2048
T = 2048
NT = 16
NDC = 16
NH = 16
DH = 128
FF = 4096
NFC = 32
NE = 9
S = 8192
NKT = 64
EPS = 1e-5
ISQ_DH = float(1.0 / np.sqrt(128.0))

BF = ml_dtypes.bfloat16
ENGS = ("sync", "gpsimd", "tensor", "vector", "scalar")



class Step:
    __slots__ = ("eng", "cum")

    def __init__(self, eng, cum):
        self.eng = eng
        self.cum = cum


class Prog:

    def __init__(self, nc):
        self.nc = nc
        self.q = {e: [] for e in ENGS}
        self.cum = {e: 0 for e in ENGS}
        self.sem_guards = []
        self.sems = {}

    def _flat(self, deps):
        for d in deps:
            if d is None:
                continue
            if isinstance(d, (list, tuple)):
                yield from self._flat(d)
            else:
                yield d

    def _mk(self, eng, fn, deps, kind, weight):
        waits = {}
        for d in self._flat(deps):
            waits[d.eng] = max(waits.get(d.eng, 0), d.cum)
        self.q[eng].append((fn, waits, kind))
        self.cum[eng] += weight
        return Step(eng, self.cum[eng])

    def step(self, eng, fn, deps=()):
        return self._mk(eng, fn, deps, "c", 1)

    def dma(self, eng, fn, deps=(), n=1):
        return self._mk(eng, fn, deps, "d", 16 * n)

    def emit(self, block, final_deps):
        for e in ENGS:
            g = self.nc.semaphore("sem_" + e)
            self.sem_guards.append(g)
            self.sems[e] = g.__enter__()
        semh = self.sems

        def section(eng_name):
            def body(eng):
                mysem = semh[eng_name]
                for fn, waits, kind in self.q[eng_name]:
                    for we, wv in waits.items():
                        eng.wait_ge(semh[we], wv)
                    r = fn(eng)
                    if kind == "c":
                        r.then_inc(mysem, 1)
                    else:
                        rr = r if isinstance(r, (list, tuple)) else [r]
                        for instr in rr:
                            instr.then_inc(mysem, 16)
            return body

        for eng_name in ENGS:
            if self.q[eng_name]:
                getattr(block, eng_name)(section(eng_name))

        def tail(eng):
            waits = {}
            for d in self._flat(final_deps):
                waits[d.eng] = max(waits.get(d.eng, 0), d.cum)
            for we, wv in waits.items():
                eng.wait_ge(semh[we], wv)
        block.sync(tail)

    def close(self):
        for g in reversed(self.sem_guards):
            g.__exit__(None, None, None)



def rec_ln(pr, st, xsq, xt, dep_x, out_ap, out_war=(), war_sq=None, base=0):
    c = [st[:, base + i:base + i + 1] for i in range(9)]
    s_sq = pr.step("scalar", lambda e: e.activation(xsq[:, :], xt, AF.Square),
                   deps=[dep_x, war_sq])
    s_s1 = pr.step("vector", lambda e: e.reduce_sum(c[0], xt, AXL.X), deps=[dep_x])
    s_s2 = pr.step("vector", lambda e: e.reduce_sum(c[1], xsq[:, :], AXL.X), deps=[s_sq])
    s_me = pr.step("vector", lambda e: [e.tensor_scalar(c[2], c[0], 1.0 / D, None, ALU.mult),
                                        e.tensor_scalar(c[3], c[1], 1.0 / D, None, ALU.mult)][-1],
                   deps=[s_s1, s_s2])
    s_m2 = pr.step("vector", lambda e: e.tensor_scalar(c[4], c[2], c[2], None, ALU.mult),
                   deps=[s_me])
    s_va = pr.step("vector", lambda e: e.tensor_tensor(c[5], c[3], c[4], ALU.subtract),
                   deps=[s_m2])
    s_sd = pr.step("scalar", lambda e: e.activation(c[6], c[5], AF.Sqrt,
                                                    bias=globals()["_EPS_AP"]),
                   deps=[s_va, globals().get("_EPS_DEP")])
    s_rs = pr.step("vector", lambda e: e.reciprocal(c[7], c[6]), deps=[s_sd])
    s_nm = pr.step("vector", lambda e: e.scalar_tensor_tensor(c[8], c[2], -1.0, c[7],
                                                              ALU.mult, ALU.mult),
                   deps=[s_rs])
    s_ap = pr.step("scalar", lambda e: e.activation(out_ap, xt, AF.Identity,
                                                    bias=c[8], scale=c[7]),
                   deps=[s_nm, dep_x] + list(out_war))
    return s_ap, s_s2


def rec_top2(pr, st, p8, msk, ps_ap, dep_ps, wout_ap, wscale=0.5):
    lm, nlm, sm, m1, m2, rs = (st[:, 24 + i:25 + i] for i in range(6))
    s_lm = pr.step("vector", lambda e: e.reduce_max(lm, ps_ap, AXL.X), deps=[dep_ps])
    s_nl = pr.step("vector", lambda e: e.tensor_scalar(nlm, lm, -1.0, None, ALU.mult),
                   deps=[s_lm])
    s_ex = pr.step("scalar", lambda e: e.activation(p8[:, :], ps_ap, AF.Exp,
                                                    bias=nlm, accum_out=sm),
                   deps=[s_nl, dep_ps])
    s_rc = pr.step("vector", lambda e: e.reciprocal(sm, sm), deps=[s_ex])
    s_pn = pr.step("vector", lambda e: e.tensor_scalar(p8[:, :], p8[:, :], sm, None, ALU.mult),
                   deps=[s_rc])
    s_m1 = pr.step("vector", lambda e: e.reduce_max(m1, p8[:, :], AXL.X), deps=[s_pn])
    s_mk = pr.step("vector", lambda e: e.tensor_scalar(msk[:, :], p8[:, :], m1, None, ALU.is_lt),
                   deps=[s_m1])
    s_mk2 = pr.step("vector", lambda e: e.tensor_tensor(msk[:, :], msk[:, :], p8[:, :], ALU.mult),
                    deps=[s_mk])
    s_m2 = pr.step("vector", lambda e: e.reduce_max(m2, msk[:, :], AXL.X), deps=[s_mk2])
    s_su = pr.step("vector", lambda e: e.tensor_tensor(rs, m1, m2, ALU.add), deps=[s_m2])
    s_rr = pr.step("vector", lambda e: e.reciprocal(rs, rs), deps=[s_su])
    s_rh = pr.step("vector", lambda e: e.tensor_scalar(rs, rs, float(wscale), None, ALU.mult),
                   deps=[s_rr])
    s_ge = pr.step("vector", lambda e: e.tensor_scalar(wout_ap, p8[:, :], m2, None, ALU.is_ge),
                   deps=[s_m2])
    s_w1 = pr.step("vector", lambda e: e.tensor_tensor(wout_ap, wout_ap, p8[:, :], ALU.mult),
                   deps=[s_ge])
    s_w2 = pr.step("vector", lambda e: e.tensor_scalar(wout_ap, wout_ap, rs, None, ALU.mult),
                   deps=[s_w1, s_rh])
    return s_w2, s_ex


def rec_ffn0(pr, bufs, xnf_d, w_getcol, x_src, x_dst, W1r, W2b, dep_ready, xres_war):
    xnfh, hfs, w1sb, w2b, xres, bank = (bufs[k] for k in
                                        ("xnfh", "hfs", "w1sb", "w2b", "xres", "bank"))
    last_h_mm = [None] * 4
    last_silu = [None, None]
    last_w1_mm = [None, None]
    last_w2_mm = [None, None]
    last_o_dve = [None] * 4
    last_hfs_reader = None
    store_of = {}
    load_war = [list(xres_war), list(xres_war)]
    s_xnfh = None

    for th in range(2):
        xv = xnf_d.rearrange("(dc p) t -> p dc t", p=P)

        def ld_xnfh(e, th=th, xv=xv):
            outv = xnfh.rearrange("p (dc t) -> p dc t", t=1024)
            return [e.dma_start(out=outv[:, 4 * q:4 * q + 4, :],
                                in_=xv[:, 4 * q:4 * q + 4, th * 1024:(th + 1) * 1024])
                    for q in range(4)]
        s_xnfh = pr.dma("gpsimd", ld_xnfh, deps=[dep_ready] + last_h_mm, n=4)

        for e_i in range(NE):
            for fc in range(NFC):
                sl = fc % 2
                w1dst = w1sb[:, sl * D:(sl + 1) * D]

                def ld_w1(e, e_i=e_i, fc=fc, w1dst=w1dst):
                    return [e.dma_start(out=w1dst.rearrange("p (dc f) -> p dc f", f=P),
                                        in_=W1r[e_i, fc].rearrange("dc p f -> p dc f"))]
                s_w1l = pr.dma("gpsimd", ld_w1, deps=[last_w1_mm[sl]], n=1)

                b0, b1 = 2 * sl, 2 * sl + 1

                def mm_w1(e, w1dst=w1dst, b0=b0, b1=b1, xnfh=xnfh):
                    for dc in range(NDC):
                        e.matmul(bank[b0], w1dst[:, dc * P:(dc + 1) * P],
                                 xnfh[:, dc * 1024:dc * 1024 + 512],
                                 start=(dc == 0), stop=(dc == NDC - 1))
                        mi = e.matmul(bank[b1], w1dst[:, dc * P:(dc + 1) * P],
                                      xnfh[:, dc * 1024 + 512:(dc + 1) * 1024],
                                      start=(dc == 0), stop=(dc == NDC - 1))
                    return mi
                s_mm = pr.step("tensor", mm_w1, deps=[s_w1l, s_xnfh, last_silu[sl]])
                last_w1_mm[sl] = s_mm
                last_h_mm[b0] = s_mm
                last_h_mm[b1] = s_mm

                def silu(e, fc=fc, b0=b0, b1=b1, hfs=hfs):
                    e.activation(hfs[:, fc * 1024:fc * 1024 + 512], bank[b0], AF.Silu)
                    return e.activation(hfs[:, fc * 1024 + 512:(fc + 1) * 1024],
                                        bank[b1], AF.Silu)
                s_si = pr.step("scalar", silu, deps=[s_mm, last_hfs_reader])
                last_silu[sl] = s_si
            silu_last = last_silu[(NFC - 1) % 2]

            for Dc in range(4):
                sl = Dc % 2
                w2dst = w2b[:, sl * NFC * 512:(sl + 1) * NFC * 512]

                def ld_w2(e, e_i=e_i, Dc=Dc, w2dst=w2dst):
                    outv = w2dst.rearrange("p (fc j) -> p fc j", j=512)
                    return [e.dma_start(out=outv[:, 16 * q:16 * q + 16, :],
                                        in_=W2b[e_i, Dc, 16 * q:16 * q + 16].rearrange(
                                            "fc p j -> p fc j"))
                            for q in range(2)]
                s_w2l = pr.dma("gpsimd", ld_w2, deps=[last_w2_mm[sl]], n=2)

                for tp in range(4):
                    ob0 = 4 + 2 * (tp % 2)
                    ob1 = ob0 + 1

                    def mm_w2(e, tp=tp, w2dst=w2dst, ob0=ob0, ob1=ob1, hfs=hfs):
                        for fc in range(NFC):
                            e.matmul(bank[ob0],
                                     hfs[:, fc * 1024 + 2 * tp * P:fc * 1024 + (2 * tp + 1) * P],
                                     w2dst[:, fc * 512:(fc + 1) * 512],
                                     start=(fc == 0), stop=(fc == NFC - 1))
                            mi = e.matmul(bank[ob1],
                                          hfs[:, fc * 1024 + (2 * tp + 1) * P:
                                              fc * 1024 + (2 * tp + 2) * P],
                                          w2dst[:, fc * 512:(fc + 1) * 512],
                                          start=(fc == 0), stop=(fc == NFC - 1))
                        return mi
                    s_w2mm = pr.step("tensor", mm_w2,
                                     deps=[s_w2l, silu_last,
                                           last_o_dve[ob0 - 4], last_o_dve[ob1 - 4]])
                    last_hfs_reader = s_w2mm
                    last_w2_mm[sl] = s_w2mm

                    for tt in range(2):
                        tloc = th * 8 + tp * 2 + tt
                        obank = ob0 if tt == 0 else ob1
                        src = x_src if e_i == 0 else x_dst
                        xsl = tt
                        xr = xres[:, xsl * 512:(xsl + 1) * 512]

                        def ld_x(e, src=src, tloc=tloc, Dc=Dc, xr=xr):
                            return [e.dma_start(out=xr, in_=src[tloc * P:(tloc + 1) * P,
                                                                Dc * 512:(Dc + 1) * 512])]
                        s_lx = pr.dma("sync", ld_x,
                                      deps=load_war[xsl] + [store_of.get((tloc, Dc))], n=1)

                        wcol = w_getcol(tloc, e_i)

                        def dve_add(e, obank=obank, wcol=wcol, xr=xr):
                            return e.scalar_tensor_tensor(xr, bank[obank], wcol, xr,
                                                          ALU.mult, ALU.add)
                        s_dv = pr.step("vector", dve_add, deps=[s_w2mm, s_lx])
                        last_o_dve[obank - 4] = s_dv
                        load_war[xsl] = [s_dv]

                        def st_x(e, tloc=tloc, Dc=Dc, xr=xr):
                            return [e.dma_start(out=x_dst[tloc * P:(tloc + 1) * P,
                                                          Dc * 512:(Dc + 1) * 512], in_=xr)]
                        s_sx = pr.dma("sync", st_x, deps=[s_dv], n=1)
                        store_of[(tloc, Dc)] = s_sx
    return list(store_of.values())




def _rec_launchA_phases(pr, C, C2):
    (x_in, W1r, W2b, x_out, qf_out, kf_out, v_out, xnf_d, wqkv,
     big1, big2, mid1, w1sb, xt_sb, xsq_sb, xn_sb, gb_bc, stat_sb, w_sb,
     p8_sb, p8b_sb, wg_sb, ones_sb, gbrow_sb, bvrow_sb, bqk_sb, bank,
     s_ones, s_wg, s_bqk, s_gbrow, s_bvrow) = C
    xt_war = {0: [], 1: []}
    xn_war = {0: [], 1: []}
    stg_war = {0: [], 1: []}
    rt_free = [None, None]
    p1_stores = []
    war_sq = None
    s_w = None
    for t in range(NT):
        sl = t % 2
        xt = xt_sb[:, sl * D:(sl + 1) * D]
        s_lx = pr.dma("sync", (lambda e, t=t, xt=xt: [e.dma_start(
            out=xt, in_=x_in[t * P:(t + 1) * P, :])]), deps=xt_war[sl], n=1)
        xn = xn_sb[:, sl * D:(sl + 1) * D]
        s_ap, s_sq2 = rec_ln(pr, stat_sb, xsq_sb, xt, s_lx, xn,
                             out_war=xn_war[sl], war_sq=war_sq)
        war_sq = s_sq2
        xt_war[sl] = [s_ap]
        stg = mid1[:, sl * D:(sl + 1) * D]

        def tp_fn(e, xn=xn, stg=stg):
            return [e.dma_start(out=stg[:, dc * P:(dc + 1) * P],
                                in_=xn[:, dc * P:(dc + 1) * P], transpose=True)
                    for dc in range(NDC)]
        s_tp = pr.dma("sync", tp_fn, deps=[s_ap] + stg_war[sl], n=NDC)
        xn_war[sl] = [s_tp]

        def st_fn(e, t=t, stg=stg):
            return [e.dma_start(
                out=xnf_d.rearrange("(dc p) tt -> p dc tt", p=P)[:, :, t * P:(t + 1) * P],
                in_=stg.rearrange("p (dc j) -> p dc j", j=P))]
        s_st = pr.dma("sync", st_fn, deps=[s_tp], n=1)
        p1_stores.append(s_st)

        rb = t % 2
        ps_r = bank[rb][:, 0:8]

        def rt_fn(e, stg=stg, ps_r=ps_r):
            for dc in range(NDC):
                mi = e.matmul(ps_r, stg[:, dc * P:(dc + 1) * P],
                              wg_sb[:, dc * 8:(dc + 1) * 8],
                              start=(dc == 0), stop=(dc == NDC - 1))
            return mi
        s_rt = pr.step("tensor", rt_fn, deps=[s_tp, s_wg, rt_free[rb]])
        s_w, s_psfree = rec_top2(pr, stat_sb, p8_sb, p8b_sb, ps_r, s_rt,
                                 w_sb[:, t * 8:(t + 1) * 8])
        rt_free[rb] = s_psfree
        stg_war[sl] = [s_st, s_rt]

    bufs = dict(xnfh=mid1, hfs=big1, w1sb=w1sb, w2b=big2,
                xres=xt_sb[:, 0:1024], bank=bank)

    def w_getcol(tloc, e_i):
        if e_i == 8:
            return 0.5
        return w_sb[:, tloc * 8 + e_i:tloc * 8 + e_i + 1]

    dep_ready = p1_stores + [s_w] + stg_war[0] + stg_war[1] + \
        [rt_free[0], rt_free[1]]
    stores = rec_ffn0(pr, bufs, xnf_d, w_getcol, x_in, x_out, W1r, W2b,
                      dep_ready, xt_war[0] + xt_war[1] + [war_sq])

    ln_gb = C2["ln_gb"]
    bv_row = C2["bv_row"]
    s_gbrow = pr.dma("gpsimd", lambda e: [
        e.dma_start(out=gbrow_sb[:, 0:D], in_=ln_gb[0:1, :]),
        e.dma_start(out=gbrow_sb[:, D:2 * D], in_=ln_gb[1:2, :])], deps=[stores], n=2)
    s_bvrow = pr.dma("gpsimd", lambda e: [e.dma_start(out=bvrow_sb[:, :], in_=bv_row)],
                     deps=[stores])
    gb_done = []
    prev_bc = {0: stores, 1: stores}
    for r in range(2):
        for cq in range(4):
            bk = cq % 2

            def bc_mm(e, r=r, cq=cq, bk=bk):
                return e.matmul(bank[bk], ones_sb[0:1, 0:P],
                                gbrow_sb[0:1, r * D + cq * 512:r * D + (cq + 1) * 512],
                                start=True, stop=True)
            s_bm = pr.step("tensor", bc_mm, deps=[s_gbrow, s_ones, prev_bc[bk]])
            s_bc = pr.step("vector", (lambda e, r=r, cq=cq, bk=bk: e.tensor_copy(
                gb_bc[:, r * D + cq * 512:r * D + (cq + 1) * 512], bank[bk])),
                deps=[s_bm])
            prev_bc[bk] = [s_bc]
            gb_done.append(s_bc)

    xt3_war = {0: stores, 1: stores}
    h1_war = {0: [], 1: []}
    tmp_war = {0: stores, 1: stores}
    tp2_steps = []
    war_sq3 = war_sq
    for t in range(NT):
        sl = t % 2
        xt = xt_sb[:, sl * D:(sl + 1) * D]
        s_lx = pr.dma("sync", (lambda e, t=t, xt=xt: [e.dma_start(
            out=xt, in_=x_out[t * P:(t + 1) * P, :])]),
            deps=[stores] + xt3_war[sl], n=1)
        h1 = xn_sb[:, sl * D:(sl + 1) * D]
        s_ap1, s_sq1 = rec_ln(pr, stat_sb, xsq_sb, xt, s_lx, h1,
                              out_war=h1_war[sl], war_sq=war_sq3, base=0)
        xt3_war[sl] = [s_ap1]
        c = [stat_sb[:, 9 + i:10 + i] for i in range(9)]
        s_sqB = pr.step("scalar", (lambda e, h1=h1: e.activation(
            xsq_sb[:, :], h1, AF.Square)), deps=[s_ap1, s_sq1])
        s_s1B = pr.step("vector", (lambda e, h1=h1: e.reduce_sum(c[0], h1, AXL.X)),
                        deps=[s_ap1])
        s_s2B = pr.step("vector", lambda e: e.reduce_sum(c[1], xsq_sb[:, :], AXL.X),
                        deps=[s_sqB])
        war_sq3 = s_s2B
        s_meB = pr.step("vector", lambda e: [
            e.tensor_scalar(c[2], c[0], 1.0 / D, None, ALU.mult),
            e.tensor_scalar(c[3], c[1], 1.0 / D, None, ALU.mult)][-1],
            deps=[s_s1B, s_s2B])
        s_m2B = pr.step("vector", lambda e: e.tensor_scalar(c[4], c[2], c[2], None,
                                                            ALU.mult), deps=[s_meB])
        s_vaB = pr.step("vector", lambda e: e.tensor_tensor(c[5], c[3], c[4],
                                                            ALU.subtract), deps=[s_m2B])
        s_sdB = pr.step("scalar", lambda e: e.activation(c[6], c[5], AF.Sqrt,
                                                         bias=globals()["_EPS_AP"]),
                        deps=[s_vaB])
        s_rsB = pr.step("vector", lambda e: e.reciprocal(c[7], c[6]), deps=[s_sdB])
        s_nmB = pr.step("vector", lambda e: e.scalar_tensor_tensor(
            c[8], c[2], -1.0, c[7], ALU.mult, ALU.mult), deps=[s_rsB])
        tmp = mid1[:, sl * D:(sl + 1) * D]
        s_apB = pr.step("scalar", (lambda e, h1=h1, tmp=tmp: e.activation(
            tmp, h1, AF.Identity, bias=c[8], scale=c[7])),
            deps=[s_nmB] + tmp_war[sl])
        h1_war[sl] = [s_apB]
        s_mg = pr.step("vector", (lambda e, tmp=tmp: e.tensor_tensor(
            tmp, tmp, gb_bc[:, 0:D], ALU.mult)), deps=[s_apB] + gb_done)
        s_ab = pr.step("vector", (lambda e, tmp=tmp: e.tensor_tensor(
            tmp, tmp, gb_bc[:, D:2 * D], ALU.add)), deps=[s_mg])

        def tp2(e, tmp=tmp, t=t):
            return [e.dma_start(out=big1[:, dc * D + t * P:dc * D + (t + 1) * P],
                                in_=tmp[:, dc * P:(dc + 1) * P], transpose=True)
                    for dc in range(NDC)]
        s_t2 = pr.dma("sync", tp2, deps=[s_ab] + ([stores] if t == 0 else []), n=NDC)
        tmp_war[sl] = [s_t2]
        tp2_steps.append(s_t2)

    out_dmas = []
    wload_prev = stores
    evac_war = [None, None]
    bank_war = {k: prev_bc.get(k % 2, [None]) if k < 2 else [None] for k in range(8)}
    bank_war = {0: prev_bc[0], 1: prev_bc[1]}
    for k in range(2, 8):
        bank_war[k] = [None]
    for proj in range(3):
        def ld_w(e, proj=proj):
            src = wqkv[proj].rearrange("(dc p) cc -> p dc cc", p=P)
            outv = big2[:, 0:NDC * D].rearrange("p (dc cc) -> p dc cc", cc=D)
            return [e.dma_start(out=outv[:, 4 * q:4 * q + 4, :],
                                in_=src[:, 4 * q:4 * q + 4, :])
                    for q in range(4)]
        s_wl = pr.dma("gpsimd", ld_w, deps=[wload_prev], n=4)
        mm_all = []
        if proj < 2:
            dst = qf_out if proj == 0 else kf_out
            for hc in range(NH):
                for tg in range(4):
                    b = (hc * 4 + tg) % 2

                    def mm_qk(e, hc=hc, tg=tg, b=b):
                        for dc in range(NDC):
                            mi = e.matmul(bank[b],
                                          big2[:, dc * D + hc * P:dc * D + (hc + 1) * P],
                                          big1[:, dc * D + tg * 512:dc * D + (tg + 1) * 512],
                                          start=(dc == 0), stop=(dc == NDC - 1))
                        return mi
                    s_mm = pr.step("tensor", mm_qk,
                                   deps=[s_wl, bank_war[b]] + tp2_steps)
                    esl = (hc * 4 + tg) % 2
                    est = mid1[:, (4 + esl) * 512:(5 + esl) * 512]

                    def ev(e, hc=hc, b=b, est=est, proj=proj):
                        return e.activation(est, bank[b], AF.Identity,
                                            bias=bqk_sb[:, proj * NDC + hc:
                                                        proj * NDC + hc + 1])
                    s_ev = pr.step("scalar", ev, deps=[s_mm, s_bqk, evac_war[esl]])
                    bank_war[b] = [s_ev]

                    def so(e, hc=hc, tg=tg, est=est, dst=dst):
                        return [e.dma_start(out=dst[hc * P:(hc + 1) * P,
                                                    tg * 512:(tg + 1) * 512], in_=est)]
                    s_so = pr.dma("sync", so, deps=[s_ev], n=1)
                    evac_war[esl] = s_so
                    out_dmas.append(s_so)
                    mm_all.append(s_mm)
        else:
            for t in range(NT):
                bs = 4 * (t % 2)

                def mm_v(e, t=t, bs=bs):
                    for vc in range(4):
                        e.matmul(bank[bs + vc], ones_sb[0:1, 0:P],
                                 bvrow_sb[0:1, vc * 512:(vc + 1) * 512],
                                 start=True, stop=False)
                    for dc in range(NDC):
                        for vc in range(4):
                            mi = e.matmul(bank[bs + vc],
                                          big1[:, dc * D + t * P:dc * D + (t + 1) * P],
                                          big2[:, dc * D + vc * 512:dc * D + (vc + 1) * 512],
                                          start=False, stop=(dc == NDC - 1))
                    return mi
                s_mm = pr.step("tensor", mm_v,
                               deps=[s_wl, s_bvrow, s_ones] + tp2_steps +
                                    [bank_war[bs + i] for i in range(4)])
                for vc in range(4):
                    esl = (t * 4 + vc) % 2
                    est = mid1[:, (6 + esl) * 512:(7 + esl) * 512]

                    def ev(e, bs=bs, vc=vc, est=est):
                        return e.activation(est, bank[bs + vc], AF.Copy)
                    s_ev = pr.step("scalar", ev, deps=[s_mm, evac_war[esl]])
                    bank_war[bs + vc] = [s_ev]

                    def so(e, t=t, vc=vc, est=est):
                        return [e.dma_start(out=v_out[t * P:(t + 1) * P,
                                                      vc * 512:(vc + 1) * 512], in_=est)]
                    s_so = pr.dma("sync", so, deps=[s_ev], n=1)
                    evac_war[esl] = s_so
                    out_dmas.append(s_so)
                mm_all.append(s_mm)
        wload_prev = mm_all


    return out_dmas, stores



def build_launch_A():
    nc = bass.Bass("TRN2", target_bir_lowering=False)

    x_in = nc.dram_tensor("x_in", [T, D], F32, kind="ExternalInput").ap()
    W1r = nc.dram_tensor("W1r", [NE, NFC, NDC, P, P], F32, kind="ExternalInput").ap()
    W2b = nc.dram_tensor("W2b", [NE, 4, NFC, P, 512], F32, kind="ExternalInput").ap()
    Wg = nc.dram_tensor("Wg", [D, 8], F32, kind="ExternalInput").ap()
    ln_gb = nc.dram_tensor("ln_gb", [2, D], F32, kind="ExternalInput").ap()
    wqkv = nc.dram_tensor("wqkv", [3, D, D], F32, kind="ExternalInput").ap()
    bqk = nc.dram_tensor("bqk", [2, P, NDC], F32, kind="ExternalInput").ap()
    bv_row = nc.dram_tensor("bv_row", [1, D], F32, kind="ExternalInput").ap()
    ones_in = nc.dram_tensor("ones_in", [P, P], F32, kind="ExternalInput").ap()
    consts_in = nc.dram_tensor("consts_in", [P, 8], F32, kind="ExternalInput").ap()

    x_out = nc.dram_tensor("x_out", [T, D], F32, kind="ExternalOutput").ap()
    qf_out = nc.dram_tensor("qf_out", [D, T], BF16, kind="ExternalOutput").ap()
    kf_out = nc.dram_tensor("kf_out", [D, T], BF16, kind="ExternalOutput").ap()
    v_out = nc.dram_tensor("v_out", [T, D], BF16, kind="ExternalOutput").ap()

    xnf_d = nc.dram_tensor("xnf_d", [D, T], BF16)

    with (
        nc.sbuf_tensor("big1", [P, 32 * 1024], BF16) as big1,
        nc.sbuf_tensor("big2", [P, 2 * NFC * 512], BF16) as big2,
        nc.sbuf_tensor("mid1", [P, NDC * 1024], BF16) as mid1,
        nc.sbuf_tensor("w1sb", [P, 2 * D], BF16) as w1sb,
        nc.sbuf_tensor("xt_sb", [P, 2 * D], F32) as xt_sb,
        nc.sbuf_tensor("xn_sb", [P, 2 * D], BF16) as xn_sb,
        nc.sbuf_tensor("gb_bc", [P, 2 * D], BF16) as gb_bc,
        nc.sbuf_tensor("stat_sb", [P, 32], F32) as stat_sb,
        nc.sbuf_tensor("w_sb", [P, NT * 8], F32) as w_sb,
        nc.sbuf_tensor("p8_sb", [P, 8], F32) as p8_sb,
        nc.sbuf_tensor("p8b_sb", [P, 8], F32) as p8b_sb,
        nc.sbuf_tensor("wg_sb", [P, NDC * 8], BF16) as wg_sb,
        nc.sbuf_tensor("ones_sb", [P, P], BF16) as ones_sb,
        nc.sbuf_tensor("bqk_sb", [P, 2 * NDC], F32) as bqk_sb,
        nc.sbuf_tensor("consts_sb", [P, 8], F32) as consts_sb,
        nc.psum_tensor("ps", [P, 8 * 512], F32) as ps,
        nc.Block() as block,
    ):
        pr = Prog(nc)
        bank = [ps[:, k * 512:(k + 1) * 512] for k in range(8)]
        xsq_sb = mid1[:, 2 * D:3 * D]
        gbrow_sb = mid1[0:1, 3 * D:5 * D]
        bvrow_sb = mid1[0:1, 5 * D:6 * D]

        s_ones = pr.dma("gpsimd", lambda e: [e.dma_start(out=ones_sb[:, :], in_=ones_in)])
        s_csts = pr.dma("sync", lambda e: [e.dma_start(out=consts_sb[:, :], in_=consts_in)])
        globals()["_EPS_AP"] = consts_sb[:, 0:1]
        globals()["_EPS_DEP"] = s_csts
        s_wg = pr.dma("gpsimd", lambda e: [e.dma_start(
            out=wg_sb.rearrange("p (dc e8) -> p dc e8", e8=8),
            in_=Wg.rearrange("(dc p) e8 -> p dc e8", p=P))])
        s_bqk = pr.dma("sync", lambda e: [e.dma_start(
            out=bqk_sb.rearrange("p (r h) -> p r h", h=NDC),
            in_=bqk.rearrange("r p h -> p r h"))])
        s_gbrow = None
        s_bvrow = None

        out_dmas, stores = _rec_launchA_phases(pr, (
            x_in, W1r, W2b, x_out, qf_out, kf_out, v_out, xnf_d, wqkv,
            big1, big2, mid1, w1sb, xt_sb, xsq_sb, xn_sb, gb_bc, stat_sb, w_sb,
            p8_sb, p8b_sb, wg_sb, ones_sb, gbrow_sb, bvrow_sb, bqk_sb, bank,
            s_ones, s_wg, s_bqk, s_gbrow, s_bvrow), dict(ln_gb=ln_gb, bv_row=bv_row))

        pr.emit(block, final_deps=out_dmas + stores)
        pr.close()
    return nc




def rec_ln_router(pr, X, x_src, entry, xt_war_init, want_dram, want_sbuf,
                  wg_ap, wg_dep, wscale, psum_war):
    xt_sb, xn_sb, mid1, big1 = X["xt_sb"], X["xn_sb"], X["mid1"], X["big1"]
    stat_sb, xsq_sb, w_sb = X["stat_sb"], X["xsq_sb"], X["w_sb"]
    p8_sb, p8b_sb, bank = X["p8_sb"], X["p8b_sb"], X["bank"]
    xnf_d = X.get("xnf_d")
    xt_war = {0: list(xt_war_init), 1: list(xt_war_init)}
    xn_war = {0: [], 1: []}
    stg_war = {0: list(entry), 1: list(entry)}
    rt_free = [psum_war[0], psum_war[1]]
    stores = []
    tps = []
    war_sq = X.get("war_sq")
    s_w = None
    for t in range(NT):
        sl = t % 2
        xt = xt_sb[:, sl * D:(sl + 1) * D]
        s_lx = pr.dma("sync", (lambda e, t=t, xt=xt: [e.dma_start(
            out=xt, in_=x_src[t * P:(t + 1) * P, :])]), deps=xt_war[sl] + list(entry), n=1)
        xn = xn_sb[:, sl * D:(sl + 1) * D]
        s_ap, s_sq2 = rec_ln(pr, stat_sb, xsq_sb, xt, s_lx, xn,
                             out_war=xn_war[sl], war_sq=war_sq)
        war_sq = s_sq2
        xt_war[sl] = [s_ap]
        stg = mid1[:, sl * D:(sl + 1) * D]

        def tp_fn(e, xn=xn, stg=stg):
            return [e.dma_start(out=stg[:, dc * P:(dc + 1) * P],
                                in_=xn[:, dc * P:(dc + 1) * P], transpose=True)
                    for dc in range(NDC)]
        s_tp = pr.dma("sync", tp_fn, deps=[s_ap] + stg_war[sl], n=NDC)
        xn_war[sl] = [s_tp]
        consumers = []
        if want_dram:
            def st_fn(e, t=t, stg=stg):
                return [e.dma_start(
                    out=xnf_d.rearrange("(dc p) tt -> p dc tt", p=P)[:, :, t * P:(t + 1) * P],
                    in_=stg.rearrange("p (dc j) -> p dc j", j=P))]
            s_st = pr.dma("sync", st_fn, deps=[s_tp], n=1)
            stores.append(s_st)
            consumers.append(s_st)
        if want_sbuf:
            def sb_fn(e, t=t, stg=stg):
                return [e.dma_start(out=big1[:, dc * D + t * P:dc * D + (t + 1) * P],
                                    in_=stg[:, dc * P:(dc + 1) * P])
                        for dc in range(NDC)]
            s_sb = pr.dma("sync", sb_fn, deps=[s_tp] + list(entry), n=NDC)
            tps.append(s_sb)
            consumers.append(s_sb)

        rb = t % 2
        ps_r = bank[rb][:, 0:8]

        def rt_fn(e, stg=stg, ps_r=ps_r, wg_ap=wg_ap):
            for dc in range(NDC):
                mi = e.matmul(ps_r, stg[:, dc * P:(dc + 1) * P],
                              wg_ap[:, dc * 8:(dc + 1) * 8],
                              start=(dc == 0), stop=(dc == NDC - 1))
            return mi
        s_rt = pr.step("tensor", rt_fn, deps=[s_tp, wg_dep, rt_free[rb]])
        s_w, s_psfree = rec_top2(pr, stat_sb, p8_sb, p8b_sb, ps_r, s_rt,
                                 w_sb[:, t * 8:(t + 1) * 8], wscale=wscale)
        rt_free[rb] = s_psfree
        stg_war[sl] = consumers + [s_rt]
    return dict(stores=stores, tps=tps, s_w=s_w, xt_war=xt_war, war_sq=war_sq,
                stg_war=stg_war, rt_free=rt_free)


def _rec_attn_norm_bound(pr, X, kh, qh, nsc, s_kh, s_qh, last_exp, sGn_free,
                         nb_banks, ncol):
    stat_sb, ones_sb, bank = X["stat_sb"], X["ones_sb"], X["bank"]
    chunks = [(qh, cq, 26) for cq in range(4)] + [(kh, ck, 29) for ck in range(16)]
    first = {26: True, 29: True}
    s_mq = None
    s_mk = None
    red_hist = [None, None]
    sq_hist = [None, None]
    mm_hist = [None, None]
    run_hist = None
    for idx, (src_t, cq, col) in enumerate(chunks):
        par = idx % 2
        nb = nb_banks[par][0:1, 0:512]
        nscs = nsc[:, par * 512:(par + 1) * 512]
        s_sqc = pr.step("vector", (lambda e, src_t=src_t, cq=cq, nscs=nscs: e.tensor_tensor(
            nscs, src_t[:, cq * 512:(cq + 1) * 512],
            src_t[:, cq * 512:(cq + 1) * 512], ALU.mult)),
            deps=[s_qh, s_kh, mm_hist[par], sGn_free.get(("nsq", par))])
        sGn_free[("nsq", par)] = s_sqc
        s_nmc = pr.step("tensor", (lambda e, nb=nb, nscs=nscs: e.matmul(
            nb, ones_sb[:, 0:1], nscs, start=True, stop=True)),
            deps=[s_sqc, red_hist[par], sGn_free.get(("nbank", par))])
        mm_hist[par] = s_nmc
        if first[col]:
            s_red = pr.step("vector", (lambda e, col=col, nb=nb: e.reduce_max(
                stat_sb[0:1, col:col + 1], nb, AXL.X)), deps=[s_nmc, run_hist])
            first[col] = False
        else:
            s_t = pr.step("vector", (lambda e, nb=nb: e.reduce_max(
                stat_sb[0:1, 31:32], nb, AXL.X)), deps=[s_nmc, run_hist])
            s_red = pr.step("vector", (lambda e, col=col: e.tensor_tensor(
                stat_sb[0:1, col:col + 1], stat_sb[0:1, col:col + 1],
                stat_sb[0:1, 31:32], ALU.max)), deps=[s_t])
        red_hist[par] = s_red
        run_hist = s_red
        if col == 26:
            s_mq = s_red
        else:
            s_mk = s_red
    s_c3 = pr.step("vector", lambda e: e.tensor_tensor(
        stat_sb[0:1, 26:27], stat_sb[0:1, 26:27], stat_sb[0:1, 29:30], ALU.mult),
        deps=[s_mq, s_mk])
    s_c4 = pr.step("scalar", lambda e: e.activation(
        stat_sb[0:1, 26:27], stat_sb[0:1, 26:27], AF.Ln), deps=[s_c3, last_exp])
    s_c5 = pr.step("scalar", lambda e: e.activation(
        stat_sb[0:1, 26:27], stat_sb[0:1, 26:27], AF.Exp, scale=0.5), deps=[s_c4])
    s_c6 = pr.step("vector", lambda e: e.tensor_scalar(
        stat_sb[0:1, 26:27], stat_sb[0:1, 26:27], -ISQ_DH, None, ALU.mult),
        deps=[s_c5])
    s_c7 = pr.step("tensor", lambda e: e.matmul(
        nb_banks[0][:, 0:1], X["onesf"].ap()[0:1, 0:P], stat_sb[0:1, 26:27],
        start=True, stop=True), deps=[s_c6, mm_hist[0], sGn_free.get(("nbank", 0)),
                                      X["s_onesf"]])
    s_c8 = pr.step("vector", lambda e: e.tensor_copy(ncol, nb_banks[0][:, 0:1]),
                   deps=[s_c7] + ([last_exp] if last_exp else []))
    sGn_free[("nbank", 0)] = s_c8
    return s_c8


def _rec_attn_kt_loop(pr, X, h, n, nkt, kh, qh, pt, sbanks, bl, ba, ncol, s_c8,
                      s_kh, s_qh, s_vh, vh, sGn_free, mask_of, extra_l0=None):
    ones_sb, bank = X["ones_sb"], X["bank"]
    DEPTH = len(sbanks)
    eps_dep = X["s_csts"]
    smm_steps = [None] * nkt
    exp_steps = [None] * nkt
    amm_steps = [None] * nkt
    koffs = [(kt % 4) * 2048 + (kt // 4) * P for kt in range(nkt)]

    def emit_smm(kt):
        sb = sbanks[kt % DEPTH]
        deps = [s_kh, s_qh, s_c8]
        if kt >= DEPTH:
            deps.append(exp_steps[kt - DEPTH])
        else:
            deps.append(sGn_free.get(("s", kt % DEPTH)))
        if kt >= 1:
            deps.append(amm_steps[kt - 1])
        else:
            deps.append(sGn_free.get("pt"))

        def smm(e, koff=koffs[kt], sb=sb, n=n):
            return e.matmul(sb, kh[:, koff:koff + P],
                            qh[:, n * 512:(n + 1) * 512], start=True, stop=True)
        return pr.step("tensor", smm, deps=deps)

    for kt in range(min(2, nkt)):
        smm_steps[kt] = emit_smm(kt)
    last_exp = None
    s_l = None
    s_a = None
    for kt in range(nkt):
        sb = sbanks[kt % DEPTH]
        ptb = pt[kt % DEPTH]

        def expf(e, sb=sb, ptb=ptb):
            return e.activation(ptb, sb, AF.Exp, bias=ncol, scale=ISQ_DH)
        s_e = pr.step("scalar", expf, deps=[smm_steps[kt], s_c8, eps_dep,
                                            amm_steps[kt - DEPTH] if kt >= DEPTH
                                            else sGn_free.get("pt")])
        exp_steps[kt] = s_e
        last_exp = s_e
        dep_pt = s_e
        mk = mask_of(kt) if mask_of else None
        if mk is not None:
            slot, mskb, s_msk = mk

            def mskf(e, ptb=ptb, mskb=mskb, slot=slot):
                return e.tensor_tensor(ptb, ptb,
                                       mskb[:, slot * 512:(slot + 1) * 512], ALU.mult)
            dep_pt = pr.step("vector", mskf, deps=[s_e, s_msk])

        def lmm(e, ptb=ptb, bl=bl, kt=kt, nkt=nkt):
            return e.matmul(bl, ones_sb[:, 0:1], ptb,
                            start=(kt == 0), stop=(kt == nkt - 1))
        s_l = pr.step("tensor", lmm,
                      deps=[dep_pt] + ([sGn_free.get("l2"), extra_l0] if kt == 0 else []))

        def amm(e, ptb=ptb, ba=ba, koff=koffs[kt], kt=kt, nkt=nkt):
            return e.matmul(ba, vh[:, koff:koff + P], ptb,
                            start=(kt == 0), stop=(kt == nkt - 1))
        s_a = pr.step("tensor", amm,
                      deps=[dep_pt, s_vh] + ([sGn_free.get("a")] if kt == 0 else []))
        amm_steps[kt] = s_a
        if kt + 2 < nkt:
            smm_steps[kt + 2] = emit_smm(kt + 2)
    for j in range(DEPTH):
        sGn_free[("s", j)] = last_exp
    sGn_free["pt"] = s_a
    return s_l, s_a, last_exp


def rec_l0_attn(pr, X, C, x_src, x_dst, entry):
    (qf_in, gk, gv, msk_in, wo_in, bo_row) = C
    big1, big2, mid1 = X["big1"], X["big2"], X["mid1"]
    xt_sb, ones_sb, stat_sb, bank = X["xt_sb"], X["ones_sb"], X["stat_sb"], X["bank"]

    kh = mid1[:, 0:8192]
    qh = mid1[:, 8192:10240]
    DEPTH = 4
    pt = [mid1[:, 10240 + i * 512:10240 + (i + 1) * 512] for i in range(DEPTH)]
    nsc = mid1[:, 12288:12288 + 1024]
    rbx = mid1[:, 13312:13312 + 512]
    vh = big2[:, 0:8192]
    mskb = big2[:, 8192:16384]

    s_msk = pr.dma("gpsimd", lambda e: [e.dma_start(
        out=mskb.rearrange("p (s j) -> p s j", j=512),
        in_=msk_in.rearrange("s p j -> p s j"))], deps=list(entry), n=1)

    kh_war = list(entry)
    vh_war = [s_msk]
    qh_war = list(entry)
    ncol = stat_sb[:, 30:31]
    acc_done = []
    last_exp = None
    sGn_free = {}
    s_s = None
    s_a = None
    for h in range(NH):
        def ldk(e, h=h):
            return [e.dma_start(out=kh[:, r * 2048:(r + 1) * 2048],
                                in_=gk[r, h * P:(h + 1) * P, :]) for r in range(4)]
        s_kh = pr.dma("gpsimd", ldk, deps=kh_war, n=4)

        def ldv(e, h=h):
            return [e.dma_start(
                out=vh.rearrange("p (rm j) -> p rm j", j=P)[:, r * 16:(r + 1) * 16, :],
                in_=gv[r].rearrange("(m p) d -> p m d", p=P)[:, :, h * P:(h + 1) * P])
                for r in range(4)]
        s_vh = pr.dma("gpsimd", ldv, deps=vh_war, n=4)

        def ldq(e, h=h):
            return [e.dma_start(out=qh[:, :], in_=qf_in[h * P:(h + 1) * P, :])]
        s_qh = pr.dma("gpsimd", ldq, deps=qh_war, n=1)

        s_c8 = _rec_attn_norm_bound(pr, X, kh, qh, nsc, s_kh, s_qh, last_exp,
                                    sGn_free, [bank[5], bank[7]], ncol)

        for n in range(4):
            bl = bank[2][0:1, 0:512]
            ba = bank[3]
            nkt = 16 * (n + 1)

            def mask_of(kt, nkt=nkt, s_msk=s_msk):
                if kt >= nkt - 16:
                    return (kt - (nkt - 16), mskb, s_msk)
                return None
            s_l, s_a, last_exp = _rec_attn_kt_loop(
                pr, X, h, n, nkt, kh, qh, pt, [bank[0], bank[1], bank[5], bank[7]],
                bl, ba, ncol, s_c8, s_kh, s_qh, s_vh, vh, sGn_free, mask_of)
            s_r2 = pr.step("vector", (lambda e, bl=bl: e.reciprocal(
                X["lrow"][0:1, 0:512], bl)), deps=[s_l, sGn_free.get("lrow")])
            sGn_free["l2"] = s_r2
            s_b = pr.step("tensor", (lambda e: e.matmul(
                bank[4][:, 0:512], X["onesf"].ap()[0:1, 0:P],
                X["lrow"][0:1, 0:512], start=True, stop=True)),
                deps=[s_r2, sGn_free.get("b"), X["s_onesf"]])
            sGn_free["lrow"] = s_b
            s_bc = pr.step("vector", (lambda e: e.tensor_copy(rbx, bank[4][:, 0:512])),
                           deps=[s_b, sGn_free.get("rbx")])
            sGn_free["b"] = s_bc
            s_ev = pr.step("vector", (lambda e, h=h, n=n, ba=ba: e.tensor_tensor(
                big1[:, h * 2048 + n * 512:h * 2048 + (n + 1) * 512], ba, rbx, ALU.mult)),
                deps=[s_a, s_bc])
            sGn_free["a"] = s_ev
            sGn_free["rbx"] = s_ev
            acc_done.append(s_ev)
        sGn_free[("nbank", 0)] = last_exp
        sGn_free[("nbank", 1)] = last_exp
        kh_war = [s_s if s_s is not None else s_a, s_a, last_exp]
        vh_war = [s_a]
        qh_war = [s_a, last_exp]

    def ldwo(e):
        srcv = wo_in.rearrange("(hh p) c -> p hh c", p=P)
        outv = big2.rearrange("p (hh c) -> p hh c", c=D)
        return [e.dma_start(out=outv[:, 4 * q:4 * q + 4, :],
                            in_=srcv[:, 4 * q:4 * q + 4, :]) for q in range(4)]
    s_wo = pr.dma("gpsimd", ldwo, deps=[kh_war, vh_war, s_msk] + acc_done, n=4)
    s_bo = pr.dma("gpsimd", lambda e: [e.dma_start(out=X["borow"][0:1, :], in_=bo_row)],
                  deps=acc_done, n=1)

    xres = xt_sb[:, 0:1024]
    load_war = [list(entry), list(entry)]
    obank_war = {k: [acc_done[-1], last_exp] for k in range(8)}
    store_of = {}
    for tl in range(NT):
        for Dc in range(4):
            ob = (tl * 4 + Dc) % 8

            def omm(e, tl=tl, Dc=Dc, ob=ob):
                e.matmul(bank[ob], ones_sb[0:1, 0:P],
                         X["borow"][0:1, Dc * 512:(Dc + 1) * 512], start=True, stop=False)
                for hh in range(NH):
                    mi = e.matmul(bank[ob], big1[:, hh * 2048 + tl * P:hh * 2048 + (tl + 1) * P],
                                  big2[:, hh * D + Dc * 512:hh * D + (Dc + 1) * 512],
                                  start=False, stop=(hh == NH - 1))
                return mi
            s_omm = pr.step("tensor", omm, deps=[s_wo, s_bo, obank_war[ob]] + acc_done)
            xsl = Dc % 2
            xr = xres[:, xsl * 512:(xsl + 1) * 512]

            def lx(e, tl=tl, Dc=Dc, xr=xr):
                return [e.dma_start(out=xr, in_=x_src[tl * P:(tl + 1) * P,
                                                      Dc * 512:(Dc + 1) * 512])]
            s_lx = pr.dma("sync", lx, deps=load_war[xsl], n=1)
            s_ad = pr.step("vector", (lambda e, ob=ob, xr=xr: e.tensor_tensor(
                xr, bank[ob], xr, ALU.add)), deps=[s_omm, s_lx])
            obank_war[ob] = [s_ad]
            load_war[xsl] = [s_ad]

            def sx(e, tl=tl, Dc=Dc, xr=xr):
                return [e.dma_start(out=x_dst[tl * P:(tl + 1) * P,
                                              Dc * 512:(Dc + 1) * 512], in_=xr)]
            s_sx = pr.dma("sync", sx, deps=[s_ad], n=1)
            store_of[(tl, Dc)] = s_sx
    return list(store_of.values())



def rec_ln_feature(pr, X, x_src, entry, xt_war_init, war_sq):
    xt_sb, xn_sb, mid1, big1 = X["xt_sb"], X["xn_sb"], X["mid1"], X["big1"]
    stat_sb, xsq_sb = X["stat_sb"], X["xsq_sb"]
    xt_war = {0: list(xt_war_init), 1: list(xt_war_init)}
    xn_war = {0: [], 1: []}
    stg_war = {0: list(entry), 1: list(entry)}
    tps = []
    for t in range(NT):
        sl = t % 2
        xt = xt_sb[:, sl * D:(sl + 1) * D]
        s_lx = pr.dma("sync", (lambda e, t=t, xt=xt: [e.dma_start(
            out=xt, in_=x_src[t * P:(t + 1) * P, :])]), deps=xt_war[sl] + list(entry), n=1)
        xn = xn_sb[:, sl * D:(sl + 1) * D]
        s_ap, s_sq2 = rec_ln(pr, stat_sb, xsq_sb, xt, s_lx, xn,
                             out_war=xn_war[sl], war_sq=war_sq)
        war_sq = s_sq2
        xt_war[sl] = [s_ap]
        stg = mid1[:, sl * D:(sl + 1) * D]

        def tp_fn(e, xn=xn, stg=stg):
            return [e.dma_start(out=stg[:, dc * P:(dc + 1) * P],
                                in_=xn[:, dc * P:(dc + 1) * P], transpose=True)
                    for dc in range(NDC)]
        s_tp = pr.dma("sync", tp_fn, deps=[s_ap] + stg_war[sl], n=NDC)
        xn_war[sl] = [s_tp]

        def sb_fn(e, t=t, stg=stg):
            return [e.dma_start(out=big1[:, dc * D + t * P:dc * D + (t + 1) * P],
                                in_=stg[:, dc * P:(dc + 1) * P])
                    for dc in range(NDC)]
        s_sb = pr.dma("sync", sb_fn, deps=[s_tp] + list(entry), n=NDC)
        tps.append(s_sb)
        stg_war[sl] = [s_sb]
    return tps, xt_war, war_sq


def rec_ffn1(pr, X, C1, x_io, entry, xt_war_init, psum_war):
    (weightT, AsT, Bst, rwT_sb, rw_dep) = C1
    big1, big2, mid1 = X["big1"], X["big2"], X["mid1"]
    xt_sb, ones_sb, stat_sb, bank, w_sb = (X["xt_sb"], X["ones_sb"], X["stat_sb"],
                                           X["bank"], X["w_sb"])
    r = rec_ln_router(pr, X, x_io, entry, xt_war_init, False, True,
                      rwT_sb, rw_dep, 1.0, psum_war)
    tps, s_w = r["tps"], r["s_w"]
    ent2 = tps + [s_w] + r["stg_war"][0] + r["stg_war"][1] + r["rt_free"]

    def ldw(e):
        srcv = weightT.rearrange("(dc p) c -> p dc c", p=P)
        outv = big2.rearrange("p (dc c) -> p dc c", c=D)
        return [e.dma_start(out=outv[:, 4 * q:4 * q + 4, :],
                            in_=srcv[:, 4 * q:4 * q + 4, :]) for q in range(4)]
    s_ldw = pr.dma("gpsimd", ldw, deps=list(entry), n=4)
    ast_sb = mid1[:, 8192:10240]
    bst_sb = mid1[:, 10240:12288]
    axs = mid1[:, 12288:12800]
    t2f = mid1[:, 12800:13312]
    wts = X["lrow2"]
    s_lda = pr.dma("gpsimd", (lambda e: [e.dma_start(
        out=ast_sb.rearrange("p (dc er) -> p dc er", er=P),
        in_=AsT.rearrange("(dc p) er -> p dc er", p=P))]), deps=ent2, n=1)
    s_ldb = pr.dma("gpsimd", (lambda e: [e.dma_start(out=bst_sb, in_=Bst)]),
                   deps=ent2, n=1)

    load_war = [r["xt_war"][0], r["xt_war"][1]]
    store_of = {}
    ob_war = {k: list(psum_war) + [s_w] for k in range(8)}
    prev_ax = None
    prev_t2f = None
    prev_wt = None
    prev_wrep = None
    for tg in range(4):
        def axmm(e, tg=tg):
            for dc in range(NDC):
                mi = e.matmul(bank[4], ast_sb[:, dc * P:(dc + 1) * P],
                              big1[:, dc * D + tg * 512:dc * D + (tg + 1) * 512],
                              start=(dc == 0), stop=(dc == NDC - 1))
            return mi
        s_ax = pr.step("tensor", axmm, deps=[s_lda, prev_ax] + tps)
        s_axe = pr.step("scalar", (lambda e: e.activation(axs, bank[4], AF.Copy)),
                        deps=[s_ax, prev_t2f])
        prev_ax = s_axe
        def wtp(e, tg=tg):
            for i in range(4):
                tl = tg * 4 + i
                mi = e.transpose(bank[5][0:8, i * P:(i + 1) * P],
                                 w_sb[:, tl * 8:(tl + 1) * 8], X["identf"][:, :])
            return mi
        s_wt = pr.step("tensor", wtp, deps=[s_w, X["s_identf"], prev_wt])
        s_wte = pr.step("vector", (lambda e: e.tensor_copy(wts[0:8, :], bank[5][0:8, 0:512])),
                        deps=[s_wt, prev_wrep])
        prev_wt = s_wte

        def wrep(e):
            return e.matmul(bank[6], X["repl8"][0:8, :], wts[0:8, :],
                            start=True, stop=True)
        s_wr = pr.step("tensor", wrep, deps=[s_wte, X["s_repl8"], prev_wrep])

        s_t2 = pr.step("vector", (lambda e: e.tensor_tensor(t2f, axs, bank[6], ALU.mult)),
                       deps=[s_wr, s_axe, prev_t2f])
        prev_wrep = s_t2
        prev_t2f = s_t2

        for i in range(4):
            tl = tg * 4 + i
            for Dc in range(4):
                ob = Dc

                def omm(e, tl=tl, Dc=Dc, ob=ob, i=i):
                    for dc in range(NDC):
                        e.matmul(bank[ob], big1[:, dc * D + tl * P:dc * D + (tl + 1) * P],
                                 big2[:, dc * D + Dc * 512:dc * D + (Dc + 1) * 512],
                                 start=(dc == 0), stop=False)
                    return e.matmul(bank[ob], t2f[:, i * P:(i + 1) * P],
                                    bst_sb[:, Dc * 512:(Dc + 1) * 512],
                                    start=False, stop=True)
                s_omm = pr.step("tensor", omm,
                                deps=[s_ldw, s_ldb, s_t2, ob_war[ob]] + tps)
                xsl = Dc % 2
                xr = xt_sb[:, 0:1024][:, xsl * 512:(xsl + 1) * 512]

                def lx(e, tl=tl, Dc=Dc, xr=xr):
                    return [e.dma_start(out=xr, in_=x_io[tl * P:(tl + 1) * P,
                                                         Dc * 512:(Dc + 1) * 512])]
                s_lx = pr.dma("sync", lx, deps=load_war[xsl] +
                              [store_of.get((tl, Dc))], n=1)
                s_ad = pr.step("vector", (lambda e, ob=ob, xr=xr: e.scalar_tensor_tensor(
                    xr, bank[ob], 0.5, xr, ALU.mult, ALU.add)), deps=[s_omm, s_lx])
                ob_war[ob] = [s_ad]
                load_war[xsl] = [s_ad]

                def sx(e, tl=tl, Dc=Dc, xr=xr):
                    return [e.dma_start(out=x_io[tl * P:(tl + 1) * P,
                                                 Dc * 512:(Dc + 1) * 512], in_=xr)]
                s_sx = pr.dma("sync", sx, deps=[s_ad], n=1)
                store_of[(tl, Dc)] = s_sx
    return list(store_of.values()), r["war_sq"]


def rec_l1_qkv(pr, X, C, x_src, outs, entry, xt_war_init, war_sq):
    (w1qkvT, cos_f, sin_f, qraw_d, kraw_d) = C
    (qf_out, kf_out, v_out, mem_out) = outs
    big1, big2, mid1 = X["big1"], X["big2"], X["mid1"]
    xt_sb, ones_sb, bank = X["xt_sb"], X["ones_sb"], X["bank"]

    tps, xt_war, war_sq = rec_ln_feature(pr, X, x_src, entry, xt_war_init, war_sq)

    out_dmas = []
    wload_prev = list(entry)
    evac_war = [None, None]
    bank_war = {k: [None] for k in range(8)}
    raw_done = {}
    v_dmas = []
    for proj in range(3):
        def ld_w(e, proj=proj):
            srcv = w1qkvT[proj].rearrange("(dc p) c -> p dc c", p=P)
            outv = big2.rearrange("p (dc c) -> p dc c", c=D)
            return [e.dma_start(out=outv[:, 4 * q:4 * q + 4, :],
                                in_=srcv[:, 4 * q:4 * q + 4, :]) for q in range(4)]
        s_wl = pr.dma("gpsimd", ld_w, deps=[wload_prev], n=4)
        mm_all = []
        if proj == 2:
            for t in range(NT):
                bs = 4 * (t % 2)

                def mm_v(e, t=t, bs=bs):
                    for dc in range(NDC):
                        for vc in range(4):
                            mi = e.matmul(bank[bs + vc],
                                          big1[:, dc * D + t * P:dc * D + (t + 1) * P],
                                          big2[:, dc * D + vc * 512:dc * D + (vc + 1) * 512],
                                          start=(dc == 0), stop=(dc == NDC - 1))
                    return mi
                s_mm = pr.step("tensor", mm_v, deps=[s_wl] + tps +
                               [bank_war[bs + i] for i in range(4)])
                for vc in range(4):
                    esl = (t * 4 + vc) % 2
                    est = X["xn_sb"][:, esl * 512:(esl + 1) * 512]

                    def ev(e, bs=bs, vc=vc, est=est):
                        return e.activation(est, bank[bs + vc], AF.Copy)
                    s_ev = pr.step("scalar", ev, deps=[s_mm, evac_war[esl]])
                    bank_war[bs + vc] = [s_ev]

                    def so(e, t=t, vc=vc, est=est):
                        return [e.dma_start(out=v_out[t * P:(t + 1) * P,
                                                      vc * 512:(vc + 1) * 512], in_=est)]
                    s_so = pr.dma("sync", so, deps=[s_ev], n=1)
                    evac_war[esl] = s_so
                    out_dmas.append(s_so)
                    v_dmas.append(s_so)
                mm_all.append(s_mm)
        else:
            dst = qraw_d if proj == 0 else kraw_d
            for hc in range(NH):
                for tg in range(4):
                    b = (hc * 4 + tg) % 2

                    def mm_qk(e, hc=hc, tg=tg, b=b):
                        for dc in range(NDC):
                            mi = e.matmul(bank[b],
                                          big2[:, dc * D + hc * P:dc * D + (hc + 1) * P],
                                          big1[:, dc * D + tg * 512:dc * D + (tg + 1) * 512],
                                          start=(dc == 0), stop=(dc == NDC - 1))
                        return mi
                    s_mm = pr.step("tensor", mm_qk, deps=[s_wl, bank_war[b]] + tps)
                    esl = (hc * 4 + tg) % 2
                    est = X["xn_sb"][:, esl * 512:(esl + 1) * 512]

                    def ev(e, b=b, est=est):
                        return e.activation(est, bank[b], AF.Copy)
                    s_ev = pr.step("scalar", ev, deps=[s_mm, evac_war[esl]])
                    bank_war[b] = [s_ev]

                    def so(e, hc=hc, tg=tg, est=est, dst=dst):
                        return [e.dma_start(out=dst[hc * P:(hc + 1) * P,
                                                    tg * 512:(tg + 1) * 512], in_=est)]
                    s_so = pr.dma("sync", so, deps=[s_ev], n=1)
                    evac_war[esl] = s_so
                    out_dmas.append(s_so)
                    mm_all.append(s_mm)
            raw_done[proj] = list(out_dmas)
        wload_prev = mm_all

    rope_done = {}
    A2 = [mid1[:, i * 2048:(i + 1) * 2048] for i in range(2)]
    B2 = [mid1[:, 4096 + i * 2048:4096 + (i + 1) * 2048] for i in range(2)]
    CS = [mid1[:, 8192 + i * 2048:8192 + (i + 1) * 2048] for i in range(2)]
    SN = [mid1[:, 12288 + i * 2048:12288 + (i + 1) * 2048] for i in range(2)]
    slot_war = {0: list(xt_war[0]) + list(xt_war[1]), 1: list(xt_war[0]) + list(xt_war[1])}
    for proj in range(2):
        raw = qraw_d if proj == 0 else kraw_d
        dst = qf_out if proj == 0 else kf_out
        dones = []
        for hc in range(NH):
            sl = hc % 2
            dep0 = raw_done[proj] + slot_war[sl]

            def lda(e, raw=raw, hc=hc, sl=sl):
                return [e.dma_start(out=A2[sl], in_=raw[hc * P:(hc + 1) * P, :]),
                        e.dma_start(out=B2[sl], in_=raw[(hc ^ 8) * P:((hc ^ 8) + 1) * P, :]),
                        e.dma_start(out=CS[sl], in_=cos_f[hc * P:(hc + 1) * P, :]),
                        e.dma_start(out=SN[sl], in_=sin_f[hc * P:(hc + 1) * P, :])]
            s_l1 = pr.dma("sync", lda, deps=dep0, n=4)
            s_m1 = pr.step("vector", (lambda e, sl=sl: e.tensor_tensor(
                CS[sl], A2[sl], CS[sl], ALU.mult)), deps=[s_l1] + slot_war[sl])
            s_m2 = pr.step("vector", (lambda e, sl=sl: e.tensor_tensor(
                B2[sl], B2[sl], SN[sl], ALU.mult)), deps=[s_l1, s_m1])
            op = ALU.subtract if hc < 8 else ALU.add
            s_m3 = pr.step("vector", (lambda e, op=op, sl=sl: e.tensor_tensor(
                A2[sl], CS[sl], B2[sl], op)), deps=[s_m1, s_m2])

            def so(e, hc=hc, dst=dst, sl=sl):
                return [e.dma_start(out=dst[hc * P:(hc + 1) * P, :], in_=A2[sl])]
            s_so = pr.dma("sync", so, deps=[s_m3], n=1)
            slot_war[sl] = [s_so, s_m3]
            dones.append(s_so)
            out_dmas.append(s_so)
        rope_done[proj] = dones

    kf_done = []
    KT2 = [mid1[:, i * 2048:(i + 1) * 2048] for i in range(2)]
    R1 = [mid1[:, 4096 + i * 2048:4096 + (i + 1) * 2048] for i in range(2)]
    R3 = [mid1[:, 8192 + i * 2048:8192 + (i + 1) * 2048] for i in range(2)]
    kslot_war = {0: rope_done[1] + [wload_prev] + slot_war[0] + slot_war[1],
                 1: rope_done[1] + [wload_prev] + slot_war[0] + slot_war[1]}
    for t in range(NT):
        sl = t % 2

        def tpk(e, t=t, sl=sl):
            return [e.dma_start(out=KT2[sl][:, hc * P:(hc + 1) * P],
                                in_=kf_out[hc * P:(hc + 1) * P, t * P:(t + 1) * P],
                                transpose=True) for hc in range(NH)]
        s_tk = pr.dma("sync", tpk, deps=kslot_war[sl] + rope_done[1], n=NH)
        s_e1 = pr.step("scalar", (lambda e, sl=sl: e.activation(
            R1[sl], KT2[sl], AF.Relu, scale=-1.0)), deps=[s_tk] + kslot_war[sl])
        s_e2 = pr.step("scalar", (lambda e, sl=sl: e.activation(
            R1[sl], R1[sl], AF.Exp, scale=-1.0)), deps=[s_e1])
        s_e3 = pr.step("vector", (lambda e, sl=sl: e.tensor_relu(
            R3[sl], KT2[sl])), deps=[s_tk] + kslot_war[sl])
        s_e4 = pr.step("vector", (lambda e, t=t, sl=sl: e.tensor_tensor(
            big1[:, t * D:(t + 1) * D], R1[sl], R3[sl], ALU.add)),
            deps=[s_e2, s_e3] + (tps if t == 0 else []))
        kslot_war[sl] = [s_e4]
        kf_done.append(s_e4)

    vt_done = []
    prevv = [wload_prev]
    for t in range(NT):
        def ldvt(e, t=t):
            return [e.dma_start(out=big2[:, t * D:(t + 1) * D],
                                in_=v_out[t * P:(t + 1) * P, :])]
        s_lv = pr.dma("gpsimd", ldvt, deps=prevv + v_dmas, n=1)
        prevv = [s_lv]
        vt_done.append(s_lv)

    mem_dmas = []
    mb_war = {0: [None], 1: [None]}
    est_war = {0: [None], 1: [None]}
    for h in range(NH):
        mb = bank[h % 2]
        nbk = bank[2 + h % 2]

        def memm(e, h=h, mb=mb, nbk=nbk):
            for t in range(NT):
                e.matmul(mb[:, 0:128], big1[:, t * D + h * P:t * D + (h + 1) * P],
                         big2[:, t * D + h * P:t * D + (h + 1) * P],
                         start=(t == 0), stop=(t == NT - 1))
                mi = e.matmul(nbk[:, 0:1], big1[:, t * D + h * P:t * D + (h + 1) * P],
                              ones_sb[:, 0:1], start=(t == 0), stop=(t == NT - 1))
            return mi
        s_me = pr.step("tensor", memm, deps=kf_done + vt_done + mb_war[h % 2])
        est = mid1[:, 12288 + (h % 2) * 256:12288 + (h % 2) * 256 + 129]
        s_ev = pr.step("vector", (lambda e, mb=mb, est=est: e.tensor_copy(
            est[:, 0:128], mb[:, 0:128])), deps=[s_me, est_war[h % 2]])
        s_ev2 = pr.step("vector", (lambda e, nbk=nbk, est=est: e.tensor_copy(
            est[:, 128:129], nbk[:, 0:1])), deps=[s_me, s_ev])
        mb_war[h % 2] = [s_ev2]

        def so(e, h=h, est=est):
            return [e.dma_start(out=mem_out[h * P:(h + 1) * P, :], in_=est)]
        s_so = pr.dma("sync", so, deps=[s_ev2], n=1)
        est_war[h % 2] = s_so
        mem_dmas.append(s_so)
    return out_dmas + mem_dmas



def build_launch_B():
    nc = bass.Bass("TRN2", target_bir_lowering=False)

    x_in = nc.dram_tensor("x_in", [T, D], F32, kind="ExternalInput").ap()
    qf_in = nc.dram_tensor("qf_in", [D, T], BF16, kind="ExternalInput").ap()
    gk = nc.dram_tensor("gk", [4, D, T], BF16, kind="ExternalInput").ap()
    gv = nc.dram_tensor("gv", [4, T, D], BF16, kind="ExternalInput").ap()
    msk_in = nc.dram_tensor("msk_in", [16, P, 512], BF16, kind="ExternalInput").ap()
    wo_in = nc.dram_tensor("wo_in", [D, D], F32, kind="ExternalInput").ap()
    bo_row = nc.dram_tensor("bo_row", [1, D], F32, kind="ExternalInput").ap()
    W1r = nc.dram_tensor("W1r", [NE, NFC, NDC, P, P], F32, kind="ExternalInput").ap()
    W2b = nc.dram_tensor("W2b", [NE, 4, NFC, P, 512], F32, kind="ExternalInput").ap()
    Wg = nc.dram_tensor("Wg", [D, 8], F32, kind="ExternalInput").ap()
    weightT = nc.dram_tensor("weightT", [D, D], F32, kind="ExternalInput").ap()
    AsT = nc.dram_tensor("AsT", [D, P], F32, kind="ExternalInput").ap()
    Bst = nc.dram_tensor("Bst", [P, D], F32, kind="ExternalInput").ap()
    rwT = nc.dram_tensor("rwT", [D, 8], F32, kind="ExternalInput").ap()
    w1qkvT = nc.dram_tensor("w1qkvT", [3, D, D], F32, kind="ExternalInput").ap()
    cos_f = nc.dram_tensor("cos_f", [D, T], BF16, kind="ExternalInput").ap()
    sin_f = nc.dram_tensor("sin_f", [D, T], BF16, kind="ExternalInput").ap()
    repl8_in = nc.dram_tensor("repl8_in", [8, P], F32, kind="ExternalInput").ap()
    ident_in = nc.dram_tensor("ident_in", [P, P], F32, kind="ExternalInput").ap()
    ones_in = nc.dram_tensor("ones_in", [P, P], F32, kind="ExternalInput").ap()
    consts_in = nc.dram_tensor("consts_in", [P, 8], F32, kind="ExternalInput").ap()

    x_out = nc.dram_tensor("x_out", [T, D], F32, kind="ExternalOutput").ap()
    qf_out = nc.dram_tensor("qf_out", [D, T], BF16, kind="ExternalOutput").ap()
    kf_out = nc.dram_tensor("kf_out", [D, T], BF16, kind="ExternalOutput").ap()
    v_out = nc.dram_tensor("v_out", [T, D], BF16, kind="ExternalOutput").ap()
    mem_out = nc.dram_tensor("mem_out", [NH * P, 129], BF16, kind="ExternalOutput").ap()

    xnf_d = nc.dram_tensor("xnf_d", [D, T], BF16)
    qraw_d = nc.dram_tensor("qraw_d", [D, T], BF16)
    kraw_d = nc.dram_tensor("kraw_d", [D, T], BF16)

    with contextlib.ExitStack() as _st:
        def _sb(name, shape, dt):
            return _st.enter_context(nc.sbuf_tensor(name, shape, dt))
        big1 = _sb("big1", [P, 32 * 1024], BF16)
        big2 = _sb("big2", [P, 2 * NFC * 512], BF16)
        mid1 = _sb("mid1", [P, NDC * 1024], BF16)
        w1sb = _sb("w1sb", [P, 2 * D], BF16)
        xt_sb = _sb("xt_sb", [P, 2 * D], F32)
        xn_sb = _sb("xn_sb", [P, 2 * D], BF16)
        stat_sb = _sb("stat_sb", [P, 32], F32)
        w_sb = _sb("w_sb", [P, NT * 8], F32)
        p8_sb = _sb("p8_sb", [P, 8], F32)
        p8b_sb = _sb("p8b_sb", [P, 8], F32)
        wg_sb = _sb("wg_sb", [P, NDC * 8], BF16)
        wg2_sb = _sb("wg2_sb", [P, NDC * 8], BF16)
        ones_sb = _sb("ones_sb", [P, P], BF16)
        consts_sb = _sb("consts_sb", [P, 8], F32)
        lrow_sb = _sb("lrow_sb", [1, 512], F32)
        lrow2_sb = _sb("lrow2_sb", [8, 512], BF16)
        borow_sb = _sb("borow_sb", [1, D], BF16)
        repl8_sb = _sb("repl8_sb", [8, P], BF16)
        identf_sb = _sb("identf_sb", [P, P], F32)
        onesf_sb = _sb("onesf_sb", [P, P], F32)
        ps = _st.enter_context(nc.psum_tensor("ps", [P, 8 * 512], F32))
        block = _st.enter_context(nc.Block())
        pr = Prog(nc)
        bank = [ps[:, k * 512:(k + 1) * 512] for k in range(8)]
        xsq_sb = mid1[:, 2 * D:3 * D]

        s_ones = pr.dma("gpsimd", lambda e: [e.dma_start(out=ones_sb[:, :], in_=ones_in)])
        s_csts = pr.dma("sync", lambda e: [e.dma_start(out=consts_sb[:, :], in_=consts_in)])
        globals()["_EPS_AP"] = consts_sb[:, 0:1]
        globals()["_EPS_DEP"] = s_csts
        s_wg = pr.dma("gpsimd", lambda e: [e.dma_start(
            out=wg_sb.rearrange("p (dc e8) -> p dc e8", e8=8),
            in_=Wg.rearrange("(dc p) e8 -> p dc e8", p=P))])
        s_wg2 = pr.dma("gpsimd", lambda e: [e.dma_start(
            out=wg2_sb.rearrange("p (dc e8) -> p dc e8", e8=8),
            in_=rwT.rearrange("(dc p) e8 -> p dc e8", p=P))])
        s_repl = pr.dma("gpsimd", lambda e: [e.dma_start(out=repl8_sb[:, :], in_=repl8_in)])
        s_idf = pr.dma("sync", lambda e: [e.dma_start(out=identf_sb[:, :], in_=ident_in)])
        s_onf = pr.dma("sync", lambda e: [e.dma_start(out=onesf_sb[:, :], in_=ones_in)])

        X = dict(big1=big1, big2=big2, mid1=mid1, w1sb=w1sb, xt_sb=xt_sb,
                 xn_sb=xn_sb, stat_sb=stat_sb, w_sb=w_sb, p8_sb=p8_sb,
                 p8b_sb=p8b_sb, ones_sb=ones_sb, bank=bank, xsq_sb=xsq_sb,
                 s_csts=s_csts, lrow=lrow_sb, lrow2=lrow2_sb, borow=borow_sb,
                 repl8=repl8_sb, identf=identf_sb, s_repl8=s_repl, s_identf=s_idf,
                 onesf=onesf_sb, s_onesf=s_onf, xnf_d=xnf_d)

        attn_stores = rec_l0_attn(pr, X, (qf_in, gk, gv, msk_in, wo_in, bo_row),
                                  x_in, x_out, [s_csts, s_ones])

        r = rec_ln_router(pr, X, x_out, attn_stores, attn_stores, True, False,
                          wg_sb, s_wg, 0.5, [attn_stores, attn_stores])
        bufs = dict(xnfh=mid1, hfs=big1, w1sb=w1sb, w2b=big2,
                    xres=xt_sb[:, 0:1024], bank=bank)

        def w_getcol(tloc, e_i):
            if e_i == 8:
                return 0.5
            return w_sb[:, tloc * 8 + e_i:tloc * 8 + e_i + 1]

        dep_ready = r["stores"] + [r["s_w"]] + r["stg_war"][0] + r["stg_war"][1] + \
            r["rt_free"] + attn_stores
        ffn0_stores = rec_ffn0(pr, bufs, xnf_d, w_getcol, x_out, x_out, W1r, W2b,
                               dep_ready, r["xt_war"][0] + r["xt_war"][1] + [r["war_sq"]])
        X["war_sq"] = None

        ffn1_stores, wsq = rec_ffn1(pr, X, (weightT, AsT, Bst, wg2_sb, s_wg2),
                                    x_out, ffn0_stores, ffn0_stores,
                                    [ffn0_stores, ffn0_stores])

        qkv_dmas = rec_l1_qkv(pr, X, (w1qkvT, cos_f, sin_f, qraw_d, kraw_d),
                              x_out, (qf_out, kf_out, v_out, mem_out),
                              ffn1_stores, [ffn1_stores], wsq)

        pr.emit(block, final_deps=qkv_dmas + ffn1_stores + attn_stores + ffn0_stores)
        pr.close()
    return nc


def make_masks(g):
    m = np.zeros((16, P, 512), np.float32)
    for s_ in range(16):
        for jj in range(4):
            for kr in range(P):
                lo = s_ * P + kr - (4 * jj + g) * P
                lo = max(lo, 0)
                if lo < P:
                    m[s_, kr, jj * P + lo:(jj + 1) * P] = 1.0
    return m


def rope_tables(core):
    b, g = core // 4, core % 4
    pos = (np.arange(NT)[:, None] * 4 + g) * P + np.arange(P)[None, :]
    pos = pos.reshape(-1).astype(np.float64)
    inv = 1.0 / (10000.0 ** (np.arange(0, D, 2, dtype=np.float64) / D))
    ang = inv[:, None] * pos[None, :]
    cos = np.cos(ang)
    sin = np.sin(ang)
    cos_f = np.concatenate([cos, cos], 0).astype(np.float32).astype(BF)
    sin_f = np.concatenate([sin, sin], 0).astype(np.float32).astype(BF)
    return np.ascontiguousarray(cos_f), np.ascontiguousarray(sin_f)


def prep_launch_B_inputs(inputs, x1_shards, qf, kf, v):
    W1 = np.concatenate([inputs["L0_ffn_W1"], inputs["L0_ffn_Ws1"][None]], axis=0)
    W1r = np.ascontiguousarray(W1.reshape(NE, NDC, P, NFC, P).transpose(0, 3, 1, 2, 4))
    W2 = np.concatenate([inputs["L0_ffn_W2"], inputs["L0_ffn_Ws2"][None]], axis=0)
    W2bh = np.ascontiguousarray(W2.reshape(NE, NFC, P, 4, 512).transpose(0, 3, 1, 2, 4))
    A = inputs["L1_ffn_lora_A"]
    AsT = np.ascontiguousarray(A.transpose(2, 0, 1).reshape(D, P))
    Bm = inputs["L1_ffn_lora_B"]
    Bst = np.ascontiguousarray(Bm.transpose(0, 2, 1).reshape(P, D))
    common = dict(
        W1r=W1r, W2b=W2bh, Wg=np.ascontiguousarray(inputs["L0_ffn_Wg"]),
        wo_in=np.ascontiguousarray(inputs["L0_attn_Wo"]),
        bo_row=np.ascontiguousarray(inputs["L0_attn_bo"][None]),
        weightT=np.ascontiguousarray(inputs["L1_ffn_weight"].T),
        AsT=AsT, Bst=Bst,
        rwT=np.ascontiguousarray(inputs["L1_ffn_router_w"].T),
        w1qkvT=np.ascontiguousarray(np.stack([inputs["L1_attn_Wq"].T,
                                              inputs["L1_attn_Wk"].T,
                                              inputs["L1_attn_Wv"].T])),
        repl8_in=np.ascontiguousarray(
            np.kron(np.eye(8, dtype=np.float32), np.ones((1, 16), np.float32)) * 2.0),
        ident_in=np.eye(P, dtype=np.float32),
        ones_in=np.ones((P, P), np.float32),
        consts_in=np.tile(np.array([[EPS, 1.0, 0, 0, 0, 0, 0, 0]], np.float32), (P, 1)),
    )
    maps = []
    for c in range(8):
        grp = (c // 4) * 4
        cosf, sinf = rope_tables(c)
        maps.append(dict(
            common, x_in=x1_shards[c], qf_in=qf[c],
            gk=np.ascontiguousarray(np.stack([kf[grp + r] for r in range(4)])),
            gv=np.ascontiguousarray(np.stack([v[grp + r] for r in range(4)])),
            msk_in=make_masks(c % 4).astype(BF),
            cos_f=cosf, sin_f=sinf,
        ))
    return maps



def rec_l1_attn(pr, X, C, x_src, x_dst, entry):
    (qf_in, gk, gv, mem_parts, gate_in, woT_in) = C
    big1, big2, mid1 = X["big1"], X["big2"], X["mid1"]
    xt_sb, ones_sb, stat_sb, bank = X["xt_sb"], X["ones_sb"], X["stat_sb"], X["bank"]
    memf, memb, gsb = X["memf"], X["memb"], X["gsb"]

    kh = mid1[:, 0:8192]
    qh = mid1[:, 8192:10240]
    DEPTH = 3
    pt = [mid1[:, 10240 + i * 512:10240 + (i + 1) * 512] for i in range(DEPTH)]
    nsc = mid1[:, 11776:11776 + 1024]
    rbx = mid1[:, 12800:12800 + 512]
    rbx2 = mid1[:, 13312:13312 + 512]
    qfe = mid1[:, 14336:14336 + 2048]
    cmbt = mid1[:, 13824:13824 + 512]
    vh = big2[:, 0:8192]

    s_g0 = pr.dma("sync", lambda e: [e.dma_start(out=gsb[0:1, 0:16], in_=gate_in)],
                  deps=list(entry), n=1)
    s_g1 = pr.step("scalar", lambda e: e.activation(gsb[0:1, 16:32], gsb[0:1, 0:16],
                                                    AF.Sigmoid), deps=[s_g0])
    s_g2 = pr.step("vector", lambda e: e.tensor_scalar(gsb[0:1, 32:48], gsb[0:1, 16:32],
                                                       -1.0, 1.0, ALU.mult, ALU.add),
                   deps=[s_g1])

    prevm = list(entry)
    s_sum = None
    tmpm = big2[:, 8192:8192 + 2064]
    for r in range(4):
        def ldm(e, r=r, tmpm=tmpm):
            return [e.dma_start(out=tmpm.rearrange("p (hh c) -> p hh c", c=129),
                                in_=mem_parts[r].rearrange("(hh p) c -> p hh c", p=P))]
        s_lm = pr.dma("sync", ldm, deps=prevm, n=1)
        if r == 0:
            s_sum = pr.step("vector", (lambda e, tmpm=tmpm: e.tensor_copy(
                memf[:, 0:2064], tmpm)), deps=[s_lm, s_sum])
        else:
            s_sum = pr.step("vector", (lambda e, tmpm=tmpm: e.tensor_tensor(
                memf[:, 0:2064], memf[:, 0:2064], tmpm, ALU.add)), deps=[s_lm, s_sum])
        prevm = [s_sum]
    s_memb = pr.step("vector", lambda e: e.tensor_copy(memb[:, 0:2064], memf[:, 0:2064]),
                     deps=[s_sum])

    accs_done = []
    kh_war = list(entry)
    vh_war = [s_sum] + list(entry)
    qh_war = list(entry)
    ncol = stat_sb[:, 30:31]
    last_exp = None
    sGn_free = {}
    s_a = None
    for h in range(NH):
        def ldk(e, h=h):
            return [e.dma_start(out=kh[:, r * 2048:(r + 1) * 2048],
                                in_=gk[r, h * P:(h + 1) * P, :]) for r in range(4)]
        s_kh = pr.dma("gpsimd", ldk, deps=kh_war, n=4)

        def ldv(e, h=h):
            return [e.dma_start(
                out=vh.rearrange("p (rm j) -> p rm j", j=P)[:, r * 16:(r + 1) * 16, :],
                in_=gv[r].rearrange("(m p) d -> p m d", p=P)[:, :, h * P:(h + 1) * P])
                for r in range(4)]
        s_vh = pr.dma("gpsimd", ldv, deps=vh_war, n=4)

        def ldq(e, h=h):
            return [e.dma_start(out=qh[:, :], in_=qf_in[h * P:(h + 1) * P, :])]
        s_qh = pr.dma("gpsimd", ldq, deps=qh_war, n=1)

        s_c8 = _rec_attn_norm_bound(pr, X, kh, qh, nsc, s_kh, s_qh, last_exp,
                                    sGn_free, [bank[0], bank[1]], ncol)

        s_f1 = pr.step("scalar", (lambda e: e.activation(nsc[:, 0:1024], qh[:, 0:1024],
                                                         AF.Relu, scale=-1.0)),
                       deps=[s_qh, s_c8] + ([last_exp] if last_exp else []))
        s_f1b = pr.step("scalar", (lambda e: e.activation(nsc[:, 1024:2048] if False else
                                                          mid1[:, 11776 + 1024:11776 + 2048],
                                                          qh[:, 1024:2048],
                                                          AF.Relu, scale=-1.0)),
                        deps=[s_qh, s_c8, s_f1])
        s_f2 = pr.step("scalar", (lambda e: e.activation(qfe[:, 0:1024], nsc[:, 0:1024],
                                                         AF.Exp, scale=-1.0)),
                       deps=[s_f1, sGn_free.get("qfe")])
        s_f2b = pr.step("scalar", (lambda e: e.activation(qfe[:, 1024:2048],
                                                          mid1[:, 11776 + 1024:11776 + 2048],
                                                          AF.Exp, scale=-1.0)),
                        deps=[s_f1b, s_f2])
        s_f3 = pr.step("vector", lambda e: e.tensor_relu(nsc[:, 0:1024], qh[:, 0:1024]),
                       deps=[s_f2, s_qh])
        s_f3b = pr.step("vector", (lambda e: e.tensor_relu(
            mid1[:, 11776 + 1024:11776 + 2048], qh[:, 1024:2048])), deps=[s_f2b, s_f3])
        s_f4 = pr.step("vector", lambda e: e.tensor_tensor(qfe[:, 0:1024], qfe[:, 0:1024],
                                                           nsc[:, 0:1024], ALU.add),
                       deps=[s_f3])
        s_f4b = pr.step("vector", (lambda e: e.tensor_tensor(
            qfe[:, 1024:2048], qfe[:, 1024:2048],
            mid1[:, 11776 + 1024:11776 + 2048], ALU.add)), deps=[s_f3b, s_f4])
        s_qfe = s_f4b

        for n in range(4):
            bl = bank[2][0:1, 0:512]
            bd = bank[7][0:1, 0:512]
            ba = bank[3]
            bm = bank[5]
            s_mo = pr.step("tensor", (lambda e, h=h, n=n, bm=bm: e.matmul(
                bm, memb[:, h * 129:h * 129 + 128], qfe[:, n * 512:(n + 1) * 512],
                start=True, stop=True)),
                deps=[s_memb, s_qfe, sGn_free.get("mo")])
            s_de = pr.step("tensor", (lambda e, h=h, n=n, bd=bd: e.matmul(
                bd, memb[:, h * 129 + 128:h * 129 + 129], qfe[:, n * 512:(n + 1) * 512],
                start=True, stop=True)),
                deps=[s_memb, s_qfe, sGn_free.get("l2")])
            s_l, s_a, last_exp = _rec_attn_kt_loop(
                pr, X, h, n, NKT, kh, qh, pt, [bank[0], bank[1], bank[4]],
                bl, ba, ncol, s_c8, s_kh, s_qh, s_vh, vh, sGn_free, None,
                extra_l0=s_de)
            s_r1 = pr.step("vector", (lambda e, bl=bl: e.reciprocal(
                X["lrow"][0:1, 0:512], bl)), deps=[s_l, sGn_free.get("lrow")])
            s_r2 = pr.step("vector", (lambda e, bd=bd: e.reciprocal(
                X["lrow"][0:1, 512:1024], bd)), deps=[s_de, s_r1, sGn_free.get("lrow")])
            sGn_free["l2"] = s_r2
            s_r3 = pr.step("vector", (lambda e, h=h: e.tensor_scalar(
                X["lrow"][0:1, 0:512], X["lrow"][0:1, 0:512],
                gsb[0:1, 32 + h:33 + h], None, ALU.mult)), deps=[s_r1, s_g2])
            s_r4 = pr.step("vector", (lambda e, h=h: e.tensor_scalar(
                X["lrow"][0:1, 512:1024], X["lrow"][0:1, 512:1024],
                gsb[0:1, 16 + h:17 + h], None, ALU.mult)), deps=[s_r2, s_g1])
            s_b1 = pr.step("tensor", (lambda e: e.matmul(
                bank[6][:, 0:512], X["onesf"].ap()[0:1, 0:P],
                X["lrow"][0:1, 0:512], start=True, stop=True)),
                deps=[s_r3, sGn_free.get("b"), X["s_onesf"]])
            s_bc1 = pr.step("vector", (lambda e: e.tensor_copy(rbx, bank[6][:, 0:512])),
                            deps=[s_b1, sGn_free.get("rbx")])
            s_b2 = pr.step("tensor", (lambda e: e.matmul(
                bank[6][:, 0:512], X["onesf"].ap()[0:1, 0:P],
                X["lrow"][0:1, 512:1024], start=True, stop=True)),
                deps=[s_r4, s_bc1])
            sGn_free["lrow"] = s_b2
            s_bc2 = pr.step("vector", (lambda e: e.tensor_copy(rbx2, bank[6][:, 0:512])),
                            deps=[s_b2, sGn_free.get("rbx2")])
            sGn_free["b"] = s_bc2
            s_cv1 = pr.step("vector", (lambda e, h=h, n=n, ba=ba: e.tensor_tensor(
                big1[:, h * 2048 + n * 512:h * 2048 + (n + 1) * 512], ba, rbx, ALU.mult)),
                deps=[s_a, s_bc1])
            sGn_free["a"] = s_cv1
            sGn_free["rbx"] = s_cv1
            s_cv2 = pr.step("vector", (lambda e, bm=bm: e.tensor_tensor(
                cmbt, bm, rbx2, ALU.mult)), deps=[s_mo, s_bc2, sGn_free.get("cmbt")])
            sGn_free["mo"] = s_cv2
            sGn_free["rbx2"] = s_cv2
            s_cv3 = pr.step("vector", (lambda e, h=h, n=n: e.tensor_tensor(
                big1[:, h * 2048 + n * 512:h * 2048 + (n + 1) * 512],
                big1[:, h * 2048 + n * 512:h * 2048 + (n + 1) * 512], cmbt, ALU.add)),
                deps=[s_cv1, s_cv2])
            sGn_free["cmbt"] = s_cv3
            sGn_free["qfe"] = s_mo
            accs_done.append(s_cv3)
        sGn_free[("nbank", 0)] = last_exp
        sGn_free[("nbank", 1)] = last_exp
        sGn_free[("nsq", 0)] = None
        sGn_free[("nsq", 1)] = None
        kh_war = [s_a, last_exp]
        vh_war = [s_a]
        qh_war = [s_a, last_exp, s_qfe]

    def ldwo(e):
        srcv = woT_in.rearrange("(hh p) c -> p hh c", p=P)
        outv = big2.rearrange("p (hh c) -> p hh c", c=D)
        return [e.dma_start(out=outv[:, 4 * q:4 * q + 4, :],
                            in_=srcv[:, 4 * q:4 * q + 4, :]) for q in range(4)]
    s_wo = pr.dma("gpsimd", ldwo, deps=[kh_war, vh_war] + accs_done, n=4)

    xres = xt_sb[:, 0:1024]
    load_war = [list(entry), list(entry)]
    obank_war = {k: [accs_done[-1], last_exp] for k in range(8)}
    store_of = {}
    for tl in range(NT):
        for Dc in range(4):
            ob = (tl * 4 + Dc) % 8

            def omm(e, tl=tl, Dc=Dc, ob=ob):
                for hh in range(NH):
                    mi = e.matmul(bank[ob],
                                  big1[:, hh * 2048 + tl * P:hh * 2048 + (tl + 1) * P],
                                  big2[:, hh * D + Dc * 512:hh * D + (Dc + 1) * 512],
                                  start=(hh == 0), stop=(hh == NH - 1))
                return mi
            s_omm = pr.step("tensor", omm, deps=[s_wo, obank_war[ob]] + accs_done)
            xsl = Dc % 2
            xr = xres[:, xsl * 512:(xsl + 1) * 512]

            def lx(e, tl=tl, Dc=Dc, xr=xr):
                return [e.dma_start(out=xr, in_=x_src[tl * P:(tl + 1) * P,
                                                      Dc * 512:(Dc + 1) * 512])]
            s_lx = pr.dma("sync", lx, deps=load_war[xsl], n=1)
            s_ad = pr.step("vector", (lambda e, ob=ob, xr=xr: e.tensor_tensor(
                xr, bank[ob], xr, ALU.add)), deps=[s_omm, s_lx])
            obank_war[ob] = [s_ad]
            load_war[xsl] = [s_ad]

            def sx(e, tl=tl, Dc=Dc, xr=xr):
                return [e.dma_start(out=x_dst[tl * P:(tl + 1) * P,
                                              Dc * 512:(Dc + 1) * 512], in_=xr)]
            s_sx = pr.dma("sync", sx, deps=[s_ad], n=1)
            store_of[(tl, Dc)] = s_sx
    return list(store_of.values())



def build_launch_C():
    nc = bass.Bass("TRN2", target_bir_lowering=False)

    x_in = nc.dram_tensor("x_in", [T, D], F32, kind="ExternalInput").ap()
    qf_in = nc.dram_tensor("qf_in", [D, T], BF16, kind="ExternalInput").ap()
    gk = nc.dram_tensor("gk", [4, D, T], BF16, kind="ExternalInput").ap()
    gv = nc.dram_tensor("gv", [4, T, D], BF16, kind="ExternalInput").ap()
    mem_parts = nc.dram_tensor("mem_parts", [4, NH * P, 129], BF16,
                               kind="ExternalInput").ap()
    gate_in = nc.dram_tensor("gate_in", [1, 16], F32, kind="ExternalInput").ap()
    woT_in = nc.dram_tensor("woT_in", [D, D], F32, kind="ExternalInput").ap()
    weightT = nc.dram_tensor("weightT", [D, D], F32, kind="ExternalInput").ap()
    AsT = nc.dram_tensor("AsT", [D, P], F32, kind="ExternalInput").ap()
    Bst = nc.dram_tensor("Bst", [P, D], F32, kind="ExternalInput").ap()
    rwT = nc.dram_tensor("rwT", [D, 8], F32, kind="ExternalInput").ap()
    repl8_in = nc.dram_tensor("repl8_in", [8, P], F32, kind="ExternalInput").ap()
    ident_in = nc.dram_tensor("ident_in", [P, P], F32, kind="ExternalInput").ap()
    ones_in = nc.dram_tensor("ones_in", [P, P], F32, kind="ExternalInput").ap()
    consts_in = nc.dram_tensor("consts_in", [P, 8], F32, kind="ExternalInput").ap()

    y_out = nc.dram_tensor("y_out", [T, D], F32, kind="ExternalOutput").ap()

    with contextlib.ExitStack() as _st:
        def _sb(name, shape, dt):
            return _st.enter_context(nc.sbuf_tensor(name, shape, dt))
        big1 = _sb("big1", [P, 32 * 1024], BF16)
        big2 = _sb("big2", [P, 2 * NFC * 512], BF16)
        mid1 = _sb("mid1", [P, NDC * 1024], BF16)
        xt_sb = _sb("xt_sb", [P, 2 * D], F32)
        xn_sb = _sb("xn_sb", [P, 2 * D], BF16)
        stat_sb = _sb("stat_sb", [P, 32], F32)
        w_sb = _sb("w_sb", [P, NT * 8], F32)
        p8_sb = _sb("p8_sb", [P, 8], F32)
        p8b_sb = _sb("p8b_sb", [P, 8], F32)
        wg2_sb = _sb("wg2_sb", [P, NDC * 8], BF16)
        ones_sb = _sb("ones_sb", [P, P], BF16)
        consts_sb = _sb("consts_sb", [P, 8], F32)
        lrow_sb = _sb("lrow_sb", [1, 1024], F32)
        lrow2_sb = _sb("lrow2_sb", [8, 512], BF16)
        repl8_sb = _sb("repl8_sb", [8, P], BF16)
        identf_sb = _sb("identf_sb", [P, P], F32)
        onesf_sb = _sb("onesf_sb", [P, P], F32)
        memf_sb = _sb("memf_sb", [P, 2064], F32)
        memb_sb = _sb("memb_sb", [P, 2064], BF16)
        gsb = _sb("gsb", [1, 64], F32)
        ps = _st.enter_context(nc.psum_tensor("ps", [P, 8 * 512], F32))
        block = _st.enter_context(nc.Block())

        pr = Prog(nc)
        bank = [ps[:, k * 512:(k + 1) * 512] for k in range(8)]
        xsq_sb = mid1[:, 2 * D:3 * D]

        s_ones = pr.dma("gpsimd", lambda e: [e.dma_start(out=ones_sb[:, :], in_=ones_in)])
        s_csts = pr.dma("sync", lambda e: [e.dma_start(out=consts_sb[:, :], in_=consts_in)])
        globals()["_EPS_AP"] = consts_sb[:, 0:1]
        globals()["_EPS_DEP"] = s_csts
        s_wg2 = pr.dma("gpsimd", lambda e: [e.dma_start(
            out=wg2_sb.rearrange("p (dc e8) -> p dc e8", e8=8),
            in_=rwT.rearrange("(dc p) e8 -> p dc e8", p=P))])
        s_repl = pr.dma("gpsimd", lambda e: [e.dma_start(out=repl8_sb[:, :], in_=repl8_in)])
        s_idf = pr.dma("sync", lambda e: [e.dma_start(out=identf_sb[:, :], in_=ident_in)])
        s_onf = pr.dma("sync", lambda e: [e.dma_start(out=onesf_sb[:, :], in_=ones_in)])

        X = dict(big1=big1, big2=big2, mid1=mid1, xt_sb=xt_sb,
                 xn_sb=xn_sb, stat_sb=stat_sb, w_sb=w_sb, p8_sb=p8_sb,
                 p8b_sb=p8b_sb, ones_sb=ones_sb, bank=bank, xsq_sb=xsq_sb,
                 s_csts=s_csts, lrow=lrow_sb, lrow2=lrow2_sb,
                 repl8=repl8_sb, identf=identf_sb, s_repl8=s_repl, s_identf=s_idf,
                 onesf=onesf_sb, s_onesf=s_onf, memf=memf_sb, memb=memb_sb, gsb=gsb)

        attn_stores = rec_l1_attn(pr, X, (qf_in, gk, gv, mem_parts, gate_in, woT_in),
                                  x_in, y_out, [s_csts, s_ones])

        ffn1_stores, _ = rec_ffn1(pr, X, (weightT, AsT, Bst, wg2_sb, s_wg2),
                                  y_out, attn_stores, attn_stores,
                                  [attn_stores, attn_stores])

        pr.emit(block, final_deps=ffn1_stores + attn_stores)
        pr.close()
    return nc


def prep_launch_C_inputs(inputs, x4_shards, qf, kf, v, mems):
    A = inputs["L1_ffn_lora_A"]
    AsT = np.ascontiguousarray(A.transpose(2, 0, 1).reshape(D, P))
    Bm = inputs["L1_ffn_lora_B"]
    Bst = np.ascontiguousarray(Bm.transpose(0, 2, 1).reshape(P, D))
    common = dict(
        gate_in=np.ascontiguousarray(inputs["L1_attn_gate"].reshape(1, 16)),
        woT_in=np.ascontiguousarray(inputs["L1_attn_Wo"].T),
        weightT=np.ascontiguousarray(inputs["L1_ffn_weight"].T),
        AsT=AsT, Bst=Bst,
        rwT=np.ascontiguousarray(inputs["L1_ffn_router_w"].T),
        repl8_in=np.ascontiguousarray(
            np.kron(np.eye(8, dtype=np.float32), np.ones((1, 16), np.float32)) * 2.0),
        ident_in=np.eye(P, dtype=np.float32),
        ones_in=np.ones((P, P), np.float32),
        consts_in=np.tile(np.array([[EPS, 1.0, 0, 0, 0, 0, 0, 0]], np.float32), (P, 1)),
    )
    maps = []
    for c in range(8):
        grp = (c // 4) * 4
        maps.append(dict(
            common, x_in=x4_shards[c], qf_in=qf[c],
            gk=np.ascontiguousarray(np.stack([kf[grp + r] for r in range(4)])),
            gv=np.ascontiguousarray(np.stack([v[grp + r] for r in range(4)])),
            mem_parts=np.ascontiguousarray(np.stack([mems[grp + r] for r in range(4)])),
        ))
    return maps



def shard_tokens(x_full):
    shards = []
    xt = np.asarray(x_full).reshape(2, 64, P, -1)
    for core in range(8):
        b, g = core // 4, core % 4
        shards.append(np.ascontiguousarray(xt[b, g::4].reshape(T, -1)))
    return shards


def unshard_tokens(shards):
    d = shards[0].shape[-1]
    out = np.empty((2, 64, P, d), dtype=np.asarray(shards[0]).dtype)
    for core in range(8):
        b, g = core // 4, core % 4
        out[b, g::4] = np.asarray(shards[core]).reshape(NT, P, d)
    return out.reshape(2, S, d)


def prep_launch_A_inputs(inputs, x_shards):
    W1 = np.concatenate([inputs["L0_ffn_W1"], inputs["L0_ffn_Ws1"][None]], axis=0)
    W1r = np.ascontiguousarray(W1.reshape(NE, NDC, P, NFC, P).transpose(0, 3, 1, 2, 4))
    W2 = np.concatenate([inputs["L0_ffn_W2"], inputs["L0_ffn_Ws2"][None]], axis=0)
    W2bh = np.ascontiguousarray(W2.reshape(NE, NFC, P, 4, 512).transpose(0, 3, 1, 2, 4))
    wqkv = np.ascontiguousarray(np.stack([inputs["L0_attn_Wq"],
                                          inputs["L0_attn_Wk"],
                                          inputs["L0_attn_Wv"]]))
    bqk = np.ascontiguousarray(np.stack([
        inputs["L0_attn_bq"].reshape(NDC, P).T,
        inputs["L0_attn_bk"].reshape(NDC, P).T]))
    common = dict(
        W1r=W1r, W2b=W2bh, Wg=np.ascontiguousarray(inputs["L0_ffn_Wg"]),
        ln_gb=np.ascontiguousarray(np.stack([inputs["L0_attn_ln_g"],
                                             inputs["L0_attn_ln_b"]])),
        wqkv=wqkv, bqk=bqk,
        bv_row=np.ascontiguousarray(inputs["L0_attn_bv"][None]),
        ones_in=np.ones((P, P), np.float32),
        consts_in=np.tile(np.array([[EPS, 1.0, 0.0, 0.0, 0.0, 0.0, 0.0, 0.0]],
                                   np.float32), (P, 1)),
    )
    return [dict(common, x_in=x_shards[c]) for c in range(8)]


def run_launch_A(inputs):
    x_shards = shard_tokens(inputs["L0_attn_X"])
    in_maps = prep_launch_A_inputs(inputs, x_shards)
    nc = build_launch_A()
    res = run_bass_kernel_spmd(nc, in_maps, core_ids=list(range(8)))
    return res.results


_NC_CACHE = {}


def _get_nc(which):
    if which not in _NC_CACHE:
        _NC_CACHE[which] = {"A": build_launch_A, "B": build_launch_B,
                            "C": build_launch_C}[which]()
    return _NC_CACHE[which]


def kernel(**inputs):
    cores = list(range(8))
    x_shards = shard_tokens(np.ascontiguousarray(inputs["L0_attn_X"]))

    resA = run_bass_kernel_spmd(_get_nc("A"), prep_launch_A_inputs(inputs, x_shards),
                                core_ids=cores).results
    x1 = [np.asarray(resA[c]["x_out"]) for c in range(8)]
    qf = [np.asarray(resA[c]["qf_out"]) for c in range(8)]
    kf = [np.asarray(resA[c]["kf_out"]) for c in range(8)]
    vv = [np.asarray(resA[c]["v_out"]) for c in range(8)]

    resB = run_bass_kernel_spmd(_get_nc("B"),
                                prep_launch_B_inputs(inputs, x1, qf, kf, vv),
                                core_ids=cores).results
    x4 = [np.asarray(resB[c]["x_out"]) for c in range(8)]
    qf1 = [np.asarray(resB[c]["qf_out"]) for c in range(8)]
    kf1 = [np.asarray(resB[c]["kf_out"]) for c in range(8)]
    vv1 = [np.asarray(resB[c]["v_out"]) for c in range(8)]
    mems = [np.asarray(resB[c]["mem_out"]) for c in range(8)]

    resC = run_bass_kernel_spmd(_get_nc("C"),
                                prep_launch_C_inputs(inputs, x4, qf1, kf1, vv1, mems),
                                core_ids=cores).results
    y = unshard_tokens([np.asarray(resC[c]["y_out"]) for c in range(8)])
    return np.ascontiguousarray(y.astype(np.float32))
